# version 03
# baseline (speedup 1.0000x reference)
import contextlib
import numpy as np
import ml_dtypes
import concourse.bass as bass
import concourse.mybir as mybir
from concourse.bass_utils import run_bass_kernel_spmd

F32 = mybir.dt.float32
BF16 = mybir.dt.bfloat16
ALU = mybir.AluOpType
ACT = mybir.ActivationFunctionType
AX = mybir.AxisListType
BF = ml_dtypes.bfloat16

EXT = 2432
OWN = 2048
OWN0 = 256
NTT = 19
DC = 8
NG = 2
HPG = 8
NPAIR = 8
FC = 32
NST = 4
ENGS = ("tensor", "vector", "scalar", "gpsimd", "sync")
T, V, S, G, SY = ENGS


class Sched:

    def __init__(self):
        self.q = {e: [] for e in ENGS}
        self.val = {e: 0 for e in ENGS}
        self.lastw = {e: {} for e in ENGS}

    def _prune(self, eng, waits):
        wl = []
        for (se, v) in waits:
            if v and v > self.lastw[eng].get(se, 0):
                self.lastw[eng][se] = v
                wl.append((se, v))
        return wl

    def emit(self, eng, fn, waits=(), amt=1):
        self.q[eng].append((tuple(self._prune(eng, waits)), fn, amt))
        self.val[eng] += amt
        return self.val[eng]

    def dma(self, eng, fn, waits=()):
        return self.emit(eng, fn, waits, amt=16)

    def wait_only(self, eng, waits):
        wl = self._prune(eng, waits)
        if wl:
            self.q[eng].append((tuple(wl), None, 0))

    def barrier(self):
        snap = dict(self.val)
        for e in ENGS:
            self.wait_only(e, [(o, snap[o]) for o in ENGS if o != e])

    def run(self, eng, e, sems):
        own = sems[eng]
        cnt = 0
        chain = (eng == "vector")
        for wl, fn, amt in self.q[eng]:
            for (se, v) in wl:
                e.wait_ge(sems[se], v)
            if fn is not None:
                if chain and cnt:
                    e.wait_ge(own, cnt)
                fn(e).then_inc(own, amt)
                cnt += amt




DBG = False


def build(phase_limit=99):
    nc = bass.Bass("TRN2", target_bir_lowering=False)

    d_x = nc.dram_tensor("x_ext", [EXT, 1024], F32, kind="ExternalInput").ap()
    d_wq = nc.dram_tensor("wq_img", [128, 8192], F32, kind="ExternalInput").ap()
    d_wk = nc.dram_tensor("wk_img", [128, 8192], F32, kind="ExternalInput").ap()
    d_wv = nc.dram_tensor("wv_img", [128, 8192], F32, kind="ExternalInput").ap()
    d_bq = nc.dram_tensor("bq_img", [128, 8], F32, kind="ExternalInput").ap()
    d_bk = nc.dram_tensor("bk_img", [128, 8], F32, kind="ExternalInput").ap()
    d_bv = nc.dram_tensor("bv_img", [128, 1024], F32, kind="ExternalInput").ap()
    d_wp = nc.dram_tensor("wproj_img", [128, 8192], F32, kind="ExternalInput").ap()
    d_bp = nc.dram_tensor("bproj_img", [128, 1024], F32, kind="ExternalInput").ap()
    d_gA = nc.dram_tensor("gA_img", [128, 320], F32, kind="ExternalInput").ap()
    d_gb = nc.dram_tensor("gateb_img", [128, 8], F32, kind="ExternalInput").ap()
    d_B = nc.dram_tensor("B_img", [32, 4096], F32, kind="ExternalInput").ap()
    d_wi = nc.dram_tensor("wi_img", [128, 32768], F32, kind="ExternalInput").ap()
    d_wib = nc.dram_tensor("wib_img", [128, 32], F32, kind="ExternalInput").ap()
    d_wo = nc.dram_tensor("wo_img", [128, 32768], F32, kind="ExternalInput").ap()
    d_wob = nc.dram_tensor("wob_img", [128, 1024], F32, kind="ExternalInput").ap()
    d_msk = nc.dram_tensor("masks", [128, 3840], BF16, kind="ExternalInput").ap()
    d_db = nc.dram_tensor("den_bias", [1, 256], F32, kind="ExternalInput").ap()
    d_gc = nc.dram_tensor("g_col", [128, 1], F32, kind="ExternalInput").ap()
    d_id = nc.dram_tensor("ident", [128, 128], BF16, kind="ExternalInput").ap()
    d_out = nc.dram_tensor("out", [OWN, 1024], F32, kind="ExternalOutput").ap()
    d_dbgy = nc.dram_tensor("dbg_y", [128, 16384], F32, kind="ExternalOutput").ap() if DBG else None
    d_dbgx1 = nc.dram_tensor("dbg_x1", [128, 16384], F32, kind="ExternalOutput").ap() if DBG else None
    cc_in = nc.dram_tensor("cc_in", [65, 512], F32)
    cc_out = nc.dram_tensor("cc_out", [65, 512], F32)

    sch = Sched()
    es = contextlib.ExitStack()
    sems = {e: es.enter_context(nc.semaphore("sem_" + e)) for e in ENGS}

    def sbt(name, shape, dt, side="left"):
        return es.enter_context(nc.sbuf_tensor(name, shape, dt, side=side))

    identsb = sbt("identsb", [128, 128], BF16)
    ones_sb = sbt("ones_sb", [128, 64], BF16)
    bq_sb = sbt("bq_sb", [128, 8], F32)
    bk_sb = sbt("bk_sb", [128, 8], F32)
    gateb_sb = sbt("gateb_sb", [128, 8], F32)
    gcol_sb = sbt("gcol_sb", [128, 1], F32)
    stc = sbt("stat", [128, 8], F32)
    epsc = sbt("epsc", [128, 1], F32)
    rs = contextlib.ExitStack()

    def rbt(name, shape, dt):
        return rs.enter_context(nc.sbuf_tensor(name, shape, dt, side="right"))

    y_fm = rbt("y_fm", [128, 8 * 2048], BF16)
    allred = rbt("allred", [65, 512], F32)
    globred = rbt("globred", [65, 512], F32)
    dengb = rbt("dengb", [128, 512], F32)
    ygn = rbt("ygn", [128, 512], BF16)

    P0 = es.enter_context(nc.psum_tensor("P0", [128, 1024], F32))
    P1 = es.enter_context(nc.psum_tensor("P1", [128, 1024], F32))
    P2 = es.enter_context(nc.psum_tensor("P2", [128, 512], F32))
    PTB = es.enter_context(nc.psum_tensor("PTB", [128, 1024], BF16))
    P4 = es.enter_context(nc.psum_tensor("P4", [128, 1024], F32))
    QKVROT = [P0[:, 0:512], P0[:, 512:1024], P1[:, 0:512], P1[:, 512:1024]]

    def lp_recip(out_ap, in_ap):
        def f(e):
            with nc.allow_low_precision(reason="softmax recip as bf16 operand of bf16 broadcast matmul"):
                return e.reciprocal(out_ap, in_ap)
        return f

    open_scopes = [rs]

    def finish_early(scratch_f32):
        for qq in range(16):
            sch.dma(SY, lambda e, q=qq: e.dma_start(
                out=d_out[q * 128:(q + 1) * 128, :], in_=scratch_f32[:, :]),
                waits=[(V, sch.val[V]), (T, sch.val[T]), (S, sch.val[S]), (G, sch.val[G])])
        sch.wait_only(SY, [(SY, sch.val[SY])])
        sch.wait_only(G, [(SY, sch.val[SY])])

    def seal():
        for sstack in reversed(open_scopes):
            sstack.close()
        with nc.Block() as block:
            @block.tensor
            def _(e):
                sch.run(T, e, sems)

            @block.vector
            def _(e):
                sch.run(V, e, sems)

            @block.scalar
            def _(e):
                sch.run(S, e, sems)

            @block.gpsimd
            def _(e):
                sch.run(G, e, sems)

            @block.sync
            def _(e):
                sch.run(SY, e, sems)
        es.close()
        return nc

    e_id = sch.dma(SY, lambda e: e.dma_start(out=identsb[:, :], in_=d_id))

    open_scopes = []
    att = contextlib.ExitStack()
    open_scopes.append(att)

    def abt(name, shape, dt):
        return att.enter_context(nc.sbuf_tensor(name, shape, dt, side="left"))

    masksb = abt("masksb", [128, 3840], BF16)
    bv_sb = abt("bv_sb", [128, 1024], F32)
    db_sb = abt("db_sb", [128, 256], F32)

    def MS(si, s):
        c = (si * 5 + s) * 256
        return masksb[:, c:c + 256]

    def MSW(si):
        c = (si * 5 + 1) * 256
        return masksb[:, c:c + 1024]

    for dst, src in ((masksb, d_msk), (bq_sb, d_bq), (bk_sb, d_bk), (bv_sb, d_bv),
                     (gateb_sb, d_gb), (gcol_sb, d_gc)):
        sch.dma(SY, lambda e, d=dst, s=src: e.dma_start(out=d[:, :], in_=s))
    e_db = sch.dma(SY, lambda e: e.dma_start(out=db_sb[64:65, :], in_=d_db))
    e_ones = sch.emit(V, lambda e: e.memset(ones_sb[64:65, :], 1.0))
    sch.emit(V, lambda e: e.memset(epsc[:, :], 1e-5))

    x_bf = abt("x_bf", [128, DC * EXT], BF16)
    q_fm = abt("q_fm", [128, 4 * OWN], BF16)
    k_fm = abt("k_fm", [128, 4 * EXT], BF16)
    v_sb = abt("v_sb", [128, NTT * 520], BF16)
    wq_sb = abt("wq_sb", [128, 4096], BF16)
    wk_sb = abt("wk_sb", [128, 4096], BF16)
    wv_sb = abt("wv_sb", [128, 4096], BF16)
    qg_fm = abt("qg_fm", [128, 4 * 32], BF16)
    ptgq = [abt(f"ptgq{i}", [128, 256], BF16) for i in range(2)]
    PTg = [abt(f"PTg{i}", [128, 2048], BF16) for i in range(2)]
    PTw = [abt(f"PTw{i}", [128, 1024], BF16) for i in range(2)]
    PTgz = abt("PTgz", [128, 256], BF16)
    rcp2 = [abt(f"rcp{i}", [128, 256], BF16) for i in range(2)]
    rden_sb = abt("rden_sb", [128, 256], F32)
    den2 = [abt(f"den{i}", [128, 256], F32) for i in range(2)]
    xtm = [abt(f"xtm{i}", [128, 1024], F32) for i in range(2)]
    xbt = [abt(f"xbt{i}", [128, 1024], BF16) for i in range(2)]

    ptb_free = 0
    e_xdma = [0, 0]
    e_xcast = [0, 0]
    for tt in range(NTT):
        par = tt % 2
        w = [(V, e_xcast[par])]
        e_xdma[par] = sch.dma(SY, lambda e, t=tt: e.dma_start(
            out=xtm[t % 2][:, :], in_=d_x[t * 128:(t + 1) * 128, :]), waits=w)
        e_xcast[par] = sch.emit(V, lambda e, t=tt: e.tensor_copy(
            xbt[t % 2][:, :], xtm[t % 2][:, :]), waits=[(SY, e_xdma[par])])
        last = 0
        for ci in range(DC):
            w = [(V, e_xcast[par]), (SY, e_id)]
            if ci == 0:
                w.append((V, ptb_free))
            last = sch.emit(T, lambda e, t=tt, c=ci: e.transpose(
                PTB[:, c * 128:(c + 1) * 128], xbt[t % 2][:, c * 128:(c + 1) * 128],
                identsb[:, :]), waits=w)
        ptb_free = sch.emit(V, lambda e, t=tt: e.tensor_copy(
            x_bf[:, :].rearrange("p (c ee) -> p c ee", ee=EXT)[:, :, t * 128:(t + 1) * 128],
            PTB[:, :].rearrange("p (c j) -> p c j", j=128)), waits=[(T, last)])
    e_xbf = sch.val[V]

    def load_grp(g, waits):
        sch.dma(G, lambda e, gg=g: e.dma_start(
            out=wq_sb[:, :], in_=d_wq[:, gg * 4096:(gg + 1) * 4096]), waits=waits)
        sch.dma(G, lambda e, gg=g: e.dma_start(
            out=wk_sb[:, :], in_=d_wk[:, gg * 4096:(gg + 1) * 4096]))
        return sch.dma(G, lambda e, gg=g: e.dma_start(
            out=wv_sb[:, :], in_=d_wv[:, gg * 4096:(gg + 1) * 4096]))

    e_w = load_grp(0, [])
    e_allred_evac = 0
    e_ccr = 0

    if phase_limit <= 1:
        finish_early(xtm[0])
        return seal()

    for g in range(NG):
        if g == 1:
            sch.wait_only(T, [(V, sch.val[V]), (S, sch.val[S])])
        e_vfree = sch.val[T]

        rot_i = 0
        e_rotev = [0, 0, 0, 0]

        def qkv_mm(dst_ap, bias_col, ntok, lhs_of, rhs_ap):
            nonlocal rot_i
            rot = rot_i % 4
            rot_i += 1
            ps = QKVROT[rot]
            w = [(V, e_xbf), (G, e_w), (V, e_rotev[rot])]
            last = 0
            for ci in range(DC):
                last = sch.emit(T, lambda e, c=ci, p=ps, lo=lhs_of, r=rhs_ap, n=ntok: e.matmul(
                    p[:, 0:n], lo(c), r(c), start=(c == 0), stop=(c == DC - 1)),
                    waits=w if ci == 0 else [])
            ev = sch.emit(V, lambda e, p=ps, d=dst_ap, bc=bias_col, n=ntok: e.tensor_scalar(
                d, p[:, 0:n], bc, None, op0=ALU.add), waits=[(T, last)])
            e_rotev[rot] = ev
            return last

        for co in range(4):
            cof = g * 4 + co
            for nt in range(4):
                qkv_mm(q_fm[:, co * OWN + nt * 512: co * OWN + (nt + 1) * 512],
                       bq_sb[:, cof:cof + 1], 512,
                       lambda ci, co_=co: wq_sb[:, (co_ * DC + ci) * 128:(co_ * DC + ci + 1) * 128],
                       lambda ci, n0=nt * 512: x_bf[:, ci * EXT + OWN0 + n0: ci * EXT + OWN0 + n0 + 512])
            for nt in range(5):
                n0 = nt * 512
                ntok = min(512, EXT - n0)
                qkv_mm(k_fm[:, co * EXT + n0: co * EXT + n0 + ntok],
                       bk_sb[:, cof:cof + 1], ntok,
                       lambda ci, co_=co: wk_sb[:, (co_ * DC + ci) * 128:(co_ * DC + ci + 1) * 128],
                       lambda ci, n0_=n0, nt_=ntok: x_bf[:, ci * EXT + n0_: ci * EXT + n0_ + nt_])
        e_vones = sch.emit(V, lambda e: e.memset(
            v_sb[:, :].rearrange("p (t h c) -> p t h c", h=8, c=65)[:, :, :, 64:65], 1.0),
            waits=[(T, e_vfree)])
        for tt in range(NTT):
            rot = rot_i % 4
            rot_i += 1
            ps = QKVROT[rot]
            w = [(V, e_xbf), (G, e_w), (V, e_rotev[rot]), (V, e_vones)]
            last = 0
            for ci in range(DC):
                last = sch.emit(T, lambda e, c=ci, p=ps, t=tt: e.matmul(
                    p, x_bf[:, c * EXT + t * 128: c * EXT + (t + 1) * 128],
                    wv_sb[:, c * 512:(c + 1) * 512], start=(c == 0), stop=(c == DC - 1)),
                    waits=w if ci == 0 else [])
            ev = sch.emit(V, lambda e, p=ps, t=tt, gg=g: e.tensor_tensor(
                v_sb[:, t * 520:(t + 1) * 520].rearrange("p (h c) -> p h c", c=65)[:, :, 0:64],
                p.rearrange("p (h c) -> p h c", c=64),
                bv_sb[:, gg * 512:(gg + 1) * 512].rearrange("p (h c) -> p h c", c=64),
                op=ALU.add), waits=[(T, last)])
            e_rotev[rot] = ev
        e_A_T, e_A_V = sch.val[T], sch.val[V]

        if g == 0:
            e_w_next = load_grp(1, [(T, e_A_T)])

        if phase_limit <= 2 + 3 * g:
            finish_early(xtm[0])
            return seal()
        e_qg = 0
        for co in range(4):
            cof = g * 4 + co
            last = 0
            for ci in range(DC):
                w = []
                if ci == 0:
                    w = [(V, e_A_V), (V, e_qg)]
                last = sch.emit(T, lambda e, c=ci, co_=co: e.matmul(
                    P2[:, 0:32], wq_sb[:, (co_ * DC + c) * 128:(co_ * DC + c + 1) * 128],
                    x_bf[:, c * EXT: c * EXT + 32],
                    start=(c == 0), stop=(c == DC - 1)), waits=w)
            e_qg = sch.emit(V, lambda e, co_=co, cf=cof: e.tensor_scalar(
                qg_fm[:, co_ * 32:(co_ + 1) * 32], P2[:, 0:32], bq_sb[:, cf:cf + 1],
                None, op0=ALU.add), waits=[(T, last)])

        e_gexp = [0, 0]
        e_gpv = 0
        HORD = (0, 2, 4, 6, 1, 3, 5, 7)
        for kb in range(16):
            par = kb % 2
            last_sc = 0
            for hh in HORD:
                base = (hh % 2) * 64
                co = hh // 2
                w = []
                if hh == 0:
                    w = [(V, e_qg), (S, e_gexp[par]), (V, e_allred_evac if g == 1 else 0)]
                cc = (hh % 2) * 512 + par * 256 + (hh // 2) * 32
                last_sc = sch.emit(T, lambda e, h=hh, b=base, c=co, k=kb, cc_=cc: e.matmul(
                    P4[:, cc_: cc_ + 32],
                    k_fm[b:b + 64, c * EXT + (k + 2) * 128: c * EXT + (k + 3) * 128],
                    qg_fm[b:b + 64, c * 32:(c + 1) * 32],
                    start=(h in (0, 1)), stop=(h in (6, 7))), waits=w)
            e_gexp[par] = sch.emit(S, lambda e, p=par, o=par * 256: e.activation(
                ptgq[p][:, :],
                P4[:, :].rearrange("q (bk c) -> q bk c", c=512)[:, :, o:o + 128],
                ACT.Exp, scale=0.125),
                waits=[(T, last_sc), (T, e_gpv if kb >= 2 else 0)])
            for hh in range(HPG):
                tgt = (P2[0:65, hh * 32: (hh + 1) * 32] if hh < 4
                       else P0[0:65, (hh - 4) * 32: (hh - 3) * 32])
                e_gpv = sch.emit(T, lambda e, h=hh, k=kb, p=par, tg=tgt: e.matmul(
                    tg, v_sb[:, (k + 2) * 520 + h * 65: (k + 2) * 520 + (h + 1) * 65],
                    ptgq[p][:, (h % 2) * 128 + (h // 2) * 32: (h % 2) * 128 + (h // 2) * 32 + 32],
                    start=(k == 0 and h in (0, 4)), stop=(k == 15 and h in (3, 7))),
                    waits=[(S, e_gexp[par])])
        sch.emit(V, lambda e, gg=g: e.tensor_copy(
            allred[:, gg * 256: gg * 256 + 128], P2[0:65, 0:128]), waits=[(T, e_gpv)])
        e_allred_evac = sch.emit(V, lambda e, gg=g: e.tensor_copy(
            allred[:, gg * 256 + 128: gg * 256 + 256], P0[0:65, 0:128]))

        if g == 1:
            e_ccw = sch.dma(G, lambda e: e.dma_start(out=cc_in.ap(), in_=allred[:, :]),
                            waits=[(V, e_allred_evac), (G, e_w)])
            e_cc = sch.emit(G, lambda e: e.collective_compute(
                "AllReduce", ALU.add, replica_groups=[[0, 1, 2, 3], [4, 5, 6, 7]],
                ins=[cc_in.ap()], outs=[cc_out.ap()]), waits=[(G, e_ccw)])
            e_ccr = sch.dma(G, lambda e: e.dma_start(out=globred[:, :], in_=cc_out.ap()),
                            waits=[(G, e_cc)])

        if phase_limit <= 3 + 3 * g:
            finish_early(xtm[0])
            return seal()
        yb = [P4[0:65, 0:256], P2[0:65, 0:256]]
        ybsrc = [P4, P2]
        scb = [P0, P1]
        e_exp = {}
        e_dencp = {}
        e_pv = {}
        e_ym = {}
        e_bc = {}
        e_ptg_ms = [0, 0]
        e_ptg0v = [0, 0]
        e_ptg_exp = 0
        for hi, hh in enumerate((0, 2, 4, 6, 1, 3, 5, 7)):
            H = g * HPG + hh
            hp = hi % 2
            base = (hh % 2) * 64
            co = hh // 2
            ch = H // 2
            if hi == 4:
                sch.emit(T, lambda e: e.drain())
            for half in range(2):
                for q4 in range(2):
                    w = []
                    if half == 0 and q4 == 0:
                        w = [(V, e_allred_evac), (S, e_gexp[0]), (S, e_gexp[1]),
                             (T, e_pv.get((hi - 2, 7), 0)),
                             (V, e_ym.get((hi - 1, 6), 0)),
                             (V, e_ptg_ms[hp]), (V, e_ptg0v[hp])]
                    elif half == 1 and q4 == 0:
                        w = [(S, e_ptg_exp)]
                    lastmm = sch.emit(T, lambda e, b=base, c=co, h2=half, q=q4: e.matmul(
                        P4[:, q * 512:(q + 1) * 512],
                        k_fm[b:b + 64, c * EXT: c * EXT + 128],
                        q_fm[b:b + 64, c * OWN + h2 * 1024 + q * 512:
                             c * OWN + h2 * 1024 + (q + 1) * 512],
                        start=True, stop=True), waits=w)
                e_ptg_exp = sch.emit(S, lambda e, h2=half, p=hp: e.activation(
                    PTg[p][:, h2 * 1024:(h2 + 1) * 1024], P4[:, 0:1024], ACT.Exp,
                    scale=0.125), waits=[(T, lastmm)])
            sch.emit(V, lambda e, p=hp: e.memset(PTg[p][32:64, :], 0.0),
                     waits=[(S, e_ptg_exp)])
            e_ptg_ms[hp] = sch.emit(V, lambda e, p=hp: e.memset(PTg[p][64:128, :], 0.0))
            e_ptg0v[hp] = sch.emit(V, lambda e, p=hp: e.tensor_tensor(
                PTgz[:, :], PTg[p][:, 0:256], MS(1, 0), op=ALU.mult),
                waits=[(V, e_ptg_ms[hp]), (SY, e_db)])

            for p in range(NPAIR):
                par = p % 2

                def prev(d):
                    return d.get((hi, p - 2)) or (d.get((hi - 1, 6 + par)) if hi else 0) or 0

                si = 1 if p == 0 else (2 if p == 7 else 0)
                last_sc = 0
                for s in range(1, 5):
                    w = []
                    if s == 1:
                        w = [(V, e_ptg_ms[hp]), (S, prev(e_exp))]
                    last_sc = sch.emit(T, lambda e, b=base, c=co, pp=p, s_=s, pr=par: e.matmul(
                        scb[pr][:, (s_ - 1) * 256:s_ * 256],
                        k_fm[b:b + 64, c * EXT + (2 * pp + s_) * 128: c * EXT + (2 * pp + s_ + 1) * 128],
                        q_fm[b:b + 64, c * OWN + pp * 256: c * OWN + (pp + 1) * 256],
                        start=(s_ in (1, 3)), stop=(s_ == 4)), waits=w)
                e_exp[(hi, p)] = sch.emit(S, lambda e, pr=par: e.activation(
                    PTw[pr][:, :], scb[pr][:, 0:1024], ACT.Exp, scale=0.125),
                    waits=[(T, last_sc), (V, prev(e_ym))])
                e_mask = sch.emit(V, lambda e, pr=par, si_=si: e.tensor_tensor(
                    PTw[pr][:, :], PTw[pr][:, :], MSW(si_), op=ALU.mult),
                    waits=[(S, e_exp[(hi, p)])])
                wpv = [(V, e_mask), (V, prev(e_ym))]
                if p == 0:
                    wpv.append((V, e_ptg0v[hp]))
                rhs_glob = PTgz[:, :] if p == 0 else PTg[hp][:, p * 256:(p + 1) * 256]
                lastpv = sch.emit(T, lambda e, h=hh, pr=par, rg=rhs_glob: e.matmul(
                    yb[pr], v_sb[:, h * 65:(h + 1) * 65], rg, start=True, stop=False),
                    waits=wpv)
                for s in range(1, 5):
                    lastpv = sch.emit(T, lambda e, h=hh, pp=p, s_=s, pr=par: e.matmul(
                        yb[pr], v_sb[:, (2 * pp + s_) * 520 + h * 65: (2 * pp + s_) * 520 + (h + 1) * 65],
                        PTw[pr][:, (s_ - 1) * 256: s_ * 256],
                        start=False, stop=(s_ == 4)))
                e_pv[(hi, p)] = lastpv
                if p == 0:
                    e_rd = sch.emit(V, lambda e, pr=par: e.tensor_tensor(
                        rden_sb[64:65, :], ybsrc[pr][64:65, 0:256], db_sb[64:65, :],
                        op=ALU.add), waits=[(T, lastpv)])
                    e_rc = sch.emit(V, lp_recip(rcp2[par][64:65, :], rden_sb[64:65, :]),
                                    waits=[(V, e_rd), (T, prev(e_bc))])
                else:
                    e_rc = sch.emit(V, lp_recip(rcp2[par][64:65, :], ybsrc[par][64:65, 0:256]),
                                    waits=[(T, lastpv), (T, prev(e_bc))])
                e_bc[(hi, p)] = sch.emit(T, lambda e, pr=par: e.matmul(
                    ybsrc[pr][0:64, 256:512], ones_sb[64:65, :], rcp2[pr][64:65, :],
                    start=True, stop=True),
                    waits=[(V, e_rc), (S, prev(e_dencp))])
                e_dencp[(hi, p)] = sch.emit(S, lambda e, pr=par: e.copy(
                    den2[pr][0:64, :], ybsrc[pr][0:64, 256:512]),
                    waits=[(T, e_bc[(hi, p)]), (V, prev(e_ym))])
                e_ym[(hi, p)] = sch.emit(V, lambda e, b=base, c=ch, pp=p, pr=par: e.tensor_tensor(
                    y_fm[b:b + 64, c * 2048 + pp * 256: c * 2048 + (pp + 1) * 256],
                    ybsrc[pr][0:64, 0:256], den2[pr][0:64, :], op=ALU.mult),
                    waits=[(S, e_dencp[(hi, p)]), (T, e_pv[(hi, p)])])
        if g == 0:
            e_w = e_w_next

    if phase_limit <= 7:
        finish_early(xtm[0])
        return seal()
    attn_T, attn_V, attn_S = sch.val[T], sch.val[V], sch.val[S]
    e_rcg = sch.emit(V, lp_recip(rcp2[0][64:65, :], globred[64:65, 0:256]),
                     waits=[(G, e_ccr), (V, attn_V)])
    e_rcg2 = sch.emit(V, lp_recip(rcp2[1][64:65, :], globred[64:65, 256:512]))
    e_bc1 = sch.emit(T, lambda e: e.matmul(P2[0:64, 0:256], ones_sb[64:65, :],
                                           rcp2[0][64:65, :], start=True, stop=True),
                     waits=[(V, e_rcg2), (T, attn_T)])
    e_bc2 = sch.emit(T, lambda e: e.matmul(P2[0:64, 256:512], ones_sb[64:65, :],
                                           rcp2[1][64:65, :], start=False, stop=True))
    e_dgb = sch.emit(S, lambda e: e.copy(dengb[0:64, :], P2[0:64, 0:512]),
                     waits=[(T, e_bc2), (S, attn_S)])
    e_ygn = sch.emit(V, lambda e: e.scalar_tensor_tensor(
        ygn[0:64, :], globred[0:64, :], gcol_sb[0:64, :], dengb[0:64, :],
        op0=ALU.mult, op1=ALU.mult), waits=[(S, e_dgb)])
    e_ygn2 = sch.dma(SY, lambda e: e.dma_start(out=ygn[64:128, :], in_=ygn[0:64, :]),
                     waits=[(V, e_ygn)])
    for H in range(16):
        b = (H % 2) * 64
        ch = H // 2
        sch.emit(V, lambda e, b_=b, c=ch, h=H: e.tensor_tensor(
            y_fm[b_:b_ + 64, c * 2048: c * 2048 + 32],
            y_fm[b_:b_ + 64, c * 2048: c * 2048 + 32],
            ygn[b_:b_ + 64, h * 32:(h + 1) * 32], op=ALU.add),
            waits=[(SY, e_ygn2)])
    e_fix_done = sch.val[V]
    if DBG:
        sch.dma(G, lambda e: e.dma_start(out=d_dbgy, in_=y_fm[:, :]),
                waits=[(V, e_fix_done)])

    att.close()
    if phase_limit <= 8:
        finish_early(xtm[0])
        return seal()
    sch.barrier()
    bar_T = sch.val[T]

    sc2 = contextlib.ExitStack()
    open_scopes.append(sc2)

    def fbt(name, shape, dt):
        return sc2.enter_context(nc.sbuf_tensor(name, shape, dt, side="left"))

    wo_sb = fbt("wo_sb", [128, 32768], BF16)
    x1_tm = fbt("x1_tm", [128, 16 * 1024], F32)
    sqs = fbt("sqs", [128, 1024], BF16)
    wob_sb = fbt("wob_sb", [128, 1024], F32)
    scF = contextlib.ExitStack()
    open_scopes.append(scF)

    def f2t(name, shape, dt):
        return scF.enter_context(nc.sbuf_tensor(name, shape, dt, side="left"))

    wproj_sb = f2t("wproj_sb", [128, 8192], BF16)
    bpr_sb = f2t("bpr_sb", [128, 1024], F32)
    xq = [f2t(f"xq{i}", [128, 1024], F32) for i in range(2)]
    xp = xq

    e_wp = sch.dma(G, lambda e: e.dma_start(out=wproj_sb[:, :], in_=d_wp))
    for dst, srcd in ((bpr_sb, d_bp), (wob_sb, d_wob)):
        sch.dma(G, lambda e, d=dst, s=srcd: e.dma_start(out=d[:, :], in_=s))
    e_wo = 0
    for i in range(8):
        e_wo = sch.dma(G, lambda e, i_=i: e.dma_start(
            out=wo_sb[:, i_ * 4096:(i_ + 1) * 4096], in_=d_wo[:, i_ * 4096:(i_ + 1) * 4096]))

    def ln_chain(src_t, dst_ap, e_src_v, dst_waits=()):
        eS = sch.emit(S, lambda e: e.activation(sqs[:, :], src_t[:, :], ACT.Square,
                                                accum_out=stc[:, 1:2]),
                      waits=[(V, e_src_v)])
        e1 = sch.emit(V, lambda e: e.reduce_sum(stc[:, 0:1], src_t[:, :], axis=AX.X),
                      waits=[(V, e_src_v)])
        sch.wait_only(V, [(S, eS), (V, e1)])
        sch.emit(V, lambda e: e.tensor_scalar(stc[:, 2:3], stc[:, 0:1], 1.0 / 1024, None,
                                              op0=ALU.mult))
        sch.emit(V, lambda e: e.tensor_scalar(stc[:, 3:4], stc[:, 1:2], 1.0 / 1024, None,
                                              op0=ALU.mult))
        sch.emit(V, lambda e: e.tensor_tensor(stc[:, 4:5], stc[:, 2:3], stc[:, 2:3],
                                              op=ALU.mult))
        e5 = sch.emit(V, lambda e: e.tensor_tensor(stc[:, 4:5], stc[:, 3:4], stc[:, 4:5],
                                                   op=ALU.subtract))
        e6 = sch.emit(S, lambda e: e.activation(stc[:, 5:6], stc[:, 4:5], ACT.Sqrt,
                                                bias=epsc[:, :]), waits=[(V, e5)])
        e7 = sch.emit(V, lambda e: e.reciprocal(stc[:, 6:7], stc[:, 5:6]), waits=[(S, e6)])
        sch.emit(V, lambda e: e.tensor_tensor(stc[:, 7:8], stc[:, 2:3], stc[:, 6:7],
                                              op=ALU.mult))
        e9 = sch.emit(V, lambda e: e.tensor_scalar(stc[:, 7:8], stc[:, 7:8], -1.0, None,
                                                   op0=ALU.mult))
        sch.wait_only(V, [(V, e9)] + list(dst_waits))
        e10 = sch.emit(V, lambda e, d=dst_ap: e.tensor_scalar(
            d, src_t[:, :], stc[:, 6:7], stc[:, 7:8], op0=ALU.mult, op1=ALU.add))
        return e10, eS

    projb = [P0, P1]
    e_projmm = [0, 0]
    e_xq = [0, 0]
    e_x1 = {}
    e_xpS = {}
    e_xpv = {}
    for qt in range(16):
        par = qt % 2
        e_xq[par] = sch.dma(SY, lambda e, q=qt: e.dma_start(
            out=xq[q % 2][:, :], in_=d_x[OWN0 + q * 128: OWN0 + (q + 1) * 128, :]),
            waits=[(V, e_x1.get(qt - 2, (0,))[0] if qt >= 2 else 0),
                   (S, e_xpS.get(qt - 2, 0))])
        last = 0
        for dc in range(2):
            for ci in range(DC):
                w = []
                if dc == 0 and ci == 0:
                    w = [(V, e_fix_done), (G, e_wp), (V, e_x1.get(qt - 2, (0,))[0] if qt >= 2 else 0)]
                last = sch.emit(T, lambda e, d=dc, c=ci, q=qt, pb=projb[par]: e.matmul(
                    pb[:, d * 512:(d + 1) * 512],
                    y_fm[:, c * 2048 + q * 128: c * 2048 + (q + 1) * 128],
                    wproj_sb[:, c * 1024 + d * 512: c * 1024 + (d + 1) * 512],
                    start=(c == 0), stop=(c == DC - 1)), waits=w)
        e_projmm[par] = last
        ev = sch.emit(V, lambda e, pb=projb[par], q=qt: e.tensor_tensor(
            xp[q % 2][:, :], pb[:, 0:1024], xq[q % 2][:, :], op=ALU.add),
            waits=[(T, last), (SY, e_xq[par]), (S, e_xpS.get(qt - 2, 0))])
        ev = sch.emit(V, lambda e, q=qt: e.tensor_tensor(
            xp[q % 2][:, :], xp[q % 2][:, :], bpr_sb[:, :], op=ALU.add))
        e_xpv[qt] = ev
        e_x1[qt] = ln_chain(xp[par], x1_tm[:, qt * 1024:(qt + 1) * 1024], ev)
        e_xpS[qt] = e_x1[qt][1]
    e_F_T, e_F_V, e_F_S = sch.val[T], sch.val[V], sch.val[S]

    if DBG:
        for qt in range(16):
            sch.dma(SY, lambda e, q=qt: e.dma_start(
                out=d_dbgx1[:, q * 1024:(q + 1) * 1024], in_=x1_tm[:, q * 1024:(q + 1) * 1024]),
                waits=[(V, e_x1[qt][0])])
    if phase_limit <= 9:
        finish_early(xq[0])
        scF.close()
        sc2.close()
        rs.close()
        return seal()
    sch.barrier()
    scF.close()
    rs.close()

    def gbt(name, shape, dt):
        return scF.enter_context(nc.sbuf_tensor(name, shape, dt, side="left"))

    wib_sb = gbt("wib_sb", [128, 32], F32)
    gA_sb = gbt("gA_sb", [128, 320], BF16)
    B_sb = gbt("B_sb", [32, 4096], BF16)
    wi_sb = [gbt(f"wi_sb{i}", [128, 2048], BF16) for i in range(2)]
    x1bt = [gbt(f"x1bt{i}", [128, 1024], BF16) for i in range(2)]
    x1fm = gbt("x1fm", [128, 8 * 512], BF16)
    inter = gbt("inter", [128, 32 * 512], BF16)
    rT_sb = gbt("rT_sb", [32, 512], BF16)
    lg_sb = gbt("lg_sb", [128, 8], F32)
    msk_sb = gbt("msk_sb", [128, 8], F32)
    rm_sb = gbt("rm_sb", [128, 32], BF16)
    x2p = gbt("x2p", [128, 1024], F32)
    out_sb = [gbt("out_sb0", [128, 1024], F32)] * 2
    for dst, srcd in ((wib_sb, d_wib), (gA_sb, d_gA), (B_sb, d_B)):
        e_misc = sch.dma(G, lambda e, d=dst, s=srcd: e.dma_start(out=d[:, :], in_=s))

    e_wi = [0, 0]
    e_wicons = [0, 0]
    e_intermm = {}
    e_gate = {}
    e_outdma = [0, 0]
    e_wocons = {}
    e_ln2 = {}
    e_x1c = 0
    for st in range(NST):
        e_x1fm = 0
        e_tp2 = [0, 0]
        for sub in range(4):
            qt = st * 4 + sub
            e_x1c = sch.emit(V, lambda e, q=qt, s=sub: e.tensor_copy(
                x1bt[s % 2][:, :], x1_tm[:, q * 1024:(q + 1) * 1024]),
                waits=[(V, e_x1.get(qt, (0,))[0]),
                       (T, e_tp2[sub % 2]),
                       (T, e_intermm.get((st - 1, FC - 1), 0)),
                       (T, e_F_T if st == 0 else 0),
                       (V, e_F_V if st == 0 else 0)])
            last = 0
            for ci in range(DC):
                w = [(V, e_x1c), (V, ptb_free)] if ci == 0 else []
                last = sch.emit(T, lambda e, s=sub, c=ci: e.transpose(
                    PTB[:, c * 128:(c + 1) * 128],
                    x1bt[s % 2][:, c * 128:(c + 1) * 128],
                    identsb[:, :]), waits=w)
            e_tp2[sub % 2] = last
            ptb_free = sch.emit(V, lambda e, s=sub: e.tensor_copy(
                x1fm[:, :].rearrange("p (c qq) -> p c qq", qq=512)[:, :, s * 128:(s + 1) * 128],
                PTB[:, :].rearrange("p (c j) -> p c j", j=128)), waits=[(T, last)])
        e_x1fm = ptb_free

        for sub in range(4):
            last = 0
            for ci in range(DC):
                w = []
                if ci == 0:
                    w = [(V, e_x1fm), (G, e_misc), (S, e_dgb),
                         (V, e_gate.get((st, sub - 1), 0) or
                          (e_gate.get((st - 1, 3), 0) if st else 0))]
                last = sch.emit(T, lambda e, c=ci, s=sub: e.matmul(
                    P2[:, 0:40], x1fm[:, c * 512 + s * 128: c * 512 + (s + 1) * 128],
                    gA_sb[:, c * 40:(c + 1) * 40], start=(c == 0), stop=(c == DC - 1)), waits=w)
            sch.emit(V, lambda e: e.tensor_tensor(lg_sb[:, :], P2[:, 0:8], gateb_sb[:, :],
                                                  op=ALU.add), waits=[(T, last)])
            e2 = sch.emit(V, lambda e: e.reduce_max(stc[:, 0:1], lg_sb[:, :], axis=AX.X))
            sch.wait_only(V, [(V, e2)])
            e3 = sch.emit(V, lambda e: e.tensor_scalar(msk_sb[:, :], lg_sb[:, :],
                                                       stc[:, 0:1], None, op0=ALU.is_ge))
            sch.wait_only(V, [(V, e3)])
            for ex in range(8):
                e4 = sch.emit(V, lambda e, x=ex: e.tensor_scalar(
                    rm_sb[:, x * 4:(x + 1) * 4], P2[:, 8 + x * 4: 8 + (x + 1) * 4],
                    msk_sb[:, x:x + 1], None, op0=ALU.mult))
            etp = sch.emit(T, lambda e: e.transpose(PTB[0:32, 0:128], rm_sb[:, :],
                                                    identsb[:, :]),
                           waits=[(V, e4), (V, ptb_free)])
            ptb_free = sch.emit(V, lambda e, s=sub: e.tensor_copy(
                rT_sb[0:32, s * 128:(s + 1) * 128], PTB[0:32, 0:128]), waits=[(T, etp)])
            e_gate[(st, sub)] = ptb_free

        e_reluS = {}
        e_reluV = {}
        for fc in range(FC):
            par = (fc // 2) % 2
            if fc % 2 == 0:
                e_wi[par] = sch.dma(G, lambda e, f=fc: e.dma_start(
                    out=wi_sb[(f // 2) % 2][:, :], in_=d_wi[:, f * 1024: f * 1024 + 2048]),
                    waits=[(T, e_wicons[par]), (G, e_wo if (st == 0 and fc == 0) else 0)])
            rot = fc % 4
            ps = QKVROT[rot]
            last = 0
            for ci in range(DC):
                w = []
                if ci == 0:
                    w = [(G, e_wi[par]), (V, e_x1fm), (V, e_gate[(st, 3)])]
                    if fc >= 4:
                        w.append(((S if (fc - 4) % 2 == 0 else V),
                                  e_reluS.get(fc - 4, 0) or e_reluV.get(fc - 4, 0)))
                last = sch.emit(T, lambda e, c=ci, f=fc, p=ps: e.matmul(
                    p, wi_sb[(f // 2) % 2][:, (f % 2) * 1024 + c * 128:
                                           (f % 2) * 1024 + (c + 1) * 128],
                    x1fm[:, c * 512:(c + 1) * 512], start=(c == 0), stop=False), waits=w)
            last = sch.emit(T, lambda e, f=fc, p=ps: e.matmul(
                p, B_sb[:, f * 128:(f + 1) * 128], rT_sb[:, :], start=False, stop=True))
            e_intermm[(st, fc)] = last
            e_wicons[par] = last
            if fc % 2 == 0:
                e_reluS[fc] = sch.emit(S, lambda e, f=fc, p=ps: e.activation(
                    inter[:, f * 512:(f + 1) * 512], p, ACT.Relu, bias=wib_sb[:, f:f + 1]),
                    waits=[(T, last)])
            else:
                e_reluV[fc] = sch.emit(V, lambda e, f=fc, p=ps: e.tensor_scalar(
                    inter[:, f * 512:(f + 1) * 512], p, wib_sb[:, f:f + 1], 0.0,
                    op0=ALU.add, op1=ALU.max), waits=[(T, last)])
        e_relu_allS = sch.val[S]
        e_relu_allV = sch.val[V]

        for sub in range(4):
            qt = st * 4 + sub
            par = qt % 2
            last = 0
            for dc in range(2):
                for fc in range(FC):
                    w = []
                    if dc == 0 and fc == 0:
                        w = [(S, e_relu_allS), (V, e_relu_allV), (G, e_wo),
                             (V, e_wocons.get(qt - 1, 0))]
                    last = sch.emit(T, lambda e, d=dc, f=fc, s=sub: e.matmul(
                        P4[:, d * 512:(d + 1) * 512],
                        inter[:, f * 512 + s * 128: f * 512 + (s + 1) * 128],
                        wo_sb[:, f * 1024 + d * 512: f * 1024 + (d + 1) * 512],
                        start=(f == 0), stop=(f == FC - 1)), waits=w)
            ev = sch.emit(V, lambda e, q=qt: e.tensor_tensor(
                x2p[:, :], P4[:, 0:1024], x1_tm[:, q * 1024:(q + 1) * 1024], op=ALU.add),
                waits=[(T, last), (S, e_ln2.get(qt - 1, (0, 0))[1])])
            ev = sch.emit(V, lambda e: e.tensor_tensor(x2p[:, :], x2p[:, :], wob_sb[:, :],
                                                       op=ALU.add))
            e_wocons[qt] = ev
            e_ln2[qt] = ln_chain(x2p, out_sb[par][:, :], ev,
                                 dst_waits=[(SY, e_outdma[par])])
            e_outdma[par] = sch.dma(SY, lambda e, q=qt, pp=par: e.dma_start(
                out=d_out[q * 128:(q + 1) * 128, :], in_=out_sb[pp][:, :]),
                waits=[(V, e_ln2[qt][0])])

    sch.wait_only(SY, [(SY, sch.val[SY])])
    sch.wait_only(G, [(SY, sch.val[SY])])
    scF.close()
    sc2.close()
    return seal()


def _build_masks(r):
    t0 = r * 2048
    m = np.zeros((3, 5, 128, 256), np.float32)
    for si, p in ((0, 3), (1, 0), (2, 7)):
        for s in range(5):
            kb_ext = 0 if s == 0 else 2 * p + s
            ke = kb_ext * 128 + np.arange(128)
            i = t0 + 256 * p + np.arange(256)
            j = np.where(ke < 32, ke, t0 + (ke - 256))
            valid = (ke < 32) | ((ke >= 128) & (t0 + ke - 256 >= 0) & (t0 + ke - 256 < 8192))
            J, I = np.meshgrid(j, i, indexing="ij")
            Vv = valid[:, None]
            win = np.abs(I - J) <= 128
            glob = J < 32
            if s == 0:
                mm = Vv & glob & ~win & (I >= 32)
            else:
                mm = Vv & win & (I >= 32)
            m[si, s] = mm.astype(np.float32)
    return np.ascontiguousarray(m.transpose(2, 0, 1, 3).reshape(128, 3840)).astype(BF)


def kernel(**inp):
    x = np.asarray(inp["L0_attn_x"], np.float32)
    Wat = np.asarray(inp["L0_attn_W_attn"], np.float32)
    bat = np.asarray(inp["L0_attn_b_attn"], np.float32)
    Wpr = np.asarray(inp["L0_attn_W_proj"], np.float32)
    bpr = np.asarray(inp["L0_attn_b_proj"], np.float32)
    gW = np.asarray(inp["L0_ffn_gate_W"], np.float32)
    gb = np.asarray(inp["L0_ffn_gate_b"], np.float32)
    wiW = np.asarray(inp["L0_ffn_wi_W"], np.float32)
    wib = np.asarray(inp["L0_ffn_wi_b"], np.float32)
    woW = np.asarray(inp["L0_ffn_wo_W"], np.float32)
    wob = np.asarray(inp["L0_ffn_wo_b"], np.float32)
    lA = np.asarray(inp["L0_ffn_lora_A"], np.float32)
    lB = np.asarray(inp["L0_ffn_lora_B"], np.float32)

    C = np.ascontiguousarray
    wq_img = C(Wat[:, 0:1024].reshape(8, 128, 8, 128).transpose(1, 2, 0, 3).reshape(128, 8192))
    wk_img = C(Wat[:, 1024:2048].reshape(8, 128, 8, 128).transpose(1, 2, 0, 3).reshape(128, 8192))
    wv_img = C(Wat[:, 2048:3072].reshape(8, 128, 2, 512).transpose(1, 2, 0, 3).reshape(128, 8192))
    bq_img = C(bat[0:1024].reshape(8, 128).T)
    bk_img = C(bat[1024:2048].reshape(8, 128).T)
    bv_img = C(np.broadcast_to(bat[2048:3072], (128, 1024)))
    wp_img = C(Wpr.reshape(8, 128, 1024).transpose(1, 0, 2).reshape(128, 8192))
    bp_img = C(np.broadcast_to(bpr, (128, 1024)))
    M40 = np.concatenate([gW.T, lA.reshape(32, 1024).T], axis=1)
    gA_img = C(M40.reshape(8, 128, 40).transpose(1, 0, 2).reshape(128, 320))
    gb_img = C(np.broadcast_to(gb, (128, 8)))
    B_img = C(lB.transpose(0, 2, 1).reshape(32, 4096))
    wi_img = C(wiW.reshape(32, 128, 8, 128).transpose(3, 0, 2, 1).reshape(128, 32768))
    wib_img = C(wib.reshape(32, 128).T)
    wo_img = C(woW.reshape(1024, 32, 128).transpose(2, 1, 0).reshape(128, 32768))
    wob_img = C(np.broadcast_to(wob, (128, 1024)))
    ident = np.eye(128, dtype=BF)

    nc = build()
    in_maps = []
    for c in range(8):
        b, r = divmod(c, 4)
        t0 = r * 2048
        xe = np.zeros((EXT, 1024), np.float32)
        xe[0:32] = x[b, 0:32]
        if r > 0:
            xe[128:256] = x[b, t0 - 128:t0]
        xe[256:2304] = x[b, t0:t0 + 2048]
        if r < 3:
            xe[2304:2432] = x[b, t0 + 2048:t0 + 2176]
        db = np.zeros((1, 256), np.float32)
        if r == 0:
            db[0, 0:32] = 1.0
        gcol = np.full((128, 1), 1.0 if r == 0 else 0.0, np.float32)
        in_maps.append({
            "x_ext": xe, "wq_img": wq_img, "wk_img": wk_img, "wv_img": wv_img,
            "bq_img": bq_img, "bk_img": bk_img, "bv_img": bv_img,
            "wproj_img": wp_img, "bproj_img": bp_img,
            "gA_img": gA_img, "gateb_img": gb_img, "B_img": B_img,
            "wi_img": wi_img, "wib_img": wib_img, "wo_img": wo_img, "wob_img": wob_img,
            "masks": _build_masks(r), "den_bias": db, "g_col": gcol, "ident": ident,
        })
    res = run_bass_kernel_spmd(nc, in_maps, core_ids=list(range(8)))
    out = np.zeros((2, 8192, 1024), np.float32)
    for c in range(8):
        b, r = divmod(c, 4)
        out[b, r * 2048:(r + 1) * 2048] = res.results[c]["out"]
    return out


# version 08
# speedup vs baseline: 1.1972x; 1.1972x over previous
import contextlib
import numpy as np
import ml_dtypes
import concourse.bass as bass
import concourse.mybir as mybir
from concourse.bass_utils import run_bass_kernel_spmd

F32 = mybir.dt.float32
BF16 = mybir.dt.bfloat16
ALU = mybir.AluOpType
ACT = mybir.ActivationFunctionType
AX = mybir.AxisListType
BF = ml_dtypes.bfloat16

EXT = 2432
OWN = 2048
OWN0 = 256
NTT = 19
DC = 8
NG = 2
HPG = 8
NPAIR = 8
FC = 32
NST = 4
ENGS = ("tensor", "vector", "scalar", "gpsimd", "sync")
T, V, S, G, SY = ENGS


class Sched:

    def __init__(self):
        self.q = {e: [] for e in ENGS}
        self.val = {e: 0 for e in ENGS}
        self.lastw = {e: {} for e in ENGS}

    def _prune(self, eng, waits):
        wl = []
        for (se, v) in waits:
            if v and v > self.lastw[eng].get(se, 0):
                self.lastw[eng][se] = v
                wl.append((se, v))
        return wl

    def emit(self, eng, fn, waits=(), amt=1):
        self.q[eng].append((tuple(self._prune(eng, waits)), fn, amt))
        self.val[eng] += amt
        return self.val[eng]

    def dma(self, eng, fn, waits=()):
        return self.emit(eng, fn, waits, amt=16)

    def wait_only(self, eng, waits):
        wl = self._prune(eng, waits)
        if wl:
            self.q[eng].append((tuple(wl), None, 0))

    def barrier(self):
        snap = dict(self.val)
        for e in ENGS:
            self.wait_only(e, [(o, snap[o]) for o in ENGS if o != e])

    def run(self, eng, e, sems):
        own = sems[eng]
        cnt = 0
        chain = (eng == "vector")
        for wl, fn, amt in self.q[eng]:
            for (se, v) in wl:
                e.wait_ge(sems[se], v)
            if fn is not None:
                if chain and cnt:
                    e.wait_ge(own, cnt)
                fn(e).then_inc(own, amt)
                cnt += amt




DBG = False
NORM_MODE = 2


def build(phase_limit=99):
    nc = bass.Bass("TRN2", target_bir_lowering=False)

    d_x = nc.dram_tensor("x_ext", [EXT, 1024], F32, kind="ExternalInput").ap()
    d_wq = nc.dram_tensor("wq_img", [128, 8192], F32, kind="ExternalInput").ap()
    d_wk = nc.dram_tensor("wk_img", [128, 8192], F32, kind="ExternalInput").ap()
    d_wv = nc.dram_tensor("wv_img", [128, 8192], F32, kind="ExternalInput").ap()
    d_bq = nc.dram_tensor("bq_img", [128, 8], F32, kind="ExternalInput").ap()
    d_bk = nc.dram_tensor("bk_img", [128, 8], F32, kind="ExternalInput").ap()
    d_bv = nc.dram_tensor("bv_img", [128, 1024], F32, kind="ExternalInput").ap()
    d_wp = nc.dram_tensor("wproj_img", [128, 8192], F32, kind="ExternalInput").ap()
    d_bp = nc.dram_tensor("bproj_img", [128, 1024], F32, kind="ExternalInput").ap()
    d_gA = nc.dram_tensor("gA_img", [128, 320], F32, kind="ExternalInput").ap()
    d_gb = nc.dram_tensor("gateb_img", [128, 8], F32, kind="ExternalInput").ap()
    d_B = nc.dram_tensor("B_img", [32, 4096], F32, kind="ExternalInput").ap()
    d_wi = nc.dram_tensor("wi_img", [128, 32768], F32, kind="ExternalInput").ap()
    d_wib = nc.dram_tensor("wib_img", [128, 32], F32, kind="ExternalInput").ap()
    d_wo = nc.dram_tensor("wo_img", [128, 32768], F32, kind="ExternalInput").ap()
    d_wob = nc.dram_tensor("wob_img", [128, 1024], F32, kind="ExternalInput").ap()
    d_msk = nc.dram_tensor("masks", [128, 3840], BF16, kind="ExternalInput").ap()
    d_db = nc.dram_tensor("den_bias", [1, 256], F32, kind="ExternalInput").ap()
    d_gc = nc.dram_tensor("g_col", [128, 1], F32, kind="ExternalInput").ap()
    d_id = nc.dram_tensor("ident", [128, 128], BF16, kind="ExternalInput").ap()
    d_out = nc.dram_tensor("out", [OWN, 1024], F32, kind="ExternalOutput").ap()
    d_dbgy = nc.dram_tensor("dbg_y", [128, 16384], F32, kind="ExternalOutput").ap() if DBG else None
    d_dbgx1 = nc.dram_tensor("dbg_x1", [128, 16384], F32, kind="ExternalOutput").ap() if DBG else None
    cc_in = nc.dram_tensor("cc_in", [65, 512], F32)
    cc_out = nc.dram_tensor("cc_out", [65, 512], F32)

    sch = Sched()
    es = contextlib.ExitStack()
    sems = {e: es.enter_context(nc.semaphore("sem_" + e)) for e in ENGS}

    def sbt(name, shape, dt, side="left"):
        return es.enter_context(nc.sbuf_tensor(name, shape, dt, side=side))

    identsb = sbt("identsb", [128, 128], BF16)
    ones_sb = sbt("ones_sb", [128, 128], BF16)
    bq_sb = sbt("bq_sb", [128, 8], F32)
    bk_sb = sbt("bk_sb", [128, 8], F32)
    gateb_sb = sbt("gateb_sb", [128, 8], F32)
    gcol_sb = sbt("gcol_sb", [128, 1], F32)
    stc = sbt("stat", [128, 8], F32)
    epsc = sbt("epsc", [128, 1], F32)
    rs = contextlib.ExitStack()

    def rbt(name, shape, dt):
        return rs.enter_context(nc.sbuf_tensor(name, shape, dt, side="right"))

    y_fm = rbt("y_fm", [128, 8 * 2048], BF16)
    allred = rbt("allred", [65, 512], F32)
    globred = rbt("globred", [65, 512], F32)
    dengb = rbt("dengb", [128, 512], F32)
    ygn = rbt("ygn", [128, 512], BF16)

    P0 = es.enter_context(nc.psum_tensor("P0", [128, 1024], F32))
    P1 = es.enter_context(nc.psum_tensor("P1", [128, 1024], F32))
    P2 = es.enter_context(nc.psum_tensor("P2", [128, 512], F32))
    PTB = es.enter_context(nc.psum_tensor("PTB", [128, 1024], BF16))
    P4 = es.enter_context(nc.psum_tensor("P4", [128, 1024], F32))
    QKVROT = [P0[:, 0:512], P0[:, 512:1024], P1[:, 0:512], P1[:, 512:1024]]

    def lp_recip(out_ap, in_ap):
        def f(e):
            with nc.allow_low_precision(reason="softmax recip as bf16 operand of bf16 broadcast matmul"):
                return e.reciprocal(out_ap, in_ap)
        return f

    open_scopes = [rs]

    def finish_early(scratch_f32):
        for qq in range(16):
            sch.dma(SY, lambda e, q=qq: e.dma_start(
                out=d_out[q * 128:(q + 1) * 128, :], in_=scratch_f32[:, :]),
                waits=[(V, sch.val[V]), (T, sch.val[T]), (S, sch.val[S]), (G, sch.val[G])])
        sch.wait_only(SY, [(SY, sch.val[SY])])
        sch.wait_only(G, [(SY, sch.val[SY])])

    def seal():
        for sstack in reversed(open_scopes):
            sstack.close()
        with nc.Block() as block:
            @block.tensor
            def _(e):
                sch.run(T, e, sems)

            @block.vector
            def _(e):
                sch.run(V, e, sems)

            @block.scalar
            def _(e):
                sch.run(S, e, sems)

            @block.gpsimd
            def _(e):
                sch.run(G, e, sems)

            @block.sync
            def _(e):
                sch.run(SY, e, sems)
        es.close()
        return nc

    e_id = sch.dma(SY, lambda e: e.dma_start(out=identsb[:, :], in_=d_id))

    open_scopes = []
    att = contextlib.ExitStack()
    open_scopes.append(att)

    def abt(name, shape, dt):
        return att.enter_context(nc.sbuf_tensor(name, shape, dt, side="left"))

    masksb = abt("masksb", [128, 3840], BF16)
    bv_sb = abt("bv_sb", [128, 1024], F32)
    db_sb = abt("db_sb", [128, 256], F32)

    def MS(si, s):
        c = (si * 5 + s) * 256
        return masksb[:, c:c + 256]

    def MSW(si):
        c = (si * 5 + 1) * 256
        return masksb[:, c:c + 1024]

    for dst, src in ((masksb, d_msk), (bq_sb, d_bq), (bk_sb, d_bk), (bv_sb, d_bv),
                     (gateb_sb, d_gb), (gcol_sb, d_gc)):
        sch.dma(SY, lambda e, d=dst, s=src: e.dma_start(out=d[:, :], in_=s))
    e_db = sch.dma(SY, lambda e: e.dma_start(out=db_sb[64:65, :], in_=d_db))
    e_ones = sch.emit(V, lambda e: e.memset(ones_sb[64:65, :], 1.0))
    sch.emit(V, lambda e: e.memset(epsc[:, :], 1e-5))

    x_bf = abt("x_bf", [128, DC * EXT], BF16)
    q_fm = abt("q_fm", [128, 4 * OWN], BF16)
    k_fm = abt("k_fm", [128, 4 * EXT], BF16)
    v_sb = abt("v_sb", [128, NTT * 520], BF16)
    wq_sb = abt("wq_sb", [128, 4096], BF16)
    wk_sb = abt("wk_sb", [128, 4096], BF16)
    wv_sb = abt("wv_sb", [128, 4096], BF16)
    qg_fm = abt("qg_fm", [128, 4 * 32], BF16)
    ptgq = [abt(f"ptgq{i}", [128, 256], BF16) for i in range(2)]
    PTg = [abt(f"PTg{i}", [128, 2048], BF16) for i in range(2)]
    PTw = [abt(f"PTw{i}", [128, 1024], BF16) for i in range(2)]
    PTgz = abt("PTgz", [128, 256], BF16)
    rcp2 = [abt(f"rcp{i}", [128, 256], BF16) for i in range(2)]
    xsc = contextlib.ExitStack()
    xtm = [xsc.enter_context(nc.sbuf_tensor(f"xtm{i}", [128, 1024], F32, side="left"))
           for i in range(2)]
    xbt = [xsc.enter_context(nc.sbuf_tensor(f"xbt{i}", [128, 1024], BF16, side="left"))
           for i in range(2)]

    ptb_free = 0
    e_xdma = [0, 0]
    e_xcast = [0, 0]
    for tt in range(NTT):
        par = tt % 2
        w = [(V, e_xcast[par])]
        e_xdma[par] = sch.dma(SY, lambda e, t=tt: e.dma_start(
            out=xtm[t % 2][:, :], in_=d_x[t * 128:(t + 1) * 128, :]), waits=w)
        e_xcast[par] = sch.emit(V, lambda e, t=tt: e.tensor_copy(
            xbt[t % 2][:, :], xtm[t % 2][:, :]), waits=[(SY, e_xdma[par])])
        last = 0
        for ci in range(DC):
            w = [(V, e_xcast[par]), (SY, e_id)]
            if ci == 0:
                w.append((V, ptb_free))
            last = sch.emit(T, lambda e, t=tt, c=ci: e.transpose(
                PTB[:, c * 128:(c + 1) * 128], xbt[t % 2][:, c * 128:(c + 1) * 128],
                identsb[:, :]), waits=w)
        ptb_free = sch.emit(V, lambda e, t=tt: e.tensor_copy(
            x_bf[:, :].rearrange("p (c ee) -> p c ee", ee=EXT)[:, :, t * 128:(t + 1) * 128],
            PTB[:, :].rearrange("p (c j) -> p c j", j=128)), waits=[(T, last)])
    e_xbf = sch.val[V]
    xsc.close()
    den_fm = abt("den_fm", [128, 4096], BF16)
    rcp_fm = abt("rcp_fm", [128, 2048], BF16)
    lnsc = abt("lnsc", [128, 512], F32)

    def load_grp(g, waits):
        sch.dma(G, lambda e, gg=g: e.dma_start(
            out=wq_sb[:, :], in_=d_wq[:, gg * 4096:(gg + 1) * 4096]), waits=waits)
        sch.dma(G, lambda e, gg=g: e.dma_start(
            out=wk_sb[:, :], in_=d_wk[:, gg * 4096:(gg + 1) * 4096]))
        return sch.dma(G, lambda e, gg=g: e.dma_start(
            out=wv_sb[:, :], in_=d_wv[:, gg * 4096:(gg + 1) * 4096]))

    e_w = load_grp(0, [])
    e_allred_evac = 0
    e_ccr = 0
    e_nm = [0, 0]
    e_rS = [0, 0]
    pend_nm = None

    if phase_limit <= 1:
        finish_early(bv_sb)
        return seal()

    for g in range(NG):
        if g == 1:
            sch.wait_only(T, [(V, sch.val[V]), (S, sch.val[S])])
        e_vfree = sch.val[T]

        rot_i = 0
        e_rotev = [0, 0, 0, 0]

        def qkv_mm(dst_ap, bias_col, ntok, lhs_of, rhs_ap):
            nonlocal rot_i
            rot = rot_i % 4
            rot_i += 1
            ps = QKVROT[rot]
            w = [(V, e_xbf), (G, e_w), (V, e_rotev[rot])]
            last = 0
            for ci in range(DC):
                last = sch.emit(T, lambda e, c=ci, p=ps, lo=lhs_of, r=rhs_ap, n=ntok: e.matmul(
                    p[:, 0:n], lo(c), r(c), start=(c == 0), stop=(c == DC - 1)),
                    waits=w if ci == 0 else [])
            ev = sch.emit(V, lambda e, p=ps, d=dst_ap, bc=bias_col, n=ntok: e.tensor_scalar(
                d, p[:, 0:n], bc, None, op0=ALU.add), waits=[(T, last)])
            e_rotev[rot] = ev
            return last

        for co in range(4):
            cof = g * 4 + co
            for nt in range(4):
                qkv_mm(q_fm[:, co * OWN + nt * 512: co * OWN + (nt + 1) * 512],
                       bq_sb[:, cof:cof + 1], 512,
                       lambda ci, co_=co: wq_sb[:, (co_ * DC + ci) * 128:(co_ * DC + ci + 1) * 128],
                       lambda ci, n0=nt * 512: x_bf[:, ci * EXT + OWN0 + n0: ci * EXT + OWN0 + n0 + 512])
            for nt in range(5):
                n0 = nt * 512
                ntok = min(512, EXT - n0)
                qkv_mm(k_fm[:, co * EXT + n0: co * EXT + n0 + ntok],
                       bk_sb[:, cof:cof + 1], ntok,
                       lambda ci, co_=co: wk_sb[:, (co_ * DC + ci) * 128:(co_ * DC + ci + 1) * 128],
                       lambda ci, n0_=n0, nt_=ntok: x_bf[:, ci * EXT + n0_: ci * EXT + n0_ + nt_])
        e_vones = sch.emit(V, lambda e: e.memset(
            v_sb[:, :].rearrange("p (t h c) -> p t h c", h=8, c=65)[:, :, :, 64:65], 1.0),
            waits=[(T, e_vfree)])
        for tt in range(NTT):
            rot = rot_i % 4
            rot_i += 1
            ps = QKVROT[rot]
            w = [(V, e_xbf), (G, e_w), (V, e_rotev[rot]), (V, e_vones)]
            last = 0
            for ci in range(DC):
                last = sch.emit(T, lambda e, c=ci, p=ps, t=tt: e.matmul(
                    p, x_bf[:, c * EXT + t * 128: c * EXT + (t + 1) * 128],
                    wv_sb[:, c * 512:(c + 1) * 512], start=(c == 0), stop=(c == DC - 1)),
                    waits=w if ci == 0 else [])
            ev = sch.emit(V, lambda e, p=ps, t=tt, gg=g: e.tensor_tensor(
                v_sb[:, t * 520:(t + 1) * 520].rearrange("p (h c) -> p h c", c=65)[:, :, 0:64],
                p.rearrange("p (h c) -> p h c", c=64),
                bv_sb[:, gg * 512:(gg + 1) * 512].rearrange("p (h c) -> p h c", c=64),
                op=ALU.add), waits=[(T, last)])
            e_rotev[rot] = ev
        e_A_T, e_A_V = sch.val[T], sch.val[V]

        if g == 0:
            e_w_next = load_grp(1, [(T, e_A_T)])

        if phase_limit <= 2 + 3 * g:
            finish_early(bv_sb)
            return seal()
        e_qg = 0
        for co in range(4):
            cof = g * 4 + co
            last = 0
            for ci in range(DC):
                w = []
                if ci == 0:
                    w = [(V, e_A_V), (V, e_qg)]
                last = sch.emit(T, lambda e, c=ci, co_=co: e.matmul(
                    P2[:, 0:32], wq_sb[:, (co_ * DC + c) * 128:(co_ * DC + c + 1) * 128],
                    x_bf[:, c * EXT: c * EXT + 32],
                    start=(c == 0), stop=(c == DC - 1)), waits=w)
            e_qg = sch.emit(V, lambda e, co_=co, cf=cof: e.tensor_scalar(
                qg_fm[:, co_ * 32:(co_ + 1) * 32], P2[:, 0:32], bq_sb[:, cf:cf + 1],
                None, op0=ALU.add), waits=[(T, last)])

        e_gexp = [0, 0]
        e_gpv = 0
        HORD = (0, 2, 4, 6, 1, 3, 5, 7)
        for kb in range(16):
            par = kb % 2
            last_sc = 0
            for hh in HORD:
                base = (hh % 2) * 64
                co = hh // 2
                w = []
                if hh == 0:
                    w = [(V, e_qg), (S, e_gexp[par]), (V, e_allred_evac if g == 1 else 0)]
                cc = (hh % 2) * 512 + par * 256 + (hh // 2) * 32
                last_sc = sch.emit(T, lambda e, h=hh, b=base, c=co, k=kb, cc_=cc: e.matmul(
                    P4[:, cc_: cc_ + 32],
                    k_fm[b:b + 64, c * EXT + (k + 2) * 128: c * EXT + (k + 3) * 128],
                    qg_fm[b:b + 64, c * 32:(c + 1) * 32],
                    start=(h in (0, 1)), stop=(h in (6, 7))), waits=w)
            e_gexp[par] = sch.emit(S, lambda e, p=par, o=par * 256: e.activation(
                ptgq[p][:, :],
                P4[:, :].rearrange("q (bk c) -> q bk c", c=512)[:, :, o:o + 128],
                ACT.Exp, scale=0.125),
                waits=[(T, last_sc), (T, e_gpv if kb >= 2 else 0)])
            for hh in range(HPG):
                tgt = (P2[0:65, hh * 32: (hh + 1) * 32] if hh < 4
                       else P0[0:65, (hh - 4) * 32: (hh - 3) * 32])
                e_gpv = sch.emit(T, lambda e, h=hh, k=kb, p=par, tg=tgt: e.matmul(
                    tg, v_sb[:, (k + 2) * 520 + h * 65: (k + 2) * 520 + (h + 1) * 65],
                    ptgq[p][:, (h % 2) * 128 + (h // 2) * 32: (h % 2) * 128 + (h // 2) * 32 + 32],
                    start=(k == 0 and h in (0, 4)), stop=(k == 15 and h in (3, 7))),
                    waits=[(S, e_gexp[par])])
        sch.emit(V, lambda e, gg=g: e.tensor_copy(
            allred[:, gg * 256: gg * 256 + 128], P2[0:65, 0:128]), waits=[(T, e_gpv)])
        e_allred_evac = sch.emit(V, lambda e, gg=g: e.tensor_copy(
            allred[:, gg * 256 + 128: gg * 256 + 256], P0[0:65, 0:128]))

        if phase_limit <= 3 + 3 * g:
            finish_early(bv_sb)
            return seal()
        yb = [P4[0:65, 0:256], P2[0:65, 0:256]]
        ybsrc = [P4, P2]
        scb = [P0, P1]

        def nm_bc(pend, c):
            b_p, ch_p, e_dl, slot = pend
            par = c % 2
            e_b = sch.emit(T, lambda e, pr=par, c_=c, sl=slot: e.matmul(
                P4[:, 512 + pr * 256: 512 + (pr + 1) * 256], ones_sb[64:65, 0:128],
                den_fm[64:65, sl * 2048 + c_ * 256: sl * 2048 + (c_ + 1) * 256],
                start=True, stop=True),
                waits=[(V, e_dl), (S, e_rS[par]), (G, e_nm[par]), (S, e_ptg_exp)])
            return (e_b, b_p, ch_p, par, c, e_dl)

        def nm_fin(tok):
            e_b, b_p, ch_p, par, c, e_dl = tok
            sch.emit(S, lambda e, pr=par, b_=b_p: e.activation(
                lnsc[b_:b_ + 64, pr * 256:(pr + 1) * 256],
                P4[b_:b_ + 64, 512 + pr * 256: 512 + (pr + 1) * 256], ACT.Ln),
                waits=[(T, e_b)])
            e_rS[par] = sch.emit(S, lambda e, pr=par, b_=b_p, c_=c: e.activation(
                rcp_fm[b_:b_ + 64, c_ * 256:(c_ + 1) * 256],
                lnsc[b_:b_ + 64, pr * 256:(pr + 1) * 256], ACT.Exp, scale=-1.0))
            e_nm[par] = sch.emit(G, lambda e, b_=b_p, ch_=ch_p, c_=c: e.tensor_tensor(
                y_fm[b_:b_ + 64, ch_ * 2048 + c_ * 256: ch_ * 2048 + (c_ + 1) * 256],
                y_fm[b_:b_ + 64, ch_ * 2048 + c_ * 256: ch_ * 2048 + (c_ + 1) * 256],
                rcp_fm[b_:b_ + 64, c_ * 256:(c_ + 1) * 256], op=ALU.mult),
                waits=[(S, e_rS[par]), (V, e_dl)])

        e_exp = {}
        e_sc = {}
        e_pv = {}
        e_den = {}
        e_msk = {}
        e_ptg_ms = [0, 0]
        e_ptg0v = [0, 0]
        e_ptg_exp = 0
        for hi, hh in enumerate((0, 2, 4, 6, 1, 3, 5, 7)):
            H = g * HPG + hh
            hp = hi % 2
            base = (hh % 2) * 64
            co = hh // 2
            ch = H // 2
            if hi == 4:
                sch.emit(T, lambda e: e.drain())
            for half in range(2):
                for q4 in range(2):
                    w = []
                    if half == 0 and q4 == 0:
                        w = [(V, e_allred_evac), (S, e_gexp[0]), (S, e_gexp[1]),
                             (T, e_pv.get((hi - 2, 7), 0)),
                             (V, e_den.get((hi - 1, 6), 0)),
                             (S, e_rS[0]), (S, e_rS[1]),
                             (V, e_ptg_ms[hp]), (V, e_ptg0v[hp])]
                    elif half == 1 and q4 == 0:
                        w = [(S, e_ptg_exp)]
                    lastmm = sch.emit(T, lambda e, b=base, c=co, h2=half, q=q4: e.matmul(
                        P4[:, q * 512:(q + 1) * 512],
                        k_fm[b:b + 64, c * EXT: c * EXT + 128],
                        q_fm[b:b + 64, c * OWN + h2 * 1024 + q * 512:
                             c * OWN + h2 * 1024 + (q + 1) * 512],
                        start=True, stop=True), waits=w)
                e_ptg_exp = sch.emit(S, lambda e, h2=half, p=hp: e.activation(
                    PTg[p][:, h2 * 1024:(h2 + 1) * 1024], P4[:, 0:1024], ACT.Exp,
                    scale=0.125), waits=[(T, lastmm)])
            sch.emit(V, lambda e, p=hp: e.memset(PTg[p][32:64, :], 0.0),
                     waits=[(S, e_ptg_exp)])
            e_ptg_ms[hp] = sch.emit(V, lambda e, p=hp: e.memset(PTg[p][64:128, :], 0.0))
            e_ptg0v[hp] = sch.emit(V, lambda e, p=hp: e.tensor_tensor(
                PTgz[:, :], PTg[p][:, 0:256], MS(1, 0), op=ALU.mult),
                waits=[(V, e_ptg_ms[hp]), (SY, e_db)])

            def emit_scores(p):
                par = p % 2
                pexp = (e_exp.get((hi, p - 2)) or
                        (e_exp.get((hi - 1, 6 + par)) if hi else 0) or 0)
                last = 0
                for s in range(1, 5):
                    w = [(V, e_ptg_ms[hp]), (S, pexp)] if s == 1 else []
                    last = sch.emit(T, lambda e, b=base, c=co, pp=p, s_=s, pr=par: e.matmul(
                        scb[pr][:, (s_ - 1) * 256:s_ * 256],
                        k_fm[b:b + 64, c * EXT + (2 * pp + s_) * 128: c * EXT + (2 * pp + s_ + 1) * 128],
                        q_fm[b:b + 64, c * OWN + pp * 256: c * OWN + (pp + 1) * 256],
                        start=(s_ in (1, 3)), stop=(s_ == 4)), waits=w)
                e_sc[(hi, p)] = last

            emit_scores(0)
            emit_scores(1)
            for p in range(NPAIR):
                par = p % 2

                def prev(d):
                    return d.get((hi, p - 2)) or (d.get((hi - 1, 6 + par)) if hi else 0) or 0

                si = 1 if p == 0 else (2 if p == 7 else 0)
                e_exp[(hi, p)] = sch.emit(S, lambda e, pr=par: e.activation(
                    PTw[pr][:, :], scb[pr][:, 0:1024], ACT.Exp, scale=0.125),
                    waits=[(T, e_sc[(hi, p)]), (T, prev(e_pv))])
                e_msk[(hi, p)] = sch.emit(G, lambda e, pr=par, si_=si: e.tensor_tensor(
                    PTw[pr][:, :], PTw[pr][:, :], MSW(si_), op=ALU.mult),
                    waits=[(S, e_exp[(hi, p)])])
                tok = nm_bc(pend_nm, p) if pend_nm is not None else None
                if p + 2 < NPAIR:
                    emit_scores(p + 2)
                wpv = [(G, e_msk[(hi, p)]), (V, prev(e_den))]
                if p == 0:
                    wpv.append((V, e_ptg0v[hp]))
                rhs_glob = PTgz[:, :] if p == 0 else PTg[hp][:, p * 256:(p + 1) * 256]
                lastpv = sch.emit(T, lambda e, h=hh, pr=par, rg=rhs_glob: e.matmul(
                    yb[pr], v_sb[:, h * 65:(h + 1) * 65], rg, start=True, stop=False),
                    waits=wpv)
                for s in range(1, 5):
                    lastpv = sch.emit(T, lambda e, h=hh, pp=p, s_=s, pr=par: e.matmul(
                        yb[pr], v_sb[:, (2 * pp + s_) * 520 + h * 65: (2 * pp + s_) * 520 + (h + 1) * 65],
                        PTw[pr][:, (s_ - 1) * 256: s_ * 256],
                        start=False, stop=(s_ == 4)))
                e_pv[(hi, p)] = lastpv
                sch.emit(V, lambda e, b=base, c=ch, pp=p, pr=par: e.tensor_copy(
                    y_fm[b:b + 64, c * 2048 + pp * 256: c * 2048 + (pp + 1) * 256],
                    ybsrc[pr][0:64, 0:256]), waits=[(T, lastpv)])
                sl0 = (hi % 2) * 2048
                if p == 0:
                    e_den[(hi, p)] = sch.emit(V, lambda e, pr=par, s0=sl0: e.tensor_tensor(
                        den_fm[64:65, s0:s0 + 256], ybsrc[pr][64:65, 0:256],
                        db_sb[64:65, :], op=ALU.add))
                else:
                    e_den[(hi, p)] = sch.emit(V, lambda e, pp=p, pr=par, s0=sl0: e.tensor_copy(
                        den_fm[64:65, s0 + pp * 256: s0 + (pp + 1) * 256],
                        ybsrc[pr][64:65, 0:256]))
                if tok is not None:
                    nm_fin(tok)
            pend_nm = (base, ch, e_den[(hi, NPAIR - 1)], hi % 2)
        for c in range(NPAIR):
            nm_fin(nm_bc(pend_nm, c))
        pend_nm = None
        if g == 1:
            e_ccw = sch.dma(SY, lambda e: e.dma_start(out=cc_in.ap(), in_=allred[:, :]),
                            waits=[(V, e_allred_evac)])
            e_cc = sch.emit(G, lambda e: e.collective_compute(
                "AllReduce", ALU.add, replica_groups=[[0, 1, 2, 3], [4, 5, 6, 7]],
                ins=[cc_in.ap()], outs=[cc_out.ap()]), waits=[(SY, e_ccw)])
            e_ccr = sch.dma(SY, lambda e: e.dma_start(out=globred[:, :], in_=cc_out.ap()),
                            waits=[(G, e_cc)])
        if g == 0:
            e_w = e_w_next

    if phase_limit <= 7:
        finish_early(bv_sb)
        return seal()
    attn_T, attn_V, attn_S, attn_G = sch.val[T], sch.val[V], sch.val[S], sch.val[G]
    e_rcg = sch.emit(V, lp_recip(rcp2[0][64:65, :], globred[64:65, 0:256]),
                     waits=[(SY, e_ccr), (V, attn_V), (G, attn_G)])
    e_rcg2 = sch.emit(V, lp_recip(rcp2[1][64:65, :], globred[64:65, 256:512]))
    e_bc1 = sch.emit(T, lambda e: e.matmul(P2[0:64, 0:256], ones_sb[64:65, 0:64],
                                           rcp2[0][64:65, :], start=True, stop=True),
                     waits=[(V, e_rcg2), (T, attn_T), (S, attn_S)])
    e_bc2 = sch.emit(T, lambda e: e.matmul(P2[0:64, 256:512], ones_sb[64:65, 0:64],
                                           rcp2[1][64:65, :], start=False, stop=True))
    e_dgb = sch.emit(S, lambda e: e.copy(dengb[0:64, :], P2[0:64, 0:512]),
                     waits=[(T, e_bc2), (S, attn_S)])
    e_ygn = sch.emit(V, lambda e: e.scalar_tensor_tensor(
        ygn[0:64, :], globred[0:64, :], gcol_sb[0:64, :], dengb[0:64, :],
        op0=ALU.mult, op1=ALU.mult), waits=[(S, e_dgb)])
    e_ygn2 = sch.dma(SY, lambda e: e.dma_start(out=ygn[64:128, :], in_=ygn[0:64, :]),
                     waits=[(V, e_ygn)])
    for H in range(16):
        b = (H % 2) * 64
        ch = H // 2
        sch.emit(V, lambda e, b_=b, c=ch, h=H: e.tensor_tensor(
            y_fm[b_:b_ + 64, c * 2048: c * 2048 + 32],
            y_fm[b_:b_ + 64, c * 2048: c * 2048 + 32],
            ygn[b_:b_ + 64, h * 32:(h + 1) * 32], op=ALU.add),
            waits=[(SY, e_ygn2)])
    e_fix_done = sch.val[V]
    if DBG:
        sch.dma(G, lambda e: e.dma_start(out=d_dbgy, in_=y_fm[:, :]),
                waits=[(V, e_fix_done)])

    att.close()
    if phase_limit <= 8:
        finish_early(bv_sb)
        return seal()
    sch.barrier()
    bar_T = sch.val[T]

    sc2 = contextlib.ExitStack()
    open_scopes.append(sc2)

    def fbt(name, shape, dt):
        return sc2.enter_context(nc.sbuf_tensor(name, shape, dt, side="left"))

    wo_sb = fbt("wo_sb", [128, 32768], BF16)
    x1_tm = fbt("x1_tm", [128, 16 * 1024], F32)
    sqs = fbt("sqs", [128, 1024], BF16)
    wob_sb = fbt("wob_sb", [128, 1024], F32)
    scF = contextlib.ExitStack()
    open_scopes.append(scF)

    def f2t(name, shape, dt):
        return scF.enter_context(nc.sbuf_tensor(name, shape, dt, side="left"))

    wproj_sb = f2t("wproj_sb", [128, 8192], BF16)
    bpr_sb = f2t("bpr_sb", [128, 1024], F32)
    xq = [f2t(f"xq{i}", [128, 1024], F32) for i in range(2)]
    xp = xq

    e_wp = sch.dma(G, lambda e: e.dma_start(out=wproj_sb[:, :], in_=d_wp))
    for dst, srcd in ((bpr_sb, d_bp), (wob_sb, d_wob)):
        sch.dma(G, lambda e, d=dst, s=srcd: e.dma_start(out=d[:, :], in_=s))
    e_wo = 0
    for i in range(8):
        e_wo = sch.dma(G, lambda e, i_=i: e.dma_start(
            out=wo_sb[:, i_ * 4096:(i_ + 1) * 4096], in_=d_wo[:, i_ * 4096:(i_ + 1) * 4096]))

    def ln_chain(src_t, dst_ap, e_src_v, dst_waits=()):
        eS = sch.emit(S, lambda e: e.activation(sqs[:, :], src_t[:, :], ACT.Square,
                                                accum_out=stc[:, 1:2]),
                      waits=[(V, e_src_v)])
        e1 = sch.emit(V, lambda e: e.reduce_sum(stc[:, 0:1], src_t[:, :], axis=AX.X),
                      waits=[(V, e_src_v)])
        sch.wait_only(V, [(S, eS), (V, e1)])
        sch.emit(V, lambda e: e.tensor_scalar(stc[:, 2:3], stc[:, 0:1], 1.0 / 1024, None,
                                              op0=ALU.mult))
        sch.emit(V, lambda e: e.tensor_scalar(stc[:, 3:4], stc[:, 1:2], 1.0 / 1024, None,
                                              op0=ALU.mult))
        sch.emit(V, lambda e: e.tensor_tensor(stc[:, 4:5], stc[:, 2:3], stc[:, 2:3],
                                              op=ALU.mult))
        e5 = sch.emit(V, lambda e: e.tensor_tensor(stc[:, 4:5], stc[:, 3:4], stc[:, 4:5],
                                                   op=ALU.subtract))
        e6 = sch.emit(S, lambda e: e.activation(stc[:, 5:6], stc[:, 4:5], ACT.Sqrt,
                                                bias=epsc[:, :]), waits=[(V, e5)])
        e7 = sch.emit(V, lambda e: e.reciprocal(stc[:, 6:7], stc[:, 5:6]), waits=[(S, e6)])
        sch.emit(V, lambda e: e.tensor_tensor(stc[:, 7:8], stc[:, 2:3], stc[:, 6:7],
                                              op=ALU.mult))
        e9 = sch.emit(V, lambda e: e.tensor_scalar(stc[:, 7:8], stc[:, 7:8], -1.0, None,
                                                   op0=ALU.mult))
        sch.wait_only(V, [(V, e9)] + list(dst_waits))
        e10 = sch.emit(V, lambda e, d=dst_ap: e.tensor_scalar(
            d, src_t[:, :], stc[:, 6:7], stc[:, 7:8], op0=ALU.mult, op1=ALU.add))
        return e10, eS

    projb = [P0, P1]
    e_projmm = [0, 0]
    e_xq = [0, 0]
    e_x1 = {}
    e_xpS = {}
    e_xpv = {}
    for qt in range(16):
        par = qt % 2
        e_xq[par] = sch.dma(SY, lambda e, q=qt: e.dma_start(
            out=xq[q % 2][:, :], in_=d_x[OWN0 + q * 128: OWN0 + (q + 1) * 128, :]),
            waits=[(V, e_x1.get(qt - 2, (0,))[0] if qt >= 2 else 0),
                   (S, e_xpS.get(qt - 2, 0))])
        last = 0
        for dc in range(2):
            for ci in range(DC):
                w = []
                if dc == 0 and ci == 0:
                    w = [(V, e_fix_done), (G, e_wp), (V, e_x1.get(qt - 2, (0,))[0] if qt >= 2 else 0)]
                last = sch.emit(T, lambda e, d=dc, c=ci, q=qt, pb=projb[par]: e.matmul(
                    pb[:, d * 512:(d + 1) * 512],
                    y_fm[:, c * 2048 + q * 128: c * 2048 + (q + 1) * 128],
                    wproj_sb[:, c * 1024 + d * 512: c * 1024 + (d + 1) * 512],
                    start=(c == 0), stop=(c == DC - 1)), waits=w)
        e_projmm[par] = last
        ev = sch.emit(V, lambda e, pb=projb[par], q=qt: e.tensor_tensor(
            xp[q % 2][:, :], pb[:, 0:1024], xq[q % 2][:, :], op=ALU.add),
            waits=[(T, last), (SY, e_xq[par]), (S, e_xpS.get(qt - 2, 0))])
        ev = sch.emit(V, lambda e, q=qt: e.tensor_tensor(
            xp[q % 2][:, :], xp[q % 2][:, :], bpr_sb[:, :], op=ALU.add))
        e_xpv[qt] = ev
        e_x1[qt] = ln_chain(xp[par], x1_tm[:, qt * 1024:(qt + 1) * 1024], ev)
        e_xpS[qt] = e_x1[qt][1]
    e_F_T, e_F_V, e_F_S = sch.val[T], sch.val[V], sch.val[S]

    if DBG:
        for qt in range(16):
            sch.dma(SY, lambda e, q=qt: e.dma_start(
                out=d_dbgx1[:, q * 1024:(q + 1) * 1024], in_=x1_tm[:, q * 1024:(q + 1) * 1024]),
                waits=[(V, e_x1[qt][0])])
    if phase_limit <= 9:
        finish_early(xq[0])
        scF.close()
        sc2.close()
        rs.close()
        return seal()
    sch.barrier()
    scF.close()
    rs.close()

    def gbt(name, shape, dt):
        return scF.enter_context(nc.sbuf_tensor(name, shape, dt, side="left"))

    wib_sb = gbt("wib_sb", [128, 32], F32)
    gA_sb = gbt("gA_sb", [128, 320], BF16)
    B_sb = gbt("B_sb", [32, 4096], BF16)
    wi_sb = [gbt(f"wi_sb{i}", [128, 2048], BF16) for i in range(2)]
    x1bt = [gbt(f"x1bt{i}", [128, 1024], BF16) for i in range(2)]
    x1fm = gbt("x1fm", [128, 8 * 512], BF16)
    inter = gbt("inter", [128, 32 * 512], BF16)
    rT_sb = gbt("rT_sb", [32, 512], BF16)
    lg_sb = gbt("lg_sb", [128, 8], F32)
    msk_sb = gbt("msk_sb", [128, 8], F32)
    rm_sb = gbt("rm_sb", [128, 32], BF16)
    x2p = gbt("x2p", [128, 1024], F32)
    out_sb = [gbt("out_sb0", [128, 1024], F32)] * 2
    for dst, srcd in ((wib_sb, d_wib), (gA_sb, d_gA), (B_sb, d_B)):
        e_misc = sch.dma(G, lambda e, d=dst, s=srcd: e.dma_start(out=d[:, :], in_=s))

    e_wi = [0, 0]
    e_wicons = [0, 0]
    e_intermm = {}
    e_gate = {}
    e_outdma = [0, 0]
    e_wocons = {}
    e_ln2 = {}
    e_x1c = 0
    for st in range(NST):
        e_x1fm = 0
        e_tp2 = [0, 0]
        for sub in range(4):
            qt = st * 4 + sub
            e_x1c = sch.emit(V, lambda e, q=qt, s=sub: e.tensor_copy(
                x1bt[s % 2][:, :], x1_tm[:, q * 1024:(q + 1) * 1024]),
                waits=[(V, e_x1.get(qt, (0,))[0]),
                       (T, e_tp2[sub % 2]),
                       (T, e_intermm.get((st - 1, FC - 1), 0)),
                       (T, e_F_T if st == 0 else 0),
                       (V, e_F_V if st == 0 else 0)])
            last = 0
            for ci in range(DC):
                w = [(V, e_x1c), (V, ptb_free)] if ci == 0 else []
                last = sch.emit(T, lambda e, s=sub, c=ci: e.transpose(
                    PTB[:, c * 128:(c + 1) * 128],
                    x1bt[s % 2][:, c * 128:(c + 1) * 128],
                    identsb[:, :]), waits=w)
            e_tp2[sub % 2] = last
            ptb_free = sch.emit(V, lambda e, s=sub: e.tensor_copy(
                x1fm[:, :].rearrange("p (c qq) -> p c qq", qq=512)[:, :, s * 128:(s + 1) * 128],
                PTB[:, :].rearrange("p (c j) -> p c j", j=128)), waits=[(T, last)])
        e_x1fm = ptb_free

        for sub in range(4):
            last = 0
            for ci in range(DC):
                w = []
                if ci == 0:
                    w = [(V, e_x1fm), (G, e_misc), (S, e_dgb),
                         (V, e_gate.get((st, sub - 1), 0) or
                          (e_gate.get((st - 1, 3), 0) if st else 0))]
                last = sch.emit(T, lambda e, c=ci, s=sub: e.matmul(
                    P2[:, 0:40], x1fm[:, c * 512 + s * 128: c * 512 + (s + 1) * 128],
                    gA_sb[:, c * 40:(c + 1) * 40], start=(c == 0), stop=(c == DC - 1)), waits=w)
            sch.emit(V, lambda e: e.tensor_tensor(lg_sb[:, :], P2[:, 0:8], gateb_sb[:, :],
                                                  op=ALU.add), waits=[(T, last)])
            e2 = sch.emit(V, lambda e: e.reduce_max(stc[:, 0:1], lg_sb[:, :], axis=AX.X))
            sch.wait_only(V, [(V, e2)])
            e3 = sch.emit(V, lambda e: e.tensor_scalar(msk_sb[:, :], lg_sb[:, :],
                                                       stc[:, 0:1], None, op0=ALU.is_ge))
            sch.wait_only(V, [(V, e3)])
            for ex in range(8):
                e4 = sch.emit(V, lambda e, x=ex: e.tensor_scalar(
                    rm_sb[:, x * 4:(x + 1) * 4], P2[:, 8 + x * 4: 8 + (x + 1) * 4],
                    msk_sb[:, x:x + 1], None, op0=ALU.mult))
            etp = sch.emit(T, lambda e: e.transpose(PTB[0:32, 0:128], rm_sb[:, :],
                                                    identsb[:, :]),
                           waits=[(V, e4), (V, ptb_free)])
            ptb_free = sch.emit(V, lambda e, s=sub: e.tensor_copy(
                rT_sb[0:32, s * 128:(s + 1) * 128], PTB[0:32, 0:128]), waits=[(T, etp)])
            e_gate[(st, sub)] = ptb_free

        e_reluS = {}
        e_reluV = {}
        for fc in range(FC):
            par = (fc // 2) % 2
            if fc % 2 == 0:
                e_wi[par] = sch.dma(G, lambda e, f=fc: e.dma_start(
                    out=wi_sb[(f // 2) % 2][:, :], in_=d_wi[:, f * 1024: f * 1024 + 2048]),
                    waits=[(T, e_wicons[par]), (G, e_wo if (st == 0 and fc == 0) else 0)])
            rot = fc % 4
            ps = QKVROT[rot]
            last = 0
            for ci in range(DC):
                w = []
                if ci == 0:
                    w = [(G, e_wi[par]), (V, e_x1fm), (V, e_gate[(st, 3)])]
                    if fc >= 4:
                        w.append(((S if (fc - 4) % 2 == 0 else V),
                                  e_reluS.get(fc - 4, 0) or e_reluV.get(fc - 4, 0)))
                last = sch.emit(T, lambda e, c=ci, f=fc, p=ps: e.matmul(
                    p, wi_sb[(f // 2) % 2][:, (f % 2) * 1024 + c * 128:
                                           (f % 2) * 1024 + (c + 1) * 128],
                    x1fm[:, c * 512:(c + 1) * 512], start=(c == 0), stop=False), waits=w)
            last = sch.emit(T, lambda e, f=fc, p=ps: e.matmul(
                p, B_sb[:, f * 128:(f + 1) * 128], rT_sb[:, :], start=False, stop=True))
            e_intermm[(st, fc)] = last
            e_wicons[par] = last
            if fc % 2 == 0:
                e_reluS[fc] = sch.emit(S, lambda e, f=fc, p=ps: e.activation(
                    inter[:, f * 512:(f + 1) * 512], p, ACT.Relu, bias=wib_sb[:, f:f + 1]),
                    waits=[(T, last)])
            else:
                e_reluV[fc] = sch.emit(V, lambda e, f=fc, p=ps: e.tensor_scalar(
                    inter[:, f * 512:(f + 1) * 512], p, wib_sb[:, f:f + 1], 0.0,
                    op0=ALU.add, op1=ALU.max), waits=[(T, last)])
        e_relu_allS = sch.val[S]
        e_relu_allV = sch.val[V]

        for sub in range(4):
            qt = st * 4 + sub
            par = qt % 2
            last = 0
            for dc in range(2):
                for fc in range(FC):
                    w = []
                    if dc == 0 and fc == 0:
                        w = [(S, e_relu_allS), (V, e_relu_allV), (G, e_wo),
                             (V, e_wocons.get(qt - 1, 0))]
                    last = sch.emit(T, lambda e, d=dc, f=fc, s=sub: e.matmul(
                        P4[:, d * 512:(d + 1) * 512],
                        inter[:, f * 512 + s * 128: f * 512 + (s + 1) * 128],
                        wo_sb[:, f * 1024 + d * 512: f * 1024 + (d + 1) * 512],
                        start=(f == 0), stop=(f == FC - 1)), waits=w)
            ev = sch.emit(V, lambda e, q=qt: e.tensor_tensor(
                x2p[:, :], P4[:, 0:1024], x1_tm[:, q * 1024:(q + 1) * 1024], op=ALU.add),
                waits=[(T, last), (S, e_ln2.get(qt - 1, (0, 0))[1])])
            ev = sch.emit(V, lambda e: e.tensor_tensor(x2p[:, :], x2p[:, :], wob_sb[:, :],
                                                       op=ALU.add))
            e_wocons[qt] = ev
            e_ln2[qt] = ln_chain(x2p, out_sb[par][:, :], ev,
                                 dst_waits=[(SY, e_outdma[par])])
            e_outdma[par] = sch.dma(SY, lambda e, q=qt, pp=par: e.dma_start(
                out=d_out[q * 128:(q + 1) * 128, :], in_=out_sb[pp][:, :]),
                waits=[(V, e_ln2[qt][0])])

    sch.wait_only(SY, [(SY, sch.val[SY])])
    sch.wait_only(G, [(SY, sch.val[SY])])
    scF.close()
    sc2.close()
    return seal()


def _build_masks(r):
    t0 = r * 2048
    m = np.zeros((3, 5, 128, 256), np.float32)
    for si, p in ((0, 3), (1, 0), (2, 7)):
        for s in range(5):
            kb_ext = 0 if s == 0 else 2 * p + s
            ke = kb_ext * 128 + np.arange(128)
            i = t0 + 256 * p + np.arange(256)
            j = np.where(ke < 32, ke, t0 + (ke - 256))
            valid = (ke < 32) | ((ke >= 128) & (t0 + ke - 256 >= 0) & (t0 + ke - 256 < 8192))
            J, I = np.meshgrid(j, i, indexing="ij")
            Vv = valid[:, None]
            win = np.abs(I - J) <= 128
            glob = J < 32
            if s == 0:
                mm = Vv & glob & ~win & (I >= 32)
            else:
                mm = Vv & win & (I >= 32)
            m[si, s] = mm.astype(np.float32)
    return np.ascontiguousarray(m.transpose(2, 0, 1, 3).reshape(128, 3840)).astype(BF)


def kernel(**inp):
    x = np.asarray(inp["L0_attn_x"], np.float32)
    Wat = np.asarray(inp["L0_attn_W_attn"], np.float32)
    bat = np.asarray(inp["L0_attn_b_attn"], np.float32)
    Wpr = np.asarray(inp["L0_attn_W_proj"], np.float32)
    bpr = np.asarray(inp["L0_attn_b_proj"], np.float32)
    gW = np.asarray(inp["L0_ffn_gate_W"], np.float32)
    gb = np.asarray(inp["L0_ffn_gate_b"], np.float32)
    wiW = np.asarray(inp["L0_ffn_wi_W"], np.float32)
    wib = np.asarray(inp["L0_ffn_wi_b"], np.float32)
    woW = np.asarray(inp["L0_ffn_wo_W"], np.float32)
    wob = np.asarray(inp["L0_ffn_wo_b"], np.float32)
    lA = np.asarray(inp["L0_ffn_lora_A"], np.float32)
    lB = np.asarray(inp["L0_ffn_lora_B"], np.float32)

    C = np.ascontiguousarray
    wq_img = C(Wat[:, 0:1024].reshape(8, 128, 8, 128).transpose(1, 2, 0, 3).reshape(128, 8192))
    wk_img = C(Wat[:, 1024:2048].reshape(8, 128, 8, 128).transpose(1, 2, 0, 3).reshape(128, 8192))
    wv_img = C(Wat[:, 2048:3072].reshape(8, 128, 2, 512).transpose(1, 2, 0, 3).reshape(128, 8192))
    bq_img = C(bat[0:1024].reshape(8, 128).T)
    bk_img = C(bat[1024:2048].reshape(8, 128).T)
    bv_img = C(np.broadcast_to(bat[2048:3072], (128, 1024)))
    wp_img = C(Wpr.reshape(8, 128, 1024).transpose(1, 0, 2).reshape(128, 8192))
    bp_img = C(np.broadcast_to(bpr, (128, 1024)))
    M40 = np.concatenate([gW.T, lA.reshape(32, 1024).T], axis=1)
    gA_img = C(M40.reshape(8, 128, 40).transpose(1, 0, 2).reshape(128, 320))
    gb_img = C(np.broadcast_to(gb, (128, 8)))
    B_img = C(lB.transpose(0, 2, 1).reshape(32, 4096))
    wi_img = C(wiW.reshape(32, 128, 8, 128).transpose(3, 0, 2, 1).reshape(128, 32768))
    wib_img = C(wib.reshape(32, 128).T)
    wo_img = C(woW.reshape(1024, 32, 128).transpose(2, 1, 0).reshape(128, 32768))
    wob_img = C(np.broadcast_to(wob, (128, 1024)))
    ident = np.eye(128, dtype=BF)

    nc = build()
    in_maps = []
    for c in range(8):
        b, r = divmod(c, 4)
        t0 = r * 2048
        xe = np.zeros((EXT, 1024), np.float32)
        xe[0:32] = x[b, 0:32]
        if r > 0:
            xe[128:256] = x[b, t0 - 128:t0]
        xe[256:2304] = x[b, t0:t0 + 2048]
        if r < 3:
            xe[2304:2432] = x[b, t0 + 2048:t0 + 2176]
        db = np.zeros((1, 256), np.float32)
        if r == 0:
            db[0, 0:32] = 1.0
        gcol = np.full((128, 1), 1.0 if r == 0 else 0.0, np.float32)
        in_maps.append({
            "x_ext": xe, "wq_img": wq_img, "wk_img": wk_img, "wv_img": wv_img,
            "bq_img": bq_img, "bk_img": bk_img, "bv_img": bv_img,
            "wproj_img": wp_img, "bproj_img": bp_img,
            "gA_img": gA_img, "gateb_img": gb_img, "B_img": B_img,
            "wi_img": wi_img, "wib_img": wib_img, "wo_img": wo_img, "wob_img": wob_img,
            "masks": _build_masks(r), "den_bias": db, "g_col": gcol, "ident": ident,
        })
    res = run_bass_kernel_spmd(nc, in_maps, core_ids=list(range(8)))
    out = np.zeros((2, 8192, 1024), np.float32)
    for c in range(8):
        b, r = divmod(c, 4)
        out[b, r * 2048:(r + 1) * 2048] = res.results[c]["out"]
    return out


# version 12
# speedup vs baseline: 1.2131x; 1.0132x over previous
import contextlib
import numpy as np
import ml_dtypes
import concourse.bass as bass
import concourse.mybir as mybir
from concourse.bass_utils import run_bass_kernel_spmd

F32 = mybir.dt.float32
BF16 = mybir.dt.bfloat16
ALU = mybir.AluOpType
ACT = mybir.ActivationFunctionType
AX = mybir.AxisListType
BF = ml_dtypes.bfloat16

EXT = 2432
OWN = 2048
OWN0 = 256
NTT = 19
DC = 8
NG = 2
HPG = 8
NPAIR = 8
FC = 32
NST = 4
ENGS = ("tensor", "vector", "scalar", "gpsimd", "sync")
T, V, S, G, SY = ENGS


class Sched:

    def __init__(self):
        self.q = {e: [] for e in ENGS}
        self.val = {e: 0 for e in ENGS}
        self.lastw = {e: {} for e in ENGS}

    def _prune(self, eng, waits):
        wl = []
        for (se, v) in waits:
            if v and v > self.lastw[eng].get(se, 0):
                self.lastw[eng][se] = v
                wl.append((se, v))
        return wl

    def emit(self, eng, fn, waits=(), amt=1):
        self.q[eng].append((tuple(self._prune(eng, waits)), fn, amt))
        self.val[eng] += amt
        return self.val[eng]

    def dma(self, eng, fn, waits=()):
        return self.emit(eng, fn, waits, amt=16)

    def wait_only(self, eng, waits):
        wl = self._prune(eng, waits)
        if wl:
            self.q[eng].append((tuple(wl), None, 0))

    def barrier(self):
        snap = dict(self.val)
        for e in ENGS:
            self.wait_only(e, [(o, snap[o]) for o in ENGS if o != e])

    def run(self, eng, e, sems):
        own = sems[eng]
        cnt = 0
        chain = (eng == "vector")
        for wl, fn, amt in self.q[eng]:
            for (se, v) in wl:
                e.wait_ge(sems[se], v)
            if fn is not None:
                if chain and cnt:
                    e.wait_ge(own, cnt)
                fn(e).then_inc(own, amt)
                cnt += amt




DBG = False
NORM_MODE = 2


def build(phase_limit=99):
    nc = bass.Bass("TRN2", target_bir_lowering=False)

    d_x = nc.dram_tensor("x_ext", [EXT, 1024], F32, kind="ExternalInput").ap()
    d_wq = nc.dram_tensor("wq_img", [128, 8192], F32, kind="ExternalInput").ap()
    d_wk = nc.dram_tensor("wk_img", [128, 8192], F32, kind="ExternalInput").ap()
    d_wv = nc.dram_tensor("wv_img", [128, 8192], F32, kind="ExternalInput").ap()
    d_bq = nc.dram_tensor("bq_img", [128, 8], F32, kind="ExternalInput").ap()
    d_bk = nc.dram_tensor("bk_img", [128, 8], F32, kind="ExternalInput").ap()
    d_bv = nc.dram_tensor("bv_img", [128, 1024], F32, kind="ExternalInput").ap()
    d_wp = nc.dram_tensor("wproj_img", [128, 8192], F32, kind="ExternalInput").ap()
    d_bp = nc.dram_tensor("bproj_img", [128, 1024], F32, kind="ExternalInput").ap()
    d_gA = nc.dram_tensor("gA_img", [128, 320], F32, kind="ExternalInput").ap()
    d_gb = nc.dram_tensor("gateb_img", [128, 8], F32, kind="ExternalInput").ap()
    d_B = nc.dram_tensor("B_img", [32, 4096], F32, kind="ExternalInput").ap()
    d_wi = nc.dram_tensor("wi_img", [128, 32768], F32, kind="ExternalInput").ap()
    d_wib = nc.dram_tensor("wib_img", [128, 32], F32, kind="ExternalInput").ap()
    d_wo = nc.dram_tensor("wo_img", [128, 32768], F32, kind="ExternalInput").ap()
    d_wob = nc.dram_tensor("wob_img", [128, 1024], F32, kind="ExternalInput").ap()
    d_msk = nc.dram_tensor("masks", [128, 3840], BF16, kind="ExternalInput").ap()
    d_db = nc.dram_tensor("den_bias", [1, 256], F32, kind="ExternalInput").ap()
    d_gc = nc.dram_tensor("g_col", [128, 1], F32, kind="ExternalInput").ap()
    d_id = nc.dram_tensor("ident", [128, 128], BF16, kind="ExternalInput").ap()
    d_out = nc.dram_tensor("out", [OWN, 1024], F32, kind="ExternalOutput").ap()
    d_dbgy = nc.dram_tensor("dbg_y", [128, 16384], F32, kind="ExternalOutput").ap() if DBG else None
    d_dbgx1 = nc.dram_tensor("dbg_x1", [128, 16384], F32, kind="ExternalOutput").ap() if DBG else None
    cc_in = nc.dram_tensor("cc_in", [65, 512], F32)
    cc_out = nc.dram_tensor("cc_out", [65, 512], F32)

    sch = Sched()
    es = contextlib.ExitStack()
    sems = {e: es.enter_context(nc.semaphore("sem_" + e)) for e in ENGS}

    def sbt(name, shape, dt, side="left"):
        return es.enter_context(nc.sbuf_tensor(name, shape, dt, side=side))

    identsb = sbt("identsb", [128, 128], BF16)
    ones_sb = sbt("ones_sb", [128, 128], BF16)
    bq_sb = sbt("bq_sb", [128, 8], F32)
    bk_sb = sbt("bk_sb", [128, 8], F32)
    gateb_sb = sbt("gateb_sb", [128, 8], F32)
    gcol_sb = sbt("gcol_sb", [128, 1], F32)
    stc = sbt("stat", [128, 8], F32)
    epsc = sbt("epsc", [128, 1], F32)
    rs = contextlib.ExitStack()

    def rbt(name, shape, dt):
        return rs.enter_context(nc.sbuf_tensor(name, shape, dt, side="right"))

    y_fm = rbt("y_fm", [128, 8 * 2048], BF16)
    allred = rbt("allred", [65, 512], F32)
    globred = rbt("globred", [65, 512], F32)
    dengb = rbt("dengb", [128, 512], F32)
    ygn = rbt("ygn", [128, 512], BF16)

    P0 = es.enter_context(nc.psum_tensor("P0", [128, 1024], F32))
    P1 = es.enter_context(nc.psum_tensor("P1", [128, 1024], F32))
    P2 = es.enter_context(nc.psum_tensor("P2", [128, 512], F32))
    PTB = es.enter_context(nc.psum_tensor("PTB", [128, 1024], BF16))
    P4 = es.enter_context(nc.psum_tensor("P4", [128, 1024], F32))
    QKVROT = [P0[:, 0:512], P0[:, 512:1024], P1[:, 0:512], P1[:, 512:1024]]

    def lp_recip(out_ap, in_ap):
        def f(e):
            with nc.allow_low_precision(reason="softmax recip as bf16 operand of bf16 broadcast matmul"):
                return e.reciprocal(out_ap, in_ap)
        return f

    open_scopes = [rs]

    def finish_early(scratch_f32):
        for qq in range(16):
            sch.dma(SY, lambda e, q=qq: e.dma_start(
                out=d_out[q * 128:(q + 1) * 128, :], in_=scratch_f32[:, :]),
                waits=[(V, sch.val[V]), (T, sch.val[T]), (S, sch.val[S]), (G, sch.val[G])])
        sch.wait_only(SY, [(SY, sch.val[SY])])
        sch.wait_only(G, [(SY, sch.val[SY])])

    def seal():
        for sstack in reversed(open_scopes):
            sstack.close()
        with nc.Block() as block:
            @block.tensor
            def _(e):
                sch.run(T, e, sems)

            @block.vector
            def _(e):
                sch.run(V, e, sems)

            @block.scalar
            def _(e):
                sch.run(S, e, sems)

            @block.gpsimd
            def _(e):
                sch.run(G, e, sems)

            @block.sync
            def _(e):
                sch.run(SY, e, sems)
        es.close()
        return nc

    e_id = sch.dma(SY, lambda e: e.dma_start(out=identsb[:, :], in_=d_id))

    open_scopes = []
    att = contextlib.ExitStack()
    open_scopes.append(att)

    def abt(name, shape, dt):
        return att.enter_context(nc.sbuf_tensor(name, shape, dt, side="left"))

    masksb = abt("masksb", [128, 3840], BF16)
    bv_sb = abt("bv_sb", [128, 1024], F32)
    db_sb = abt("db_sb", [128, 256], F32)
    rcp2 = [rbt(f"rcp{i}", [128, 256], BF16) for i in range(2)]

    def MS(si, s):
        c = (si * 5 + s) * 256
        return masksb[:, c:c + 256]

    def MSW(si):
        c = (si * 5 + 1) * 256
        return masksb[:, c:c + 1024]

    for dst, src in ((masksb, d_msk), (bq_sb, d_bq), (bk_sb, d_bk), (bv_sb, d_bv),
                     (gateb_sb, d_gb), (gcol_sb, d_gc)):
        sch.dma(SY, lambda e, d=dst, s=src: e.dma_start(out=d[:, :], in_=s))
    e_db = sch.dma(SY, lambda e: e.dma_start(out=db_sb[64:65, :], in_=d_db))
    e_ones = sch.emit(V, lambda e: e.memset(ones_sb[64:65, :], 1.0))
    sch.emit(V, lambda e: e.memset(epsc[:, :], 1e-5))

    x_bf = abt("x_bf", [128, DC * EXT], BF16)
    q_fm = abt("q_fm", [128, 4 * OWN], BF16)
    k_fm = abt("k_fm", [128, 4 * EXT], BF16)
    v_sb = abt("v_sb", [128, NTT * 520], BF16)
    wq_sb = abt("wq_sb", [128, 4096], BF16)
    wk_sb = abt("wk_sb", [128, 4096], BF16)
    wv_sb = abt("wv_sb", [128, 4096], BF16)
    qg_fm = abt("qg_fm", [128, 4 * 32], BF16)
    ptgq = [abt(f"ptgq{i}", [128, 256], BF16) for i in range(2)]
    PTg = [abt(f"PTg{i}", [128, 2048], BF16) for i in range(2)]
    PTw = [abt(f"PTw{i}", [128, 1024], BF16) for i in range(2)]
    PTgz = abt("PTgz", [128, 256], BF16)
    xsc = contextlib.ExitStack()
    xtm = [xsc.enter_context(nc.sbuf_tensor(f"xtm{i}", [128, 1024], F32, side="left"))
           for i in range(2)]
    xbt = [xsc.enter_context(nc.sbuf_tensor(f"xbt{i}", [128, 1024], BF16, side="left"))
           for i in range(2)]

    ptb_free = 0
    e_xdma = [0, 0]
    e_xcast = [0, 0]
    for tt in range(NTT):
        par = tt % 2
        w = [(V, e_xcast[par])]
        e_xdma[par] = sch.dma(SY, lambda e, t=tt: e.dma_start(
            out=xtm[t % 2][:, :], in_=d_x[t * 128:(t + 1) * 128, :]), waits=w)
        e_xcast[par] = sch.emit(V, lambda e, t=tt: e.tensor_copy(
            xbt[t % 2][:, :], xtm[t % 2][:, :]), waits=[(SY, e_xdma[par])])
        last = 0
        for ci in range(DC):
            w = [(V, e_xcast[par]), (SY, e_id)]
            if ci == 0:
                w.append((V, ptb_free))
            last = sch.emit(T, lambda e, t=tt, c=ci: e.transpose(
                PTB[:, c * 128:(c + 1) * 128], xbt[t % 2][:, c * 128:(c + 1) * 128],
                identsb[:, :]), waits=w)
        ptb_free = sch.emit(V, lambda e, t=tt: e.tensor_copy(
            x_bf[:, :].rearrange("p (c ee) -> p c ee", ee=EXT)[:, :, t * 128:(t + 1) * 128],
            PTB[:, :].rearrange("p (c j) -> p c j", j=128)), waits=[(T, last)])
    e_xbf = sch.val[V]
    xsc.close()
    den_fm = abt("den_fm", [128, 4096], BF16)
    rcp_fm = abt("rcp_fm", [128, 2048], BF16)
    lnsc = abt("lnsc", [128, 512], F32)

    def load_grp(g, waits):
        sch.dma(G, lambda e, gg=g: e.dma_start(
            out=wq_sb[:, :], in_=d_wq[:, gg * 4096:(gg + 1) * 4096]), waits=waits)
        sch.dma(G, lambda e, gg=g: e.dma_start(
            out=wk_sb[:, :], in_=d_wk[:, gg * 4096:(gg + 1) * 4096]))
        return sch.dma(G, lambda e, gg=g: e.dma_start(
            out=wv_sb[:, :], in_=d_wv[:, gg * 4096:(gg + 1) * 4096]))

    e_w = load_grp(0, [])
    e_allred_evac = 0
    e_ccr = 0
    e_nm = [0, 0]
    e_rS = [0, 0]
    pend_nm = None

    if phase_limit <= 1:
        finish_early(bv_sb)
        return seal()

    for g in range(NG):
        if g == 1:
            sch.wait_only(T, [(V, sch.val[V]), (S, sch.val[S])])
        e_vfree = sch.val[T]

        rot_i = 0
        e_rotev = [0, 0, 0, 0]

        def qkv_mm(dst_ap, bias_col, ntok, lhs_of, rhs_ap):
            nonlocal rot_i
            rot = rot_i % 4
            rot_i += 1
            ps = QKVROT[rot]
            w = [(V, e_xbf), (G, e_w), (V, e_rotev[rot])]
            last = 0
            for ci in range(DC):
                last = sch.emit(T, lambda e, c=ci, p=ps, lo=lhs_of, r=rhs_ap, n=ntok: e.matmul(
                    p[:, 0:n], lo(c), r(c), start=(c == 0), stop=(c == DC - 1)),
                    waits=w if ci == 0 else [])
            ev = sch.emit(V, lambda e, p=ps, d=dst_ap, bc=bias_col, n=ntok: e.tensor_scalar(
                d, p[:, 0:n], bc, None, op0=ALU.add), waits=[(T, last)])
            e_rotev[rot] = ev
            return last

        for co in range(4):
            cof = g * 4 + co
            for nt in range(4):
                qkv_mm(q_fm[:, co * OWN + nt * 512: co * OWN + (nt + 1) * 512],
                       bq_sb[:, cof:cof + 1], 512,
                       lambda ci, co_=co: wq_sb[:, (co_ * DC + ci) * 128:(co_ * DC + ci + 1) * 128],
                       lambda ci, n0=nt * 512: x_bf[:, ci * EXT + OWN0 + n0: ci * EXT + OWN0 + n0 + 512])
            for nt in range(5):
                n0 = nt * 512
                ntok = min(512, EXT - n0)
                qkv_mm(k_fm[:, co * EXT + n0: co * EXT + n0 + ntok],
                       bk_sb[:, cof:cof + 1], ntok,
                       lambda ci, co_=co: wk_sb[:, (co_ * DC + ci) * 128:(co_ * DC + ci + 1) * 128],
                       lambda ci, n0_=n0, nt_=ntok: x_bf[:, ci * EXT + n0_: ci * EXT + n0_ + nt_])
        e_vones = sch.emit(V, lambda e: e.memset(
            v_sb[:, :].rearrange("p (t h c) -> p t h c", h=8, c=65)[:, :, :, 64:65], 1.0),
            waits=[(T, e_vfree)])
        for tt in range(NTT):
            rot = rot_i % 4
            rot_i += 1
            ps = QKVROT[rot]
            w = [(V, e_xbf), (G, e_w), (V, e_rotev[rot]), (V, e_vones)]
            last = 0
            for ci in range(DC):
                last = sch.emit(T, lambda e, c=ci, p=ps, t=tt: e.matmul(
                    p, x_bf[:, c * EXT + t * 128: c * EXT + (t + 1) * 128],
                    wv_sb[:, c * 512:(c + 1) * 512], start=(c == 0), stop=(c == DC - 1)),
                    waits=w if ci == 0 else [])
            ev = sch.emit(V, lambda e, p=ps, t=tt, gg=g: e.tensor_tensor(
                v_sb[:, t * 520:(t + 1) * 520].rearrange("p (h c) -> p h c", c=65)[:, :, 0:64],
                p.rearrange("p (h c) -> p h c", c=64),
                bv_sb[:, gg * 512:(gg + 1) * 512].rearrange("p (h c) -> p h c", c=64),
                op=ALU.add), waits=[(T, last)])
            e_rotev[rot] = ev
        e_A_T, e_A_V = sch.val[T], sch.val[V]

        if g == 0:
            e_w_next = load_grp(1, [(T, e_A_T)])

        if phase_limit <= 2 + 3 * g:
            finish_early(bv_sb)
            return seal()
        e_qg = 0
        for co in range(4):
            cof = g * 4 + co
            last = 0
            for ci in range(DC):
                w = []
                if ci == 0:
                    w = [(V, e_A_V), (V, e_qg)]
                last = sch.emit(T, lambda e, c=ci, co_=co: e.matmul(
                    P2[:, 0:32], wq_sb[:, (co_ * DC + c) * 128:(co_ * DC + c + 1) * 128],
                    x_bf[:, c * EXT: c * EXT + 32],
                    start=(c == 0), stop=(c == DC - 1)), waits=w)
            e_qg = sch.emit(V, lambda e, co_=co, cf=cof: e.tensor_scalar(
                qg_fm[:, co_ * 32:(co_ + 1) * 32], P2[:, 0:32], bq_sb[:, cf:cf + 1],
                None, op0=ALU.add), waits=[(T, last)])

        e_gexp = [0, 0]
        e_gpv = 0
        HORD = (0, 2, 4, 6, 1, 3, 5, 7)
        for kb in range(16):
            par = kb % 2
            last_sc = 0
            for hh in HORD:
                base = (hh % 2) * 64
                co = hh // 2
                w = []
                if hh == 0:
                    w = [(V, e_qg), (S, e_gexp[par]), (V, e_allred_evac if g == 1 else 0)]
                cc = (hh % 2) * 512 + par * 256 + (hh // 2) * 32
                last_sc = sch.emit(T, lambda e, h=hh, b=base, c=co, k=kb, cc_=cc: e.matmul(
                    P4[:, cc_: cc_ + 32],
                    k_fm[b:b + 64, c * EXT + (k + 2) * 128: c * EXT + (k + 3) * 128],
                    qg_fm[b:b + 64, c * 32:(c + 1) * 32],
                    start=(h in (0, 1)), stop=(h in (6, 7))), waits=w)
            e_gexp[par] = sch.emit(S, lambda e, p=par, o=par * 256: e.activation(
                ptgq[p][:, :],
                P4[:, :].rearrange("q (bk c) -> q bk c", c=512)[:, :, o:o + 128],
                ACT.Exp, scale=0.125),
                waits=[(T, last_sc), (T, e_gpv if kb >= 2 else 0)])
            for hh in range(HPG):
                tgt = (P2[0:65, hh * 32: (hh + 1) * 32] if hh < 4
                       else P0[0:65, (hh - 4) * 32: (hh - 3) * 32])
                e_gpv = sch.emit(T, lambda e, h=hh, k=kb, p=par, tg=tgt: e.matmul(
                    tg, v_sb[:, (k + 2) * 520 + h * 65: (k + 2) * 520 + (h + 1) * 65],
                    ptgq[p][:, (h % 2) * 128 + (h // 2) * 32: (h % 2) * 128 + (h // 2) * 32 + 32],
                    start=(k == 0 and h in (0, 4)), stop=(k == 15 and h in (3, 7))),
                    waits=[(S, e_gexp[par])])
        sch.emit(V, lambda e, gg=g: e.tensor_copy(
            allred[:, gg * 256: gg * 256 + 128], P2[0:65, 0:128]), waits=[(T, e_gpv)])
        e_allred_evac = sch.emit(V, lambda e, gg=g: e.tensor_copy(
            allred[:, gg * 256 + 128: gg * 256 + 256], P0[0:65, 0:128]))

        if phase_limit <= 3 + 3 * g:
            finish_early(bv_sb)
            return seal()
        yb = [P4[0:65, 0:256], P2[0:65, 0:256]]
        ybsrc = [P4, P2]
        scb = [P0, P1]

        def nm_bc(pend, c):
            b_p, ch_p, e_dl, slot = pend
            par = c % 2
            e_b = sch.emit(T, lambda e, pr=par, c_=c, sl=slot: e.matmul(
                P4[:, 512 + pr * 256: 512 + (pr + 1) * 256], ones_sb[64:65, 0:128],
                den_fm[64:65, sl * 2048 + c_ * 256: sl * 2048 + (c_ + 1) * 256],
                start=True, stop=True),
                waits=[(V, e_dl), (S, e_rS[par]), (G, e_nm[par]), (S, e_ptg_exp)])
            return (e_b, b_p, ch_p, par, c, e_dl)

        def nm_fin(tok):
            e_b, b_p, ch_p, par, c, e_dl = tok
            sch.emit(S, lambda e, pr=par, b_=b_p: e.activation(
                lnsc[b_:b_ + 64, pr * 256:(pr + 1) * 256],
                P4[b_:b_ + 64, 512 + pr * 256: 512 + (pr + 1) * 256], ACT.Ln),
                waits=[(T, e_b)])
            e_rS[par] = sch.emit(S, lambda e, pr=par, b_=b_p, c_=c: e.activation(
                rcp_fm[b_:b_ + 64, c_ * 256:(c_ + 1) * 256],
                lnsc[b_:b_ + 64, pr * 256:(pr + 1) * 256], ACT.Exp, scale=-1.0))
            e_nm[par] = sch.emit(G, lambda e, b_=b_p, ch_=ch_p, c_=c: e.tensor_tensor(
                y_fm[b_:b_ + 64, ch_ * 2048 + c_ * 256: ch_ * 2048 + (c_ + 1) * 256],
                y_fm[b_:b_ + 64, ch_ * 2048 + c_ * 256: ch_ * 2048 + (c_ + 1) * 256],
                rcp_fm[b_:b_ + 64, c_ * 256:(c_ + 1) * 256], op=ALU.mult),
                waits=[(S, e_rS[par]), (V, e_dl)])

        e_exp = {}
        e_sc = {}
        e_pv = {}
        e_den = {}
        e_msk = {}
        e_ptg_ms = [0, 0]
        e_ptg0v = [0, 0]
        e_ptg_exp = 0
        for hi, hh in enumerate((0, 2, 4, 6, 1, 3, 5, 7)):
            H = g * HPG + hh
            hp = hi % 2
            base = (hh % 2) * 64
            co = hh // 2
            ch = H // 2
            if hi == 4:
                sch.emit(T, lambda e: e.drain())
            for half in range(2):
                for q4 in range(2):
                    w = []
                    if half == 0 and q4 == 0:
                        w = [(V, e_allred_evac), (S, e_gexp[0]), (S, e_gexp[1]),
                             (T, e_pv.get((hi - 2, 7), 0)),
                             (V, e_den.get((hi - 1, 6), 0)),
                             (S, e_rS[0]), (S, e_rS[1]),
                             (V, e_ptg_ms[hp]), (V, e_ptg0v[hp])]
                    elif half == 1 and q4 == 0:
                        w = [(S, e_ptg_exp)]
                    lastmm = sch.emit(T, lambda e, b=base, c=co, h2=half, q=q4: e.matmul(
                        P4[:, q * 512:(q + 1) * 512],
                        k_fm[b:b + 64, c * EXT: c * EXT + 128],
                        q_fm[b:b + 64, c * OWN + h2 * 1024 + q * 512:
                             c * OWN + h2 * 1024 + (q + 1) * 512],
                        start=True, stop=True), waits=w)
                e_ptg_exp = sch.emit(S, lambda e, h2=half, p=hp: e.activation(
                    PTg[p][:, h2 * 1024:(h2 + 1) * 1024], P4[:, 0:1024], ACT.Exp,
                    scale=0.125), waits=[(T, lastmm)])
            sch.emit(V, lambda e, p=hp: e.memset(PTg[p][32:64, :], 0.0),
                     waits=[(S, e_ptg_exp)])
            e_ptg_ms[hp] = sch.emit(V, lambda e, p=hp: e.memset(PTg[p][64:128, :], 0.0))
            e_ptg0v[hp] = sch.emit(V, lambda e, p=hp: e.tensor_tensor(
                PTgz[:, :], PTg[p][:, 0:256], MS(1, 0), op=ALU.mult),
                waits=[(V, e_ptg_ms[hp]), (SY, e_db)])

            def emit_scores(p):
                par = p % 2
                pexp = (e_exp.get((hi, p - 2)) or
                        (e_exp.get((hi - 1, 6 + par)) if hi else 0) or 0)
                last = 0
                for s in range(1, 5):
                    w = [(V, e_ptg_ms[hp]), (S, pexp)] if s == 1 else []
                    last = sch.emit(T, lambda e, b=base, c=co, pp=p, s_=s, pr=par: e.matmul(
                        scb[pr][:, (s_ - 1) * 256:s_ * 256],
                        k_fm[b:b + 64, c * EXT + (2 * pp + s_) * 128: c * EXT + (2 * pp + s_ + 1) * 128],
                        q_fm[b:b + 64, c * OWN + pp * 256: c * OWN + (pp + 1) * 256],
                        start=(s_ in (1, 3)), stop=(s_ == 4)), waits=w)
                e_sc[(hi, p)] = last

            emit_scores(0)
            emit_scores(1)
            for p in range(NPAIR):
                par = p % 2

                def prev(d):
                    return d.get((hi, p - 2)) or (d.get((hi - 1, 6 + par)) if hi else 0) or 0

                si = 1 if p == 0 else (2 if p == 7 else 0)
                e_exp[(hi, p)] = sch.emit(S, lambda e, pr=par: e.activation(
                    PTw[pr][:, :], scb[pr][:, 0:1024], ACT.Exp, scale=0.125),
                    waits=[(T, e_sc[(hi, p)]), (T, prev(e_pv))])
                e_mkv = sch.emit(V, lambda e, pr=par, si_=si: e.tensor_tensor(
                    PTw[pr][:, 0:512], PTw[pr][:, 0:512], MSW(si_)[:, 0:512],
                    op=ALU.mult), waits=[(S, e_exp[(hi, p)])])
                e_msk[(hi, p)] = sch.emit(G, lambda e, pr=par, si_=si: e.tensor_tensor(
                    PTw[pr][:, 512:1024], PTw[pr][:, 512:1024], MSW(si_)[:, 512:1024],
                    op=ALU.mult), waits=[(S, e_exp[(hi, p)])])
                tok = nm_bc(pend_nm, p) if pend_nm is not None else None
                if p + 2 < NPAIR:
                    emit_scores(p + 2)
                wpv = [(V, e_mkv), (G, e_msk[(hi, p)]), (V, prev(e_den))]
                if p == 0:
                    wpv.append((V, e_ptg0v[hp]))
                rhs_glob = PTgz[:, :] if p == 0 else PTg[hp][:, p * 256:(p + 1) * 256]
                lastpv = sch.emit(T, lambda e, h=hh, pr=par, rg=rhs_glob: e.matmul(
                    yb[pr], v_sb[:, h * 65:(h + 1) * 65], rg, start=True, stop=False),
                    waits=wpv)
                for s in range(1, 5):
                    lastpv = sch.emit(T, lambda e, h=hh, pp=p, s_=s, pr=par: e.matmul(
                        yb[pr], v_sb[:, (2 * pp + s_) * 520 + h * 65: (2 * pp + s_) * 520 + (h + 1) * 65],
                        PTw[pr][:, (s_ - 1) * 256: s_ * 256],
                        start=False, stop=(s_ == 4)))
                e_pv[(hi, p)] = lastpv
                sch.emit(V, lambda e, b=base, c=ch, pp=p, pr=par: e.tensor_copy(
                    y_fm[b:b + 64, c * 2048 + pp * 256: c * 2048 + (pp + 1) * 256],
                    ybsrc[pr][0:64, 0:256]), waits=[(T, lastpv)])
                sl0 = (hi % 2) * 2048
                if p == 0:
                    e_den[(hi, p)] = sch.emit(V, lambda e, pr=par, s0=sl0: e.tensor_tensor(
                        den_fm[64:65, s0:s0 + 256], ybsrc[pr][64:65, 0:256],
                        db_sb[64:65, :], op=ALU.add))
                else:
                    e_den[(hi, p)] = sch.emit(V, lambda e, pp=p, pr=par, s0=sl0: e.tensor_copy(
                        den_fm[64:65, s0 + pp * 256: s0 + (pp + 1) * 256],
                        ybsrc[pr][64:65, 0:256]))
                if tok is not None:
                    nm_fin(tok)
            pend_nm = (base, ch, e_den[(hi, NPAIR - 1)], hi % 2)
        for c in range(NPAIR):
            nm_fin(nm_bc(pend_nm, c))
        pend_nm = None
        if g == 0:
            e_w = e_w_next

    if phase_limit <= 7:
        finish_early(bv_sb)
        return seal()
    attn_T, attn_V, attn_S, attn_G = sch.val[T], sch.val[V], sch.val[S], sch.val[G]

    def phase_E():
        e_rcg = sch.emit(V, lp_recip(rcp2[0][64:65, :], globred[64:65, 0:256]),
                         waits=[(G, e_ccr), (V, attn_V)])
        e_rcg2 = sch.emit(V, lp_recip(rcp2[1][64:65, :], globred[64:65, 256:512]))
        e_bc1 = sch.emit(T, lambda e: e.matmul(P2[0:64, 0:256], ones_sb[64:65, 0:64],
                                               rcp2[0][64:65, :], start=True, stop=True),
                         waits=[(V, e_rcg2), (T, attn_T), (S, attn_S)])
        e_bc2 = sch.emit(T, lambda e: e.matmul(P2[0:64, 256:512], ones_sb[64:65, 0:64],
                                               rcp2[1][64:65, :], start=False, stop=True))
        e_dgb = sch.emit(S, lambda e: e.copy(dengb[0:64, :], P2[0:64, 0:512]),
                         waits=[(T, e_bc2), (S, attn_S)])
        e_ygn = sch.emit(V, lambda e: e.scalar_tensor_tensor(
            ygn[0:64, :], globred[0:64, :], gcol_sb[0:64, :], dengb[0:64, :],
            op0=ALU.mult, op1=ALU.mult), waits=[(S, e_dgb)])
        e_ygn2 = sch.dma(SY, lambda e: e.dma_start(out=ygn[64:128, :], in_=ygn[0:64, :]),
                         waits=[(V, e_ygn)])
        for H in range(16):
            b = (H % 2) * 64
            ch = H // 2
            sch.emit(V, lambda e, b_=b, c=ch, h=H: e.tensor_tensor(
                y_fm[b_:b_ + 64, c * 2048: c * 2048 + 32],
                y_fm[b_:b_ + 64, c * 2048: c * 2048 + 32],
                ygn[b_:b_ + 64, h * 32:(h + 1) * 32], op=ALU.add),
                waits=[(SY, e_ygn2)])
        return sch.val[V]

    att.close()
    if phase_limit <= 8:
        finish_early(bv_sb)
        return seal()
    bar = dict(sch.val)
    for eng in (T, V, S, G):
        sch.wait_only(eng, [(o, bar[o]) for o in (T, V, S, G) if o != eng])

    sc2 = contextlib.ExitStack()
    open_scopes.append(sc2)

    def fbt(name, shape, dt):
        return sc2.enter_context(nc.sbuf_tensor(name, shape, dt, side="left"))

    wo_sb = fbt("wo_sb", [128, 32768], BF16)
    x1_tm = fbt("x1_tm", [128, 16 * 1024], F32)
    sqs = fbt("sqs", [128, 1024], BF16)
    wob_sb = fbt("wob_sb", [128, 1024], F32)
    scF = contextlib.ExitStack()
    open_scopes.append(scF)

    def f2t(name, shape, dt):
        return scF.enter_context(nc.sbuf_tensor(name, shape, dt, side="left"))

    wproj_sb = f2t("wproj_sb", [128, 8192], BF16)
    bpr_sb = f2t("bpr_sb", [128, 1024], F32)
    xq = [f2t(f"xq{i}", [128, 1024], F32) for i in range(2)]
    xp = xq

    e_wp = sch.dma(G, lambda e: e.dma_start(out=wproj_sb[:, 0:4096], in_=d_wp[:, 0:4096]))
    e_wp2 = sch.dma(G, lambda e: e.dma_start(out=wproj_sb[:, 4096:8192],
                                             in_=d_wp[:, 4096:8192]))
    for dst, srcd in ((bpr_sb, d_bp), (wob_sb, d_wob)):
        sch.dma(G, lambda e, d=dst, s=srcd: e.dma_start(out=d[:, :], in_=s))
    e_wo = 0
    for i in range(8):
        e_wo = sch.dma(G, lambda e, i_=i: e.dma_start(
            out=wo_sb[:, i_ * 4096:(i_ + 1) * 4096], in_=d_wo[:, i_ * 4096:(i_ + 1) * 4096]))
    e_ccw = sch.dma(SY, lambda e: e.dma_start(out=cc_in.ap(), in_=allred[:, :]),
                    waits=[(V, e_allred_evac)])
    e_cc = sch.emit(G, lambda e: e.collective_compute(
        "AllReduce", ALU.add, replica_groups=[[0, 1, 2, 3], [4, 5, 6, 7]],
        ins=[cc_in.ap()], outs=[cc_out.ap()]), waits=[(SY, e_ccw)])
    e_ccr = sch.dma(G, lambda e: e.dma_start(out=globred[:, :], in_=cc_out.ap()),
                    waits=[(G, e_cc)])

    def ln_chain(src_t, dst_ap, e_src_v, dst_waits=()):
        eS = sch.emit(S, lambda e: e.activation(sqs[:, :], src_t[:, :], ACT.Square,
                                                accum_out=stc[:, 1:2]),
                      waits=[(V, e_src_v)])
        e1 = sch.emit(V, lambda e: e.reduce_sum(stc[:, 0:1], src_t[:, :], axis=AX.X),
                      waits=[(V, e_src_v)])
        sch.wait_only(V, [(S, eS), (V, e1)])
        sch.emit(V, lambda e: e.tensor_scalar(stc[:, 2:3], stc[:, 0:1], 1.0 / 1024, None,
                                              op0=ALU.mult))
        sch.emit(V, lambda e: e.tensor_scalar(stc[:, 3:4], stc[:, 1:2], 1.0 / 1024, None,
                                              op0=ALU.mult))
        sch.emit(V, lambda e: e.tensor_tensor(stc[:, 4:5], stc[:, 2:3], stc[:, 2:3],
                                              op=ALU.mult))
        e5 = sch.emit(V, lambda e: e.tensor_tensor(stc[:, 4:5], stc[:, 3:4], stc[:, 4:5],
                                                   op=ALU.subtract))
        e6 = sch.emit(S, lambda e: e.activation(stc[:, 5:6], stc[:, 4:5], ACT.Sqrt,
                                                bias=epsc[:, :]), waits=[(V, e5)])
        e7 = sch.emit(V, lambda e: e.reciprocal(stc[:, 6:7], stc[:, 5:6]), waits=[(S, e6)])
        sch.emit(V, lambda e: e.tensor_tensor(stc[:, 7:8], stc[:, 2:3], stc[:, 6:7],
                                              op=ALU.mult))
        e9 = sch.emit(V, lambda e: e.tensor_scalar(stc[:, 7:8], stc[:, 7:8], -1.0, None,
                                                   op0=ALU.mult))
        sch.wait_only(V, [(V, e9)] + list(dst_waits))
        e10 = sch.emit(V, lambda e, d=dst_ap: e.tensor_scalar(
            d, src_t[:, :], stc[:, 6:7], stc[:, 7:8], op0=ALU.mult, op1=ALU.add))
        return e10, eS

    projb = [P0, P1]
    e_projmm = [0, 0]
    e_xq = [0, 0]
    e_x1 = {}
    e_xpS = {}
    e_xpv = {}
    e_fix_done = 0
    for fi, qt in enumerate(list(range(1, 16)) + [0]):
        if qt == 0:
            e_fix_done = phase_E()
        par = qt % 2
        q2 = list(range(1, 16))[fi - 2] if fi >= 2 else None
        e_xq[par] = sch.dma(SY, lambda e, q=qt: e.dma_start(
            out=xq[q % 2][:, :], in_=d_x[OWN0 + q * 128: OWN0 + (q + 1) * 128, :]),
            waits=([(T, bar[T]), (V, bar[V]), (S, bar[S]), (G, bar[G])] if fi < 2 else
                   [(V, e_x1[q2][0]), (S, e_xpS.get(q2, 0))]))
        last = 0
        for dc in range(2):
            for ci in range(DC):
                w = []
                if dc == 0 and ci == 0:
                    w = [(V, e_fix_done) if qt == 0 else (V, attn_V),
                         (G, attn_G), (G, e_wp2),
                         (V, e_x1[q2][0] if q2 is not None else 0)]
                last = sch.emit(T, lambda e, d=dc, c=ci, q=qt, pb=projb[par]: e.matmul(
                    pb[:, d * 512:(d + 1) * 512],
                    y_fm[:, c * 2048 + q * 128: c * 2048 + (q + 1) * 128],
                    wproj_sb[:, c * 1024 + d * 512: c * 1024 + (d + 1) * 512],
                    start=(c == 0), stop=(c == DC - 1)), waits=w)
        e_projmm[par] = last
        ev = sch.emit(V, lambda e, pb=projb[par], q=qt: e.tensor_tensor(
            xp[q % 2][:, :], pb[:, 0:1024], xq[q % 2][:, :], op=ALU.add),
            waits=[(T, last), (SY, e_xq[par]), (S, e_xpS.get(q2, 0))])
        ev = sch.emit(V, lambda e, q=qt: e.tensor_tensor(
            xp[q % 2][:, :], xp[q % 2][:, :], bpr_sb[:, :], op=ALU.add))
        e_xpv[qt] = ev
        e_x1[qt] = ln_chain(xp[par], x1_tm[:, qt * 1024:(qt + 1) * 1024], ev)
        e_xpS[qt] = e_x1[qt][1]
    e_F_T, e_F_V, e_F_S = sch.val[T], sch.val[V], sch.val[S]

    if DBG:
        for qt in range(16):
            sch.dma(SY, lambda e, q=qt: e.dma_start(
                out=d_dbgx1[:, q * 1024:(q + 1) * 1024], in_=x1_tm[:, q * 1024:(q + 1) * 1024]),
                waits=[(V, e_x1[qt][0])])
    if phase_limit <= 9:
        finish_early(xq[0])
        scF.close()
        sc2.close()
        rs.close()
        return seal()
    sch.barrier()
    scF.close()
    rs.close()

    def gbt(name, shape, dt):
        return scF.enter_context(nc.sbuf_tensor(name, shape, dt, side="left"))

    wib_sb = gbt("wib_sb", [128, 32], F32)
    gA_sb = gbt("gA_sb", [128, 320], BF16)
    B_sb = gbt("B_sb", [32, 4096], BF16)
    wi_sb = [gbt(f"wi_sb{i}", [128, 2048], BF16) for i in range(2)]
    x1bt = [gbt(f"x1bt{i}", [128, 1024], BF16) for i in range(2)]
    x1fm = gbt("x1fm", [128, 8 * 512], BF16)
    inter = gbt("inter", [128, 32 * 512], BF16)
    rT_sb = gbt("rT_sb", [32, 512], BF16)
    lg_sb = gbt("lg_sb", [128, 8], F32)
    msk_sb = gbt("msk_sb", [128, 8], F32)
    rm_sb = gbt("rm_sb", [128, 32], BF16)
    x2p = gbt("x2p", [128, 1024], F32)
    out_sb = [gbt("out_sb0", [128, 1024], F32)] * 2
    for dst, srcd in ((wib_sb, d_wib), (gA_sb, d_gA), (B_sb, d_B)):
        e_misc = sch.dma(G, lambda e, d=dst, s=srcd: e.dma_start(out=d[:, :], in_=s))

    e_wi = [0, 0]
    e_wicons = [0, 0]
    e_intermm = {}
    e_gate = {}
    e_outdma = [0, 0]
    e_wocons = {}
    e_ln2 = {}
    e_x1c = 0
    for st in range(NST):
        e_x1fm = 0
        e_tp2 = [0, 0]
        for sub in range(4):
            qt = st * 4 + sub
            e_x1c = sch.emit(V, lambda e, q=qt, s=sub: e.tensor_copy(
                x1bt[s % 2][:, :], x1_tm[:, q * 1024:(q + 1) * 1024]),
                waits=[(V, e_x1.get(qt, (0,))[0]),
                       (T, e_tp2[sub % 2]),
                       (T, e_intermm.get((st - 1, FC - 1), 0)),
                       (T, e_F_T if st == 0 else 0),
                       (V, e_F_V if st == 0 else 0)])
            last = 0
            for ci in range(DC):
                w = [(V, e_x1c), (V, ptb_free)] if ci == 0 else []
                last = sch.emit(T, lambda e, s=sub, c=ci: e.transpose(
                    PTB[:, c * 128:(c + 1) * 128],
                    x1bt[s % 2][:, c * 128:(c + 1) * 128],
                    identsb[:, :]), waits=w)
            e_tp2[sub % 2] = last
            ptb_free = sch.emit(V, lambda e, s=sub: e.tensor_copy(
                x1fm[:, :].rearrange("p (c qq) -> p c qq", qq=512)[:, :, s * 128:(s + 1) * 128],
                PTB[:, :].rearrange("p (c j) -> p c j", j=128)), waits=[(T, last)])
        e_x1fm = ptb_free

        for sub in range(4):
            last = 0
            for ci in range(DC):
                w = []
                if ci == 0:
                    w = [(V, e_x1fm), (G, e_misc),
                         (V, e_gate.get((st, sub - 1), 0) or
                          (e_gate.get((st - 1, 3), 0) if st else 0))]
                last = sch.emit(T, lambda e, c=ci, s=sub: e.matmul(
                    P2[:, 0:40], x1fm[:, c * 512 + s * 128: c * 512 + (s + 1) * 128],
                    gA_sb[:, c * 40:(c + 1) * 40], start=(c == 0), stop=(c == DC - 1)), waits=w)
            sch.emit(V, lambda e: e.tensor_tensor(lg_sb[:, :], P2[:, 0:8], gateb_sb[:, :],
                                                  op=ALU.add), waits=[(T, last)])
            e2 = sch.emit(V, lambda e: e.reduce_max(stc[:, 0:1], lg_sb[:, :], axis=AX.X))
            sch.wait_only(V, [(V, e2)])
            e3 = sch.emit(V, lambda e: e.tensor_scalar(msk_sb[:, :], lg_sb[:, :],
                                                       stc[:, 0:1], None, op0=ALU.is_ge))
            sch.wait_only(V, [(V, e3)])
            for ex in range(8):
                e4 = sch.emit(V, lambda e, x=ex: e.tensor_scalar(
                    rm_sb[:, x * 4:(x + 1) * 4], P2[:, 8 + x * 4: 8 + (x + 1) * 4],
                    msk_sb[:, x:x + 1], None, op0=ALU.mult))
            etp = sch.emit(T, lambda e: e.transpose(PTB[0:32, 0:128], rm_sb[:, :],
                                                    identsb[:, :]),
                           waits=[(V, e4), (V, ptb_free)])
            ptb_free = sch.emit(V, lambda e, s=sub: e.tensor_copy(
                rT_sb[0:32, s * 128:(s + 1) * 128], PTB[0:32, 0:128]), waits=[(T, etp)])
            e_gate[(st, sub)] = ptb_free

        e_reluS = {}
        e_reluV = {}
        for fc in range(FC):
            par = (fc // 2) % 2
            if fc % 2 == 0:
                e_wi[par] = sch.dma(G, lambda e, f=fc: e.dma_start(
                    out=wi_sb[(f // 2) % 2][:, :], in_=d_wi[:, f * 1024: f * 1024 + 2048]),
                    waits=[(T, e_wicons[par]), (G, e_wo if (st == 0 and fc == 0) else 0)])
            rot = fc % 4
            ps = QKVROT[rot]
            last = 0
            for ci in range(DC):
                w = []
                if ci == 0:
                    w = [(G, e_wi[par]), (V, e_x1fm), (V, e_gate[(st, 3)])]
                    if fc >= 4:
                        w.append(((S if (fc - 4) % 2 == 0 else V),
                                  e_reluS.get(fc - 4, 0) or e_reluV.get(fc - 4, 0)))
                last = sch.emit(T, lambda e, c=ci, f=fc, p=ps: e.matmul(
                    p, wi_sb[(f // 2) % 2][:, (f % 2) * 1024 + c * 128:
                                           (f % 2) * 1024 + (c + 1) * 128],
                    x1fm[:, c * 512:(c + 1) * 512], start=(c == 0), stop=False), waits=w)
            last = sch.emit(T, lambda e, f=fc, p=ps: e.matmul(
                p, B_sb[:, f * 128:(f + 1) * 128], rT_sb[:, :], start=False, stop=True))
            e_intermm[(st, fc)] = last
            e_wicons[par] = last
            if fc % 2 == 0:
                e_reluS[fc] = sch.emit(S, lambda e, f=fc, p=ps: e.activation(
                    inter[:, f * 512:(f + 1) * 512], p, ACT.Relu, bias=wib_sb[:, f:f + 1]),
                    waits=[(T, last)])
            else:
                e_reluV[fc] = sch.emit(V, lambda e, f=fc, p=ps: e.tensor_scalar(
                    inter[:, f * 512:(f + 1) * 512], p, wib_sb[:, f:f + 1], 0.0,
                    op0=ALU.add, op1=ALU.max), waits=[(T, last)])
        e_relu_allS = sch.val[S]
        e_relu_allV = sch.val[V]

        for sub in range(4):
            qt = st * 4 + sub
            par = qt % 2
            last = 0
            for dc in range(2):
                for fc in range(FC):
                    w = []
                    if dc == 0 and fc == 0:
                        w = [(S, e_relu_allS), (V, e_relu_allV), (G, e_wo),
                             (V, e_wocons.get(qt - 1, 0))]
                    last = sch.emit(T, lambda e, d=dc, f=fc, s=sub: e.matmul(
                        P4[:, d * 512:(d + 1) * 512],
                        inter[:, f * 512 + s * 128: f * 512 + (s + 1) * 128],
                        wo_sb[:, f * 1024 + d * 512: f * 1024 + (d + 1) * 512],
                        start=(f == 0), stop=(f == FC - 1)), waits=w)
            ev = sch.emit(V, lambda e, q=qt: e.tensor_tensor(
                x2p[:, :], P4[:, 0:1024], x1_tm[:, q * 1024:(q + 1) * 1024], op=ALU.add),
                waits=[(T, last), (S, e_ln2.get(qt - 1, (0, 0))[1])])
            ev = sch.emit(V, lambda e: e.tensor_tensor(x2p[:, :], x2p[:, :], wob_sb[:, :],
                                                       op=ALU.add))
            e_wocons[qt] = ev
            e_ln2[qt] = ln_chain(x2p, out_sb[par][:, :], ev,
                                 dst_waits=[(SY, e_outdma[par])])
            e_outdma[par] = sch.dma(SY, lambda e, q=qt, pp=par: e.dma_start(
                out=d_out[q * 128:(q + 1) * 128, :], in_=out_sb[pp][:, :]),
                waits=[(V, e_ln2[qt][0])])

    sch.wait_only(SY, [(SY, sch.val[SY])])
    sch.wait_only(G, [(SY, sch.val[SY])])
    scF.close()
    sc2.close()
    return seal()


def _build_masks(r):
    t0 = r * 2048
    m = np.zeros((3, 5, 128, 256), np.float32)
    for si, p in ((0, 3), (1, 0), (2, 7)):
        for s in range(5):
            kb_ext = 0 if s == 0 else 2 * p + s
            ke = kb_ext * 128 + np.arange(128)
            i = t0 + 256 * p + np.arange(256)
            j = np.where(ke < 32, ke, t0 + (ke - 256))
            valid = (ke < 32) | ((ke >= 128) & (t0 + ke - 256 >= 0) & (t0 + ke - 256 < 8192))
            J, I = np.meshgrid(j, i, indexing="ij")
            Vv = valid[:, None]
            win = np.abs(I - J) <= 128
            glob = J < 32
            if s == 0:
                mm = Vv & glob & ~win & (I >= 32)
            else:
                mm = Vv & win & (I >= 32)
            m[si, s] = mm.astype(np.float32)
    return np.ascontiguousarray(m.transpose(2, 0, 1, 3).reshape(128, 3840)).astype(BF)


def kernel(**inp):
    x = np.asarray(inp["L0_attn_x"], np.float32)
    Wat = np.asarray(inp["L0_attn_W_attn"], np.float32)
    bat = np.asarray(inp["L0_attn_b_attn"], np.float32)
    Wpr = np.asarray(inp["L0_attn_W_proj"], np.float32)
    bpr = np.asarray(inp["L0_attn_b_proj"], np.float32)
    gW = np.asarray(inp["L0_ffn_gate_W"], np.float32)
    gb = np.asarray(inp["L0_ffn_gate_b"], np.float32)
    wiW = np.asarray(inp["L0_ffn_wi_W"], np.float32)
    wib = np.asarray(inp["L0_ffn_wi_b"], np.float32)
    woW = np.asarray(inp["L0_ffn_wo_W"], np.float32)
    wob = np.asarray(inp["L0_ffn_wo_b"], np.float32)
    lA = np.asarray(inp["L0_ffn_lora_A"], np.float32)
    lB = np.asarray(inp["L0_ffn_lora_B"], np.float32)

    C = np.ascontiguousarray
    wq_img = C(Wat[:, 0:1024].reshape(8, 128, 8, 128).transpose(1, 2, 0, 3).reshape(128, 8192))
    wk_img = C(Wat[:, 1024:2048].reshape(8, 128, 8, 128).transpose(1, 2, 0, 3).reshape(128, 8192))
    wv_img = C(Wat[:, 2048:3072].reshape(8, 128, 2, 512).transpose(1, 2, 0, 3).reshape(128, 8192))
    bq_img = C(bat[0:1024].reshape(8, 128).T)
    bk_img = C(bat[1024:2048].reshape(8, 128).T)
    bv_img = C(np.broadcast_to(bat[2048:3072], (128, 1024)))
    wp_img = C(Wpr.reshape(8, 128, 1024).transpose(1, 0, 2).reshape(128, 8192))
    bp_img = C(np.broadcast_to(bpr, (128, 1024)))
    M40 = np.concatenate([gW.T, lA.reshape(32, 1024).T], axis=1)
    gA_img = C(M40.reshape(8, 128, 40).transpose(1, 0, 2).reshape(128, 320))
    gb_img = C(np.broadcast_to(gb, (128, 8)))
    B_img = C(lB.transpose(0, 2, 1).reshape(32, 4096))
    wi_img = C(wiW.reshape(32, 128, 8, 128).transpose(3, 0, 2, 1).reshape(128, 32768))
    wib_img = C(wib.reshape(32, 128).T)
    wo_img = C(woW.reshape(1024, 32, 128).transpose(2, 1, 0).reshape(128, 32768))
    wob_img = C(np.broadcast_to(wob, (128, 1024)))
    ident = np.eye(128, dtype=BF)

    nc = build()
    in_maps = []
    for c in range(8):
        b, r = divmod(c, 4)
        t0 = r * 2048
        xe = np.zeros((EXT, 1024), np.float32)
        xe[0:32] = x[b, 0:32]
        if r > 0:
            xe[128:256] = x[b, t0 - 128:t0]
        xe[256:2304] = x[b, t0:t0 + 2048]
        if r < 3:
            xe[2304:2432] = x[b, t0 + 2048:t0 + 2176]
        db = np.zeros((1, 256), np.float32)
        if r == 0:
            db[0, 0:32] = 1.0
        gcol = np.full((128, 1), 1.0 if r == 0 else 0.0, np.float32)
        in_maps.append({
            "x_ext": xe, "wq_img": wq_img, "wk_img": wk_img, "wv_img": wv_img,
            "bq_img": bq_img, "bk_img": bk_img, "bv_img": bv_img,
            "wproj_img": wp_img, "bproj_img": bp_img,
            "gA_img": gA_img, "gateb_img": gb_img, "B_img": B_img,
            "wi_img": wi_img, "wib_img": wib_img, "wo_img": wo_img, "wob_img": wob_img,
            "masks": _build_masks(r), "den_bias": db, "g_col": gcol, "ident": ident,
        })
    res = run_bass_kernel_spmd(nc, in_maps, core_ids=list(range(8)))
    out = np.zeros((2, 8192, 1024), np.float32)
    for c in range(8):
        b, r = divmod(c, 4)
        out[b, r * 2048:(r + 1) * 2048] = res.results[c]["out"]
    return out


# version 13 (final)
# speedup vs baseline: 1.2213x; 1.0067x over previous
import contextlib
import numpy as np
import ml_dtypes
import concourse.bass as bass
import concourse.mybir as mybir
from concourse.bass_utils import run_bass_kernel_spmd

F32 = mybir.dt.float32
BF16 = mybir.dt.bfloat16
ALU = mybir.AluOpType
ACT = mybir.ActivationFunctionType
AX = mybir.AxisListType
BF = ml_dtypes.bfloat16

EXT = 2432
OWN = 2048
OWN0 = 256
NTT = 19
DC = 8
NG = 2
HPG = 8
NPAIR = 8
FC = 32
NST = 4
ENGS = ("tensor", "vector", "scalar", "gpsimd", "sync")
T, V, S, G, SY = ENGS


class Sched:

    def __init__(self):
        self.q = {e: [] for e in ENGS}
        self.val = {e: 0 for e in ENGS}
        self.lastw = {e: {} for e in ENGS}

    def _prune(self, eng, waits):
        wl = []
        for (se, v) in waits:
            if v and v > self.lastw[eng].get(se, 0):
                self.lastw[eng][se] = v
                wl.append((se, v))
        return wl

    def emit(self, eng, fn, waits=(), amt=1):
        self.q[eng].append((tuple(self._prune(eng, waits)), fn, amt))
        self.val[eng] += amt
        return self.val[eng]

    def dma(self, eng, fn, waits=()):
        return self.emit(eng, fn, waits, amt=16)

    def wait_only(self, eng, waits):
        wl = self._prune(eng, waits)
        if wl:
            self.q[eng].append((tuple(wl), None, 0))

    def barrier(self):
        snap = dict(self.val)
        for e in ENGS:
            self.wait_only(e, [(o, snap[o]) for o in ENGS if o != e])

    def run(self, eng, e, sems):
        own = sems[eng]
        cnt = 0
        chain = (eng == "vector")
        for wl, fn, amt in self.q[eng]:
            for (se, v) in wl:
                e.wait_ge(sems[se], v)
            if fn is not None:
                if chain and cnt:
                    e.wait_ge(own, cnt)
                fn(e).then_inc(own, amt)
                cnt += amt




DBG = False
NORM_MODE = 2


def build(phase_limit=99):
    nc = bass.Bass("TRN2", target_bir_lowering=False)

    d_x = nc.dram_tensor("x_ext", [EXT, 1024], F32, kind="ExternalInput").ap()
    d_wq = nc.dram_tensor("wq_img", [128, 8192], F32, kind="ExternalInput").ap()
    d_wk = nc.dram_tensor("wk_img", [128, 8192], F32, kind="ExternalInput").ap()
    d_wv = nc.dram_tensor("wv_img", [128, 8192], F32, kind="ExternalInput").ap()
    d_bq = nc.dram_tensor("bq_img", [128, 8], F32, kind="ExternalInput").ap()
    d_bk = nc.dram_tensor("bk_img", [128, 8], F32, kind="ExternalInput").ap()
    d_bv = nc.dram_tensor("bv_img", [128, 1024], F32, kind="ExternalInput").ap()
    d_wp = nc.dram_tensor("wproj_img", [128, 8192], F32, kind="ExternalInput").ap()
    d_bp = nc.dram_tensor("bproj_img", [128, 1024], F32, kind="ExternalInput").ap()
    d_gA = nc.dram_tensor("gA_img", [128, 320], F32, kind="ExternalInput").ap()
    d_gb = nc.dram_tensor("gateb_img", [128, 8], F32, kind="ExternalInput").ap()
    d_B = nc.dram_tensor("B_img", [32, 4096], F32, kind="ExternalInput").ap()
    d_wi = nc.dram_tensor("wi_img", [128, 32768], F32, kind="ExternalInput").ap()
    d_wib = nc.dram_tensor("wib_img", [128, 32], F32, kind="ExternalInput").ap()
    d_wo = nc.dram_tensor("wo_img", [128, 32768], F32, kind="ExternalInput").ap()
    d_wob = nc.dram_tensor("wob_img", [128, 1024], F32, kind="ExternalInput").ap()
    d_msk = nc.dram_tensor("masks", [128, 3840], BF16, kind="ExternalInput").ap()
    d_db = nc.dram_tensor("den_bias", [1, 256], F32, kind="ExternalInput").ap()
    d_gc = nc.dram_tensor("g_col", [128, 1], F32, kind="ExternalInput").ap()
    d_id = nc.dram_tensor("ident", [128, 128], BF16, kind="ExternalInput").ap()
    d_out = nc.dram_tensor("out", [OWN, 1024], F32, kind="ExternalOutput").ap()
    d_dbgy = nc.dram_tensor("dbg_y", [128, 16384], F32, kind="ExternalOutput").ap() if DBG else None
    d_dbgx1 = nc.dram_tensor("dbg_x1", [128, 16384], F32, kind="ExternalOutput").ap() if DBG else None
    cc_in = nc.dram_tensor("cc_in", [65, 512], F32)
    cc_out = nc.dram_tensor("cc_out", [65, 512], F32)

    sch = Sched()
    es = contextlib.ExitStack()
    sems = {e: es.enter_context(nc.semaphore("sem_" + e)) for e in ENGS}

    def sbt(name, shape, dt, side="left"):
        return es.enter_context(nc.sbuf_tensor(name, shape, dt, side=side))

    identsb = sbt("identsb", [128, 128], BF16)
    ones_sb = sbt("ones_sb", [128, 128], BF16)
    bq_sb = sbt("bq_sb", [128, 8], F32)
    bk_sb = sbt("bk_sb", [128, 8], F32)
    gateb_sb = sbt("gateb_sb", [128, 8], F32)
    gcol_sb = sbt("gcol_sb", [128, 1], F32)
    stc = sbt("stat", [128, 8], F32)
    epsc = sbt("epsc", [128, 1], F32)
    rs = contextlib.ExitStack()

    def rbt(name, shape, dt):
        return rs.enter_context(nc.sbuf_tensor(name, shape, dt, side="right"))

    y_fm = rbt("y_fm", [128, 8 * 2048], BF16)
    allred = rbt("allred", [65, 512], F32)
    globred = rbt("globred", [65, 512], F32)
    dengb = rbt("dengb", [128, 512], F32)
    ygn = rbt("ygn", [128, 512], BF16)

    P0 = es.enter_context(nc.psum_tensor("P0", [128, 1024], F32))
    P1 = es.enter_context(nc.psum_tensor("P1", [128, 1024], F32))
    P2 = es.enter_context(nc.psum_tensor("P2", [128, 512], F32))
    PTB = es.enter_context(nc.psum_tensor("PTB", [128, 1024], BF16))
    P4 = es.enter_context(nc.psum_tensor("P4", [128, 1024], F32))
    QKVROT = [P0[:, 0:512], P0[:, 512:1024], P1[:, 0:512], P1[:, 512:1024]]

    def lp_recip(out_ap, in_ap):
        def f(e):
            with nc.allow_low_precision(reason="softmax recip as bf16 operand of bf16 broadcast matmul"):
                return e.reciprocal(out_ap, in_ap)
        return f

    open_scopes = [rs]

    def finish_early(scratch_f32):
        for qq in range(16):
            sch.dma(SY, lambda e, q=qq: e.dma_start(
                out=d_out[q * 128:(q + 1) * 128, :], in_=scratch_f32[:, :]),
                waits=[(V, sch.val[V]), (T, sch.val[T]), (S, sch.val[S]), (G, sch.val[G])])
        sch.wait_only(SY, [(SY, sch.val[SY])])
        sch.wait_only(G, [(SY, sch.val[SY])])

    def seal():
        for sstack in reversed(open_scopes):
            sstack.close()
        with nc.Block() as block:
            @block.tensor
            def _(e):
                sch.run(T, e, sems)

            @block.vector
            def _(e):
                sch.run(V, e, sems)

            @block.scalar
            def _(e):
                sch.run(S, e, sems)

            @block.gpsimd
            def _(e):
                sch.run(G, e, sems)

            @block.sync
            def _(e):
                sch.run(SY, e, sems)
        es.close()
        return nc

    e_id = sch.dma(SY, lambda e: e.dma_start(out=identsb[:, :], in_=d_id))

    open_scopes = []
    att = contextlib.ExitStack()
    open_scopes.append(att)

    def abt(name, shape, dt):
        return att.enter_context(nc.sbuf_tensor(name, shape, dt, side="left"))

    masksb = abt("masksb", [128, 3840], BF16)
    bv_sb = abt("bv_sb", [128, 1024], F32)
    db_sb = abt("db_sb", [128, 256], F32)
    rcp2 = [rbt(f"rcp{i}", [128, 256], BF16) for i in range(2)]

    def MS(si, s):
        c = (si * 5 + s) * 256
        return masksb[:, c:c + 256]

    def MSW(si):
        c = (si * 5 + 1) * 256
        return masksb[:, c:c + 1024]

    for dst, src in ((masksb, d_msk), (bq_sb, d_bq), (bk_sb, d_bk), (bv_sb, d_bv),
                     (gateb_sb, d_gb), (gcol_sb, d_gc)):
        sch.dma(SY, lambda e, d=dst, s=src: e.dma_start(out=d[:, :], in_=s))
    e_db = sch.dma(SY, lambda e: e.dma_start(out=db_sb[64:65, :], in_=d_db))
    e_ones = sch.emit(V, lambda e: e.memset(ones_sb[64:65, :], 1.0))
    sch.emit(V, lambda e: e.memset(epsc[:, :], 1e-5))

    x_bf = abt("x_bf", [128, DC * EXT], BF16)
    q_fm = abt("q_fm", [128, 4 * OWN], BF16)
    k_fm = abt("k_fm", [128, 4 * EXT], BF16)
    v_sb = abt("v_sb", [128, NTT * 520], BF16)
    wq_sb = abt("wq_sb", [128, 4096], BF16)
    wk_sb = abt("wk_sb", [128, 4096], BF16)
    wv_sb = abt("wv_sb", [128, 4096], BF16)
    qg_fm = abt("qg_fm", [128, 4 * 32], BF16)
    ptgq = [abt(f"ptgq{i}", [128, 256], BF16) for i in range(2)]
    PTg = [abt(f"PTg{i}", [128, 2048], BF16) for i in range(2)]
    PTw = [abt(f"PTw{i}", [128, 1024], BF16) for i in range(2)]
    PTgz = abt("PTgz", [128, 256], BF16)
    xsc = contextlib.ExitStack()
    xtm = [xsc.enter_context(nc.sbuf_tensor(f"xtm{i}", [128, 1024], F32, side="left"))
           for i in range(2)]
    xbt = [xsc.enter_context(nc.sbuf_tensor(f"xbt{i}", [128, 1024], BF16, side="left"))
           for i in range(2)]

    ptb_free = 0
    e_xdma = [0, 0]
    e_xcast = [0, 0]
    for tt in range(NTT):
        par = tt % 2
        w = [(V, e_xcast[par])]
        e_xdma[par] = sch.dma(SY, lambda e, t=tt: e.dma_start(
            out=xtm[t % 2][:, :], in_=d_x[t * 128:(t + 1) * 128, :]), waits=w)
        e_xcast[par] = sch.emit(V, lambda e, t=tt: e.tensor_copy(
            xbt[t % 2][:, :], xtm[t % 2][:, :]), waits=[(SY, e_xdma[par])])
        last = 0
        for ci in range(DC):
            w = [(V, e_xcast[par]), (SY, e_id)]
            if ci == 0:
                w.append((V, ptb_free))
            last = sch.emit(T, lambda e, t=tt, c=ci: e.transpose(
                PTB[:, c * 128:(c + 1) * 128], xbt[t % 2][:, c * 128:(c + 1) * 128],
                identsb[:, :]), waits=w)
        ptb_free = sch.emit(V, lambda e, t=tt: e.tensor_copy(
            x_bf[:, :].rearrange("p (c ee) -> p c ee", ee=EXT)[:, :, t * 128:(t + 1) * 128],
            PTB[:, :].rearrange("p (c j) -> p c j", j=128)), waits=[(T, last)])
    e_xbf = sch.val[V]
    xsc.close()
    den_fm = abt("den_fm", [128, 4096], BF16)
    rcp_fm = abt("rcp_fm", [128, 2048], BF16)
    lnsc = abt("lnsc", [128, 512], F32)

    def load_grp(g, waits):
        sch.dma(G, lambda e, gg=g: e.dma_start(
            out=wq_sb[:, :], in_=d_wq[:, gg * 4096:(gg + 1) * 4096]), waits=waits)
        sch.dma(G, lambda e, gg=g: e.dma_start(
            out=wk_sb[:, :], in_=d_wk[:, gg * 4096:(gg + 1) * 4096]))
        return sch.dma(G, lambda e, gg=g: e.dma_start(
            out=wv_sb[:, :], in_=d_wv[:, gg * 4096:(gg + 1) * 4096]))

    e_w = load_grp(0, [])
    e_allred_evac = 0
    e_ccr = 0
    e_nm = [0, 0]
    e_rS = [0, 0]
    pend_nm = None

    if phase_limit <= 1:
        finish_early(bv_sb)
        return seal()

    for g in range(NG):
        if g == 1:
            sch.wait_only(T, [(V, sch.val[V]), (S, sch.val[S])])
        e_vfree = sch.val[T]

        rot_i = 0
        e_rotev = [0, 0, 0, 0]

        def qkv_mm(dst_ap, bias_col, ntok, lhs_of, rhs_ap):
            nonlocal rot_i
            rot = rot_i % 4
            rot_i += 1
            ps = QKVROT[rot]
            w = [(V, e_xbf), (G, e_w), (V, e_rotev[rot])]
            last = 0
            for ci in range(DC):
                last = sch.emit(T, lambda e, c=ci, p=ps, lo=lhs_of, r=rhs_ap, n=ntok: e.matmul(
                    p[:, 0:n], lo(c), r(c), start=(c == 0), stop=(c == DC - 1)),
                    waits=w if ci == 0 else [])
            ev = sch.emit(V, lambda e, p=ps, d=dst_ap, bc=bias_col, n=ntok: e.tensor_scalar(
                d, p[:, 0:n], bc, None, op0=ALU.add), waits=[(T, last)])
            e_rotev[rot] = ev
            return last

        for co in range(4):
            cof = g * 4 + co
            for nt in range(4):
                qkv_mm(q_fm[:, co * OWN + nt * 512: co * OWN + (nt + 1) * 512],
                       bq_sb[:, cof:cof + 1], 512,
                       lambda ci, co_=co: wq_sb[:, (co_ * DC + ci) * 128:(co_ * DC + ci + 1) * 128],
                       lambda ci, n0=nt * 512: x_bf[:, ci * EXT + OWN0 + n0: ci * EXT + OWN0 + n0 + 512])
            for nt in range(5):
                n0 = nt * 512
                ntok = min(512, EXT - n0)
                qkv_mm(k_fm[:, co * EXT + n0: co * EXT + n0 + ntok],
                       bk_sb[:, cof:cof + 1], ntok,
                       lambda ci, co_=co: wk_sb[:, (co_ * DC + ci) * 128:(co_ * DC + ci + 1) * 128],
                       lambda ci, n0_=n0, nt_=ntok: x_bf[:, ci * EXT + n0_: ci * EXT + n0_ + nt_])
        e_vones = sch.emit(V, lambda e: e.memset(
            v_sb[:, :].rearrange("p (t h c) -> p t h c", h=8, c=65)[:, :, :, 64:65], 1.0),
            waits=[(T, e_vfree)])
        for tt in range(NTT):
            rot = rot_i % 4
            rot_i += 1
            ps = QKVROT[rot]
            w = [(V, e_xbf), (G, e_w), (V, e_rotev[rot]), (V, e_vones)]
            last = 0
            for ci in range(DC):
                last = sch.emit(T, lambda e, c=ci, p=ps, t=tt: e.matmul(
                    p, x_bf[:, c * EXT + t * 128: c * EXT + (t + 1) * 128],
                    wv_sb[:, c * 512:(c + 1) * 512], start=(c == 0), stop=(c == DC - 1)),
                    waits=w if ci == 0 else [])
            ev = sch.emit(V, lambda e, p=ps, t=tt, gg=g: e.tensor_tensor(
                v_sb[:, t * 520:(t + 1) * 520].rearrange("p (h c) -> p h c", c=65)[:, :, 0:64],
                p.rearrange("p (h c) -> p h c", c=64),
                bv_sb[:, gg * 512:(gg + 1) * 512].rearrange("p (h c) -> p h c", c=64),
                op=ALU.add), waits=[(T, last)])
            e_rotev[rot] = ev
        e_A_T, e_A_V = sch.val[T], sch.val[V]

        if g == 0:
            e_w_next = load_grp(1, [(T, e_A_T)])

        if phase_limit <= 2 + 3 * g:
            finish_early(bv_sb)
            return seal()
        e_qg = 0
        for co in range(4):
            cof = g * 4 + co
            last = 0
            for ci in range(DC):
                w = []
                if ci == 0:
                    w = [(V, e_A_V), (V, e_qg)]
                last = sch.emit(T, lambda e, c=ci, co_=co: e.matmul(
                    P2[:, 0:32], wq_sb[:, (co_ * DC + c) * 128:(co_ * DC + c + 1) * 128],
                    x_bf[:, c * EXT: c * EXT + 32],
                    start=(c == 0), stop=(c == DC - 1)), waits=w)
            e_qg = sch.emit(V, lambda e, co_=co, cf=cof: e.tensor_scalar(
                qg_fm[:, co_ * 32:(co_ + 1) * 32], P2[:, 0:32], bq_sb[:, cf:cf + 1],
                None, op0=ALU.add), waits=[(T, last)])

        e_gexp = [0, 0]
        e_gpv = 0
        HORD = (0, 2, 4, 6, 1, 3, 5, 7)
        for kb in range(16):
            par = kb % 2
            last_sc = 0
            for hh in HORD:
                base = (hh % 2) * 64
                co = hh // 2
                w = []
                if hh == 0:
                    w = [(V, e_qg), (S, e_gexp[par]), (V, e_allred_evac if g == 1 else 0)]
                cc = (hh % 2) * 512 + par * 256 + (hh // 2) * 32
                last_sc = sch.emit(T, lambda e, h=hh, b=base, c=co, k=kb, cc_=cc: e.matmul(
                    P4[:, cc_: cc_ + 32],
                    k_fm[b:b + 64, c * EXT + (k + 2) * 128: c * EXT + (k + 3) * 128],
                    qg_fm[b:b + 64, c * 32:(c + 1) * 32],
                    start=(h in (0, 1)), stop=(h in (6, 7))), waits=w)
            e_gexp[par] = sch.emit(S, lambda e, p=par, o=par * 256: e.activation(
                ptgq[p][:, :],
                P4[:, :].rearrange("q (bk c) -> q bk c", c=512)[:, :, o:o + 128],
                ACT.Exp, scale=0.125),
                waits=[(T, last_sc), (T, e_gpv if kb >= 2 else 0)])
            for hh in range(HPG):
                tgt = (P2[0:65, hh * 32: (hh + 1) * 32] if hh < 4
                       else P0[0:65, (hh - 4) * 32: (hh - 3) * 32])
                e_gpv = sch.emit(T, lambda e, h=hh, k=kb, p=par, tg=tgt: e.matmul(
                    tg, v_sb[:, (k + 2) * 520 + h * 65: (k + 2) * 520 + (h + 1) * 65],
                    ptgq[p][:, (h % 2) * 128 + (h // 2) * 32: (h % 2) * 128 + (h // 2) * 32 + 32],
                    start=(k == 0 and h in (0, 4)), stop=(k == 15 and h in (3, 7))),
                    waits=[(S, e_gexp[par])])
        sch.emit(V, lambda e, gg=g: e.tensor_copy(
            allred[:, gg * 256: gg * 256 + 128], P2[0:65, 0:128]), waits=[(T, e_gpv)])
        e_allred_evac = sch.emit(V, lambda e, gg=g: e.tensor_copy(
            allred[:, gg * 256 + 128: gg * 256 + 256], P0[0:65, 0:128]))

        if phase_limit <= 3 + 3 * g:
            finish_early(bv_sb)
            return seal()
        yb = [P4[0:65, 0:256], P2[0:65, 0:256]]
        ybsrc = [P4, P2]
        scb = [P0, P1]

        def nm_bc(pend, c):
            b_p, ch_p, e_dl, slot = pend
            e_b = sch.emit(T, lambda e, c_=c, sl=slot: e.matmul(
                P4[:, 512:1024], ones_sb[64:65, 0:128],
                den_fm[64:65, sl * 2048 + c_ * 256: sl * 2048 + (c_ + 2) * 256],
                start=True, stop=True),
                waits=[(V, e_dl), (S, e_rS[0]), (G, e_nm[0]), (S, e_ptg_exp)])
            return (e_b, b_p, ch_p, c, e_dl)

        def nm_fin(tok):
            e_b, b_p, ch_p, c, e_dl = tok
            sch.emit(S, lambda e, b_=b_p: e.activation(
                lnsc[b_:b_ + 64, 0:512], P4[b_:b_ + 64, 512:1024], ACT.Ln),
                waits=[(T, e_b)])
            e_rS[0] = sch.emit(S, lambda e, b_=b_p, c_=c: e.activation(
                rcp_fm[b_:b_ + 64, c_ * 256:(c_ + 2) * 256],
                lnsc[b_:b_ + 64, 0:512], ACT.Exp, scale=-1.0))
            e_nm[0] = sch.emit(G, lambda e, b_=b_p, ch_=ch_p, c_=c: e.tensor_tensor(
                y_fm[b_:b_ + 64, ch_ * 2048 + c_ * 256: ch_ * 2048 + (c_ + 2) * 256],
                y_fm[b_:b_ + 64, ch_ * 2048 + c_ * 256: ch_ * 2048 + (c_ + 2) * 256],
                rcp_fm[b_:b_ + 64, c_ * 256:(c_ + 2) * 256], op=ALU.mult),
                waits=[(S, e_rS[0]), (V, e_dl)])

        e_exp = {}
        e_sc = {}
        e_pv = {}
        e_den = {}
        e_msk = {}
        e_ptg_ms = [0, 0]
        e_ptg0v = [0, 0]
        e_ptg_exp = 0
        for hi, hh in enumerate((0, 2, 4, 6, 1, 3, 5, 7)):
            H = g * HPG + hh
            hp = hi % 2
            base = (hh % 2) * 64
            co = hh // 2
            ch = H // 2
            if hi == 4:
                sch.emit(T, lambda e: e.drain())
            for half in range(2):
                for q4 in range(2):
                    w = []
                    if half == 0 and q4 == 0:
                        w = [(V, e_allred_evac), (S, e_gexp[0]), (S, e_gexp[1]),
                             (T, e_pv.get((hi - 2, 7), 0)),
                             (V, e_den.get((hi - 1, 6), 0)),
                             (S, e_rS[0]),
                             (V, e_ptg_ms[hp]), (V, e_ptg0v[hp])]
                    elif half == 1 and q4 == 0:
                        w = [(S, e_ptg_exp)]
                    lastmm = sch.emit(T, lambda e, b=base, c=co, h2=half, q=q4: e.matmul(
                        P4[:, q * 512:(q + 1) * 512],
                        k_fm[b:b + 64, c * EXT: c * EXT + 128],
                        q_fm[b:b + 64, c * OWN + h2 * 1024 + q * 512:
                             c * OWN + h2 * 1024 + (q + 1) * 512],
                        start=True, stop=True), waits=w)
                e_ptg_exp = sch.emit(S, lambda e, h2=half, p=hp: e.activation(
                    PTg[p][:, h2 * 1024:(h2 + 1) * 1024], P4[:, 0:1024], ACT.Exp,
                    scale=0.125), waits=[(T, lastmm)])
            sch.emit(V, lambda e, p=hp: e.memset(PTg[p][32:64, :], 0.0),
                     waits=[(S, e_ptg_exp)])
            e_ptg_ms[hp] = sch.emit(V, lambda e, p=hp: e.memset(PTg[p][64:128, :], 0.0))
            e_ptg0v[hp] = sch.emit(V, lambda e, p=hp: e.tensor_tensor(
                PTgz[:, :], PTg[p][:, 0:256], MS(1, 0), op=ALU.mult),
                waits=[(V, e_ptg_ms[hp]), (SY, e_db)])

            def emit_scores(p):
                par = p % 2
                pexp = (e_exp.get((hi, p - 2)) or
                        (e_exp.get((hi - 1, 6 + par)) if hi else 0) or 0)
                last = 0
                for s in range(1, 5):
                    w = [(V, e_ptg_ms[hp]), (S, pexp)] if s == 1 else []
                    last = sch.emit(T, lambda e, b=base, c=co, pp=p, s_=s, pr=par: e.matmul(
                        scb[pr][:, (s_ - 1) * 256:s_ * 256],
                        k_fm[b:b + 64, c * EXT + (2 * pp + s_) * 128: c * EXT + (2 * pp + s_ + 1) * 128],
                        q_fm[b:b + 64, c * OWN + pp * 256: c * OWN + (pp + 1) * 256],
                        start=(s_ in (1, 3)), stop=(s_ == 4)), waits=w)
                e_sc[(hi, p)] = last

            emit_scores(0)
            emit_scores(1)
            for p in range(NPAIR):
                par = p % 2

                def prev(d):
                    return d.get((hi, p - 2)) or (d.get((hi - 1, 6 + par)) if hi else 0) or 0

                si = 1 if p == 0 else (2 if p == 7 else 0)
                e_exp[(hi, p)] = sch.emit(S, lambda e, pr=par: e.activation(
                    PTw[pr][:, :], scb[pr][:, 0:1024], ACT.Exp, scale=0.125),
                    waits=[(T, e_sc[(hi, p)]), (T, prev(e_pv))])
                e_mkv = sch.emit(V, lambda e, pr=par, si_=si: e.tensor_tensor(
                    PTw[pr][:, 0:512], PTw[pr][:, 0:512], MSW(si_)[:, 0:512],
                    op=ALU.mult), waits=[(S, e_exp[(hi, p)])])
                e_msk[(hi, p)] = sch.emit(G, lambda e, pr=par, si_=si: e.tensor_tensor(
                    PTw[pr][:, 512:1024], PTw[pr][:, 512:1024], MSW(si_)[:, 512:1024],
                    op=ALU.mult), waits=[(S, e_exp[(hi, p)])])
                tok = nm_bc(pend_nm, p) if (pend_nm is not None and p % 2 == 0) else None
                if p + 2 < NPAIR:
                    emit_scores(p + 2)
                wpv = [(V, e_mkv), (G, e_msk[(hi, p)]), (V, prev(e_den))]
                if p == 0:
                    wpv.append((V, e_ptg0v[hp]))
                rhs_glob = PTgz[:, :] if p == 0 else PTg[hp][:, p * 256:(p + 1) * 256]
                lastpv = sch.emit(T, lambda e, h=hh, pr=par, rg=rhs_glob: e.matmul(
                    yb[pr], v_sb[:, h * 65:(h + 1) * 65], rg, start=True, stop=False),
                    waits=wpv)
                for s in range(1, 5):
                    lastpv = sch.emit(T, lambda e, h=hh, pp=p, s_=s, pr=par: e.matmul(
                        yb[pr], v_sb[:, (2 * pp + s_) * 520 + h * 65: (2 * pp + s_) * 520 + (h + 1) * 65],
                        PTw[pr][:, (s_ - 1) * 256: s_ * 256],
                        start=False, stop=(s_ == 4)))
                e_pv[(hi, p)] = lastpv
                sch.emit(V, lambda e, b=base, c=ch, pp=p, pr=par: e.tensor_copy(
                    y_fm[b:b + 64, c * 2048 + pp * 256: c * 2048 + (pp + 1) * 256],
                    ybsrc[pr][0:64, 0:256]), waits=[(T, lastpv)])
                sl0 = (hi % 2) * 2048
                if p == 0:
                    e_den[(hi, p)] = sch.emit(V, lambda e, pr=par, s0=sl0: e.tensor_tensor(
                        den_fm[64:65, s0:s0 + 256], ybsrc[pr][64:65, 0:256],
                        db_sb[64:65, :], op=ALU.add))
                else:
                    e_den[(hi, p)] = sch.emit(V, lambda e, pp=p, pr=par, s0=sl0: e.tensor_copy(
                        den_fm[64:65, s0 + pp * 256: s0 + (pp + 1) * 256],
                        ybsrc[pr][64:65, 0:256]))
                if tok is not None:
                    nm_fin(tok)
            pend_nm = (base, ch, e_den[(hi, NPAIR - 1)], hi % 2)
        for c in range(0, NPAIR, 2):
            nm_fin(nm_bc(pend_nm, c))
        pend_nm = None
        if g == 0:
            e_w = e_w_next

    if phase_limit <= 7:
        finish_early(bv_sb)
        return seal()
    attn_T, attn_V, attn_S, attn_G = sch.val[T], sch.val[V], sch.val[S], sch.val[G]

    def phase_E():
        e_rcg = sch.emit(V, lp_recip(rcp2[0][64:65, :], globred[64:65, 0:256]),
                         waits=[(G, e_ccr), (V, attn_V)])
        e_rcg2 = sch.emit(V, lp_recip(rcp2[1][64:65, :], globred[64:65, 256:512]))
        e_bc1 = sch.emit(T, lambda e: e.matmul(P2[0:64, 0:256], ones_sb[64:65, 0:64],
                                               rcp2[0][64:65, :], start=True, stop=True),
                         waits=[(V, e_rcg2), (T, attn_T), (S, attn_S)])
        e_bc2 = sch.emit(T, lambda e: e.matmul(P2[0:64, 256:512], ones_sb[64:65, 0:64],
                                               rcp2[1][64:65, :], start=False, stop=True))
        e_dgb = sch.emit(S, lambda e: e.copy(dengb[0:64, :], P2[0:64, 0:512]),
                         waits=[(T, e_bc2), (S, attn_S)])
        e_ygn = sch.emit(V, lambda e: e.scalar_tensor_tensor(
            ygn[0:64, :], globred[0:64, :], gcol_sb[0:64, :], dengb[0:64, :],
            op0=ALU.mult, op1=ALU.mult), waits=[(S, e_dgb)])
        e_ygn2 = sch.dma(SY, lambda e: e.dma_start(out=ygn[64:128, :], in_=ygn[0:64, :]),
                         waits=[(V, e_ygn)])
        for H in range(16):
            b = (H % 2) * 64
            ch = H // 2
            sch.emit(V, lambda e, b_=b, c=ch, h=H: e.tensor_tensor(
                y_fm[b_:b_ + 64, c * 2048: c * 2048 + 32],
                y_fm[b_:b_ + 64, c * 2048: c * 2048 + 32],
                ygn[b_:b_ + 64, h * 32:(h + 1) * 32], op=ALU.add),
                waits=[(SY, e_ygn2)])
        return sch.val[V]

    att.close()
    if phase_limit <= 8:
        finish_early(bv_sb)
        return seal()
    bar = dict(sch.val)
    for eng in (T, V, S, G):
        sch.wait_only(eng, [(o, bar[o]) for o in (T, V, S, G) if o != eng])

    sc2 = contextlib.ExitStack()
    open_scopes.append(sc2)

    def fbt(name, shape, dt):
        return sc2.enter_context(nc.sbuf_tensor(name, shape, dt, side="left"))

    wo_sb = fbt("wo_sb", [128, 32768], BF16)
    x1_tm = fbt("x1_tm", [128, 16 * 1024], F32)
    sqs = fbt("sqs", [128, 1024], BF16)
    wob_sb = fbt("wob_sb", [128, 1024], F32)
    scF = contextlib.ExitStack()
    open_scopes.append(scF)

    def f2t(name, shape, dt):
        return scF.enter_context(nc.sbuf_tensor(name, shape, dt, side="left"))

    wproj_sb = f2t("wproj_sb", [128, 8192], BF16)
    bpr_sb = f2t("bpr_sb", [128, 1024], F32)
    xq = [f2t(f"xq{i}", [128, 1024], F32) for i in range(2)]
    xp = xq

    e_wp = sch.dma(G, lambda e: e.dma_start(out=wproj_sb[:, 0:4096], in_=d_wp[:, 0:4096]))
    e_wp2 = sch.dma(G, lambda e: e.dma_start(out=wproj_sb[:, 4096:8192],
                                             in_=d_wp[:, 4096:8192]))
    for dst, srcd in ((bpr_sb, d_bp), (wob_sb, d_wob)):
        sch.dma(G, lambda e, d=dst, s=srcd: e.dma_start(out=d[:, :], in_=s))
    e_wo = 0
    for i in range(8):
        e_wo = sch.dma(G, lambda e, i_=i: e.dma_start(
            out=wo_sb[:, i_ * 4096:(i_ + 1) * 4096], in_=d_wo[:, i_ * 4096:(i_ + 1) * 4096]))
    e_ccw = sch.dma(SY, lambda e: e.dma_start(out=cc_in.ap(), in_=allred[:, :]),
                    waits=[(V, e_allred_evac)])
    e_cc = sch.emit(G, lambda e: e.collective_compute(
        "AllReduce", ALU.add, replica_groups=[[0, 1, 2, 3], [4, 5, 6, 7]],
        ins=[cc_in.ap()], outs=[cc_out.ap()]), waits=[(SY, e_ccw)])
    e_ccr = sch.dma(G, lambda e: e.dma_start(out=globred[:, :], in_=cc_out.ap()),
                    waits=[(G, e_cc)])

    def ln_chain(src_t, dst_ap, e_src_v, dst_waits=()):
        eS = sch.emit(S, lambda e: e.activation(sqs[:, :], src_t[:, :], ACT.Square,
                                                accum_out=stc[:, 1:2]),
                      waits=[(V, e_src_v)])
        e1 = sch.emit(V, lambda e: e.reduce_sum(stc[:, 0:1], src_t[:, :], axis=AX.X),
                      waits=[(V, e_src_v)])
        sch.wait_only(V, [(S, eS), (V, e1)])
        sch.emit(V, lambda e: e.tensor_scalar(stc[:, 2:3], stc[:, 0:1], 1.0 / 1024, None,
                                              op0=ALU.mult))
        sch.emit(V, lambda e: e.tensor_scalar(stc[:, 3:4], stc[:, 1:2], 1.0 / 1024, None,
                                              op0=ALU.mult))
        sch.emit(V, lambda e: e.tensor_tensor(stc[:, 4:5], stc[:, 2:3], stc[:, 2:3],
                                              op=ALU.mult))
        e5 = sch.emit(V, lambda e: e.tensor_tensor(stc[:, 4:5], stc[:, 3:4], stc[:, 4:5],
                                                   op=ALU.subtract))
        e6 = sch.emit(S, lambda e: e.activation(stc[:, 5:6], stc[:, 4:5], ACT.Sqrt,
                                                bias=epsc[:, :]), waits=[(V, e5)])
        e7 = sch.emit(V, lambda e: e.reciprocal(stc[:, 6:7], stc[:, 5:6]), waits=[(S, e6)])
        sch.emit(V, lambda e: e.tensor_tensor(stc[:, 7:8], stc[:, 2:3], stc[:, 6:7],
                                              op=ALU.mult))
        e9 = sch.emit(V, lambda e: e.tensor_scalar(stc[:, 7:8], stc[:, 7:8], -1.0, None,
                                                   op0=ALU.mult))
        sch.wait_only(V, [(V, e9)] + list(dst_waits))
        e10 = sch.emit(V, lambda e, d=dst_ap: e.tensor_scalar(
            d, src_t[:, :], stc[:, 6:7], stc[:, 7:8], op0=ALU.mult, op1=ALU.add))
        return e10, eS

    projb = [P0, P1]
    e_projmm = [0, 0]
    e_xq = [0, 0]
    e_x1 = {}
    e_xpS = {}
    e_xpv = {}
    e_fix_done = 0
    for fi, qt in enumerate(list(range(1, 16)) + [0]):
        if qt == 0:
            e_fix_done = phase_E()
        par = qt % 2
        q2 = list(range(1, 16))[fi - 2] if fi >= 2 else None
        e_xq[par] = sch.dma(SY, lambda e, q=qt: e.dma_start(
            out=xq[q % 2][:, :], in_=d_x[OWN0 + q * 128: OWN0 + (q + 1) * 128, :]),
            waits=([(T, bar[T]), (V, bar[V]), (S, bar[S]), (G, bar[G])] if fi < 2 else
                   [(V, e_x1[q2][0]), (S, e_xpS.get(q2, 0))]))
        last = 0
        for dc in range(2):
            for ci in range(DC):
                w = []
                if dc == 0 and ci == 0:
                    w = [(V, e_fix_done) if qt == 0 else (V, attn_V),
                         (G, attn_G), (G, e_wp2),
                         (V, e_x1[q2][0] if q2 is not None else 0)]
                last = sch.emit(T, lambda e, d=dc, c=ci, q=qt, pb=projb[par]: e.matmul(
                    pb[:, d * 512:(d + 1) * 512],
                    y_fm[:, c * 2048 + q * 128: c * 2048 + (q + 1) * 128],
                    wproj_sb[:, c * 1024 + d * 512: c * 1024 + (d + 1) * 512],
                    start=(c == 0), stop=(c == DC - 1)), waits=w)
        e_projmm[par] = last
        ev = sch.emit(V, lambda e, pb=projb[par], q=qt: e.tensor_tensor(
            xp[q % 2][:, :], pb[:, 0:1024], xq[q % 2][:, :], op=ALU.add),
            waits=[(T, last), (SY, e_xq[par]), (S, e_xpS.get(q2, 0))])
        ev = sch.emit(V, lambda e, q=qt: e.tensor_tensor(
            xp[q % 2][:, :], xp[q % 2][:, :], bpr_sb[:, :], op=ALU.add))
        e_xpv[qt] = ev
        e_x1[qt] = ln_chain(xp[par], x1_tm[:, qt * 1024:(qt + 1) * 1024], ev)
        e_xpS[qt] = e_x1[qt][1]
    e_F_T, e_F_V, e_F_S = sch.val[T], sch.val[V], sch.val[S]

    if DBG:
        for qt in range(16):
            sch.dma(SY, lambda e, q=qt: e.dma_start(
                out=d_dbgx1[:, q * 1024:(q + 1) * 1024], in_=x1_tm[:, q * 1024:(q + 1) * 1024]),
                waits=[(V, e_x1[qt][0])])
    if phase_limit <= 9:
        finish_early(xq[0])
        scF.close()
        sc2.close()
        rs.close()
        return seal()
    sch.barrier()
    scF.close()
    rs.close()

    def gbt(name, shape, dt):
        return scF.enter_context(nc.sbuf_tensor(name, shape, dt, side="left"))

    wib_sb = gbt("wib_sb", [128, 32], F32)
    gA_sb = gbt("gA_sb", [128, 320], BF16)
    B_sb = gbt("B_sb", [32, 4096], BF16)
    wi_sb = [gbt(f"wi_sb{i}", [128, 2048], BF16) for i in range(2)]
    x1bt = [gbt(f"x1bt{i}", [128, 1024], BF16) for i in range(2)]
    x1fm = gbt("x1fm", [128, 8 * 512], BF16)
    inter = gbt("inter", [128, 32 * 512], BF16)
    rT_sb = gbt("rT_sb", [32, 512], BF16)
    lg_sb = gbt("lg_sb", [128, 8], F32)
    msk_sb = gbt("msk_sb", [128, 8], F32)
    rm_sb = gbt("rm_sb", [128, 32], BF16)
    x2p = gbt("x2p", [128, 1024], F32)
    out_sb = [gbt("out_sb0", [128, 1024], F32)] * 2
    for dst, srcd in ((wib_sb, d_wib), (gA_sb, d_gA), (B_sb, d_B)):
        e_misc = sch.dma(G, lambda e, d=dst, s=srcd: e.dma_start(out=d[:, :], in_=s))

    e_wi = [0, 0]
    e_wicons = [0, 0]
    e_intermm = {}
    e_gate = {}
    e_outdma = [0, 0]
    e_wocons = {}
    e_ln2 = {}
    e_x1c = 0
    for st in range(NST):
        e_x1fm = 0
        e_tp2 = [0, 0]
        for sub in range(4):
            qt = st * 4 + sub
            e_x1c = sch.emit(V, lambda e, q=qt, s=sub: e.tensor_copy(
                x1bt[s % 2][:, :], x1_tm[:, q * 1024:(q + 1) * 1024]),
                waits=[(V, e_x1.get(qt, (0,))[0]),
                       (T, e_tp2[sub % 2]),
                       (T, e_intermm.get((st - 1, FC - 1), 0)),
                       (T, e_F_T if st == 0 else 0),
                       (V, e_F_V if st == 0 else 0)])
            last = 0
            for ci in range(DC):
                w = [(V, e_x1c), (V, ptb_free)] if ci == 0 else []
                last = sch.emit(T, lambda e, s=sub, c=ci: e.transpose(
                    PTB[:, c * 128:(c + 1) * 128],
                    x1bt[s % 2][:, c * 128:(c + 1) * 128],
                    identsb[:, :]), waits=w)
            e_tp2[sub % 2] = last
            ptb_free = sch.emit(V, lambda e, s=sub: e.tensor_copy(
                x1fm[:, :].rearrange("p (c qq) -> p c qq", qq=512)[:, :, s * 128:(s + 1) * 128],
                PTB[:, :].rearrange("p (c j) -> p c j", j=128)), waits=[(T, last)])
        e_x1fm = ptb_free

        for sub in range(4):
            last = 0
            for ci in range(DC):
                w = []
                if ci == 0:
                    w = [(V, e_x1fm), (G, e_misc),
                         (V, e_gate.get((st, sub - 1), 0) or
                          (e_gate.get((st - 1, 3), 0) if st else 0))]
                last = sch.emit(T, lambda e, c=ci, s=sub: e.matmul(
                    P2[:, 0:40], x1fm[:, c * 512 + s * 128: c * 512 + (s + 1) * 128],
                    gA_sb[:, c * 40:(c + 1) * 40], start=(c == 0), stop=(c == DC - 1)), waits=w)
            sch.emit(V, lambda e: e.tensor_tensor(lg_sb[:, :], P2[:, 0:8], gateb_sb[:, :],
                                                  op=ALU.add), waits=[(T, last)])
            e2 = sch.emit(V, lambda e: e.reduce_max(stc[:, 0:1], lg_sb[:, :], axis=AX.X))
            sch.wait_only(V, [(V, e2)])
            e3 = sch.emit(V, lambda e: e.tensor_scalar(msk_sb[:, :], lg_sb[:, :],
                                                       stc[:, 0:1], None, op0=ALU.is_ge))
            sch.wait_only(V, [(V, e3)])
            for ex in range(8):
                e4 = sch.emit(V, lambda e, x=ex: e.tensor_scalar(
                    rm_sb[:, x * 4:(x + 1) * 4], P2[:, 8 + x * 4: 8 + (x + 1) * 4],
                    msk_sb[:, x:x + 1], None, op0=ALU.mult))
            etp = sch.emit(T, lambda e: e.transpose(PTB[0:32, 0:128], rm_sb[:, :],
                                                    identsb[:, :]),
                           waits=[(V, e4), (V, ptb_free)])
            ptb_free = sch.emit(V, lambda e, s=sub: e.tensor_copy(
                rT_sb[0:32, s * 128:(s + 1) * 128], PTB[0:32, 0:128]), waits=[(T, etp)])
            e_gate[(st, sub)] = ptb_free

        e_reluS = {}
        e_reluV = {}
        for fc in range(FC):
            par = (fc // 2) % 2
            if fc % 2 == 0:
                e_wi[par] = sch.dma(G, lambda e, f=fc: e.dma_start(
                    out=wi_sb[(f // 2) % 2][:, :], in_=d_wi[:, f * 1024: f * 1024 + 2048]),
                    waits=[(T, e_wicons[par]), (G, e_wo if (st == 0 and fc == 0) else 0)])
            rot = fc % 4
            ps = QKVROT[rot]
            last = 0
            for ci in range(DC):
                w = []
                if ci == 0:
                    w = [(G, e_wi[par]), (V, e_x1fm), (V, e_gate[(st, 3)])]
                    if fc >= 4:
                        w.append(((S if (fc - 4) % 2 == 0 else V),
                                  e_reluS.get(fc - 4, 0) or e_reluV.get(fc - 4, 0)))
                last = sch.emit(T, lambda e, c=ci, f=fc, p=ps: e.matmul(
                    p, wi_sb[(f // 2) % 2][:, (f % 2) * 1024 + c * 128:
                                           (f % 2) * 1024 + (c + 1) * 128],
                    x1fm[:, c * 512:(c + 1) * 512], start=(c == 0), stop=False), waits=w)
            last = sch.emit(T, lambda e, f=fc, p=ps: e.matmul(
                p, B_sb[:, f * 128:(f + 1) * 128], rT_sb[:, :], start=False, stop=True))
            e_intermm[(st, fc)] = last
            e_wicons[par] = last
            if fc % 2 == 0:
                e_reluS[fc] = sch.emit(S, lambda e, f=fc, p=ps: e.activation(
                    inter[:, f * 512:(f + 1) * 512], p, ACT.Relu, bias=wib_sb[:, f:f + 1]),
                    waits=[(T, last)])
            else:
                e_reluV[fc] = sch.emit(V, lambda e, f=fc, p=ps: e.tensor_scalar(
                    inter[:, f * 512:(f + 1) * 512], p, wib_sb[:, f:f + 1], 0.0,
                    op0=ALU.add, op1=ALU.max), waits=[(T, last)])
        e_relu_allS = sch.val[S]
        e_relu_allV = sch.val[V]

        for sub in range(4):
            qt = st * 4 + sub
            par = qt % 2
            last = 0
            for dc in range(2):
                for fc in range(FC):
                    w = []
                    if dc == 0 and fc == 0:
                        w = [(S, e_relu_allS), (V, e_relu_allV), (G, e_wo),
                             (V, e_wocons.get(qt - 1, 0))]
                    last = sch.emit(T, lambda e, d=dc, f=fc, s=sub: e.matmul(
                        P4[:, d * 512:(d + 1) * 512],
                        inter[:, f * 512 + s * 128: f * 512 + (s + 1) * 128],
                        wo_sb[:, f * 1024 + d * 512: f * 1024 + (d + 1) * 512],
                        start=(f == 0), stop=(f == FC - 1)), waits=w)
            ev = sch.emit(V, lambda e, q=qt: e.tensor_tensor(
                x2p[:, :], P4[:, 0:1024], x1_tm[:, q * 1024:(q + 1) * 1024], op=ALU.add),
                waits=[(T, last), (S, e_ln2.get(qt - 1, (0, 0))[1])])
            ev = sch.emit(V, lambda e: e.tensor_tensor(x2p[:, :], x2p[:, :], wob_sb[:, :],
                                                       op=ALU.add))
            e_wocons[qt] = ev
            e_ln2[qt] = ln_chain(x2p, out_sb[par][:, :], ev,
                                 dst_waits=[(SY, e_outdma[par])])
            e_outdma[par] = sch.dma(SY, lambda e, q=qt, pp=par: e.dma_start(
                out=d_out[q * 128:(q + 1) * 128, :], in_=out_sb[pp][:, :]),
                waits=[(V, e_ln2[qt][0])])

    sch.wait_only(SY, [(SY, sch.val[SY])])
    sch.wait_only(G, [(SY, sch.val[SY])])
    scF.close()
    sc2.close()
    return seal()


def _build_masks(r):
    t0 = r * 2048
    m = np.zeros((3, 5, 128, 256), np.float32)
    for si, p in ((0, 3), (1, 0), (2, 7)):
        for s in range(5):
            kb_ext = 0 if s == 0 else 2 * p + s
            ke = kb_ext * 128 + np.arange(128)
            i = t0 + 256 * p + np.arange(256)
            j = np.where(ke < 32, ke, t0 + (ke - 256))
            valid = (ke < 32) | ((ke >= 128) & (t0 + ke - 256 >= 0) & (t0 + ke - 256 < 8192))
            J, I = np.meshgrid(j, i, indexing="ij")
            Vv = valid[:, None]
            win = np.abs(I - J) <= 128
            glob = J < 32
            if s == 0:
                mm = Vv & glob & ~win & (I >= 32)
            else:
                mm = Vv & win & (I >= 32)
            m[si, s] = mm.astype(np.float32)
    return np.ascontiguousarray(m.transpose(2, 0, 1, 3).reshape(128, 3840)).astype(BF)


def kernel(**inp):
    x = np.asarray(inp["L0_attn_x"], np.float32)
    Wat = np.asarray(inp["L0_attn_W_attn"], np.float32)
    bat = np.asarray(inp["L0_attn_b_attn"], np.float32)
    Wpr = np.asarray(inp["L0_attn_W_proj"], np.float32)
    bpr = np.asarray(inp["L0_attn_b_proj"], np.float32)
    gW = np.asarray(inp["L0_ffn_gate_W"], np.float32)
    gb = np.asarray(inp["L0_ffn_gate_b"], np.float32)
    wiW = np.asarray(inp["L0_ffn_wi_W"], np.float32)
    wib = np.asarray(inp["L0_ffn_wi_b"], np.float32)
    woW = np.asarray(inp["L0_ffn_wo_W"], np.float32)
    wob = np.asarray(inp["L0_ffn_wo_b"], np.float32)
    lA = np.asarray(inp["L0_ffn_lora_A"], np.float32)
    lB = np.asarray(inp["L0_ffn_lora_B"], np.float32)

    C = np.ascontiguousarray
    wq_img = C(Wat[:, 0:1024].reshape(8, 128, 8, 128).transpose(1, 2, 0, 3).reshape(128, 8192))
    wk_img = C(Wat[:, 1024:2048].reshape(8, 128, 8, 128).transpose(1, 2, 0, 3).reshape(128, 8192))
    wv_img = C(Wat[:, 2048:3072].reshape(8, 128, 2, 512).transpose(1, 2, 0, 3).reshape(128, 8192))
    bq_img = C(bat[0:1024].reshape(8, 128).T)
    bk_img = C(bat[1024:2048].reshape(8, 128).T)
    bv_img = C(np.broadcast_to(bat[2048:3072], (128, 1024)))
    wp_img = C(Wpr.reshape(8, 128, 1024).transpose(1, 0, 2).reshape(128, 8192))
    bp_img = C(np.broadcast_to(bpr, (128, 1024)))
    M40 = np.concatenate([gW.T, lA.reshape(32, 1024).T], axis=1)
    gA_img = C(M40.reshape(8, 128, 40).transpose(1, 0, 2).reshape(128, 320))
    gb_img = C(np.broadcast_to(gb, (128, 8)))
    B_img = C(lB.transpose(0, 2, 1).reshape(32, 4096))
    wi_img = C(wiW.reshape(32, 128, 8, 128).transpose(3, 0, 2, 1).reshape(128, 32768))
    wib_img = C(wib.reshape(32, 128).T)
    wo_img = C(woW.reshape(1024, 32, 128).transpose(2, 1, 0).reshape(128, 32768))
    wob_img = C(np.broadcast_to(wob, (128, 1024)))
    ident = np.eye(128, dtype=BF)

    nc = build()
    in_maps = []
    for c in range(8):
        b, r = divmod(c, 4)
        t0 = r * 2048
        xe = np.zeros((EXT, 1024), np.float32)
        xe[0:32] = x[b, 0:32]
        if r > 0:
            xe[128:256] = x[b, t0 - 128:t0]
        xe[256:2304] = x[b, t0:t0 + 2048]
        if r < 3:
            xe[2304:2432] = x[b, t0 + 2048:t0 + 2176]
        db = np.zeros((1, 256), np.float32)
        if r == 0:
            db[0, 0:32] = 1.0
        gcol = np.full((128, 1), 1.0 if r == 0 else 0.0, np.float32)
        in_maps.append({
            "x_ext": xe, "wq_img": wq_img, "wk_img": wk_img, "wv_img": wv_img,
            "bq_img": bq_img, "bk_img": bk_img, "bv_img": bv_img,
            "wproj_img": wp_img, "bproj_img": bp_img,
            "gA_img": gA_img, "gateb_img": gb_img, "B_img": B_img,
            "wi_img": wi_img, "wib_img": wib_img, "wo_img": wo_img, "wob_img": wob_img,
            "masks": _build_masks(r), "den_bias": db, "g_col": gcol, "ident": ident,
        })
    res = run_bass_kernel_spmd(nc, in_maps, core_ids=list(range(8)))
    out = np.zeros((2, 8192, 1024), np.float32)
    for c in range(8):
        b, r = divmod(c, 4)
        out[b, r * 2048:(r + 1) * 2048] = res.results[c]["out"]
    return out
